# version 01
# baseline (speedup 1.0000x reference)
from concourse.bass_utils import run_bass_kernel_spmd

import numpy as np
import concourse.bass as bass
import concourse.mybir as mybir

F32 = mybir.dt.float32
BF16 = mybir.dt.bfloat16
I32 = mybir.dt.int32
AF = mybir.ActivationFunctionType
ALU = mybir.AluOpType
AX = mybir.AxisListType
RG8 = [[0, 1, 2, 3, 4, 5, 6, 7]]

D = 2048
MK = 256
RW = 512
DC = D // 128
RC = RW // 128
NCH = 64
NF = 16384
TLOC = 4096
TT = 1024
NTILE = TLOC // TT
NGRP = 8
NSUB = 8
NGSZ = 256
TWO_N = 2.0 / (RW * D)


class Sched:
    def __init__(self, nc):
        self.nc = nc
        self.items = []
        self.cnt = {}
        self.sems = {}

    def sem(self, ctx, name):
        h = ctx.enter_context(self.nc.semaphore(name))
        self.sems[name] = h
        self.cnt[name] = 0

    def op(self, eng, fn, waits=(), inc=None, n=1):
        v = None
        if inc is not None:
            self.cnt[inc] += n
            v = self.cnt[inc]
        self.items.append((eng, fn, tuple(waits), inc, n))
        return v

    def v(self, name):
        return self.cnt[name]

    def replay(self, ename, eng):
        for (e, fn, waits, inc, n) in self.items:
            if e != ename:
                continue
            for (s, val) in waits:
                if val > 0:
                    eng.wait_ge(self.sems[s], val)
            inst = fn(eng)
            if inc is not None:
                inst.then_inc(self.sems[inc], n)


def build(n_steps=NCH, do_ffn=True, dbg=None, force_rs=False):
    dbg = dbg or {}
    nc = bass.Bass("TRN2", target_bir_lowering=False)
    full_run = (n_steps == NCH) or force_rs

    xfull = nc.dram_tensor("xfull", [4, 8192, D], F32, kind="ExternalInput").ap()
    xloc = nc.dram_tensor("xloc", [TLOC, D], F32, kind="ExternalInput").ap()
    w0kTd = nc.dram_tensor("w0kT", [D, MK], F32, kind="ExternalInput").ap()
    w1dmd = nc.dram_tensor("w1dm", [D, MK], F32, kind="ExternalInput").ap()
    w1kTd = nc.dram_tensor("w1kT", [MK, D], F32, kind="ExternalInput").ap()
    b0kd = nc.dram_tensor("b0k", [128, 2], F32, kind="ExternalInput").ap()
    b1d = nc.dram_tensor("b1f", [128, DC], F32, kind="ExternalInput").ap()
    lrfgd = nc.dram_tensor("lrfg", [128, DC, 2], F32, kind="ExternalInput").ap()
    lrbd = nc.dram_tensor("lrb", [1], F32, kind="ExternalInput").ap()
    fgbd = nc.dram_tensor("fgb", [1], F32, kind="ExternalInput").ap()
    gwp = nc.dram_tensor("gwp", [NF, D // 2], I32, kind="ExternalInput").ap()
    gsc = nc.dram_tensor("gsc", [NF, D // 32], F32, kind="ExternalInput").ap()
    uwp = nc.dram_tensor("uwp", [NF, D // 2], I32, kind="ExternalInput").ap()
    usc = nc.dram_tensor("usc", [NF, D // 32], F32, kind="ExternalInput").ap()
    dwp = nc.dram_tensor("dwp", [D, NF // 2], I32, kind="ExternalInput").ap()
    dsc = nc.dram_tensor("dsc", [D, NF // 32], F32, kind="ExternalInput").ap()
    out = nc.dram_tensor("out", [TLOC, D], F32, kind="ExternalOutput").ap()

    car_in = nc.dram_tensor("car_in", [D, RW], BF16, kind="Internal").ap()
    car_out = nc.dram_tensor("car_out", [D, RW], BF16, kind="Internal").ap()
    npair = max(1, min(32, n_steps))
    pairbuf = nc.dram_tensor("pairbuf", [npair, 8, D, 128], BF16, kind="Internal").ap()
    myattn = nc.dram_tensor("myattn", [npair, D, 128], BF16, kind="Internal").ap()
    wg_nm = nc.dram_tensor("wg_nm", [NF, D], BF16, kind="Internal").ap()
    wu_nm = nc.dram_tensor("wu_nm", [NF, D], BF16, kind="Internal").ap()
    wd_nm = nc.dram_tensor("wd_nm", [D, NF], BF16, kind="Internal").ap()
    wgT = nc.dram_tensor("wgT", [D, NF], BF16, kind="Internal").ap()
    wuT = nc.dram_tensor("wuT", [D, NF], BF16, kind="Internal").ap()
    wdT = nc.dram_tensor("wdT", [NF, D], BF16, kind="Internal").ap()
    ynbuf = nc.dram_tensor("ynbuf", [TT, D], BF16, kind="Internal").ap()

    dbg_outs = {}
    if dbg.get("p2st"):
        dbg_outs["d_p2st"] = nc.dram_tensor("d_p2st", [n_steps, D, RW], BF16,
                                            kind="ExternalOutput").ap()
    if dbg.get("pred"):
        dbg_outs["d_pred"] = nc.dram_tensor("d_pred", [n_steps, D, RW], BF16,
                                            kind="ExternalOutput").ap()
    if dbg.get("myattn"):
        dbg_outs["d_myattn"] = nc.dram_tensor("d_myattn", [npair, D, 128], BF16,
                                              kind="ExternalOutput").ap()
    if dbg.get("wrow"):
        dbg_outs["d_wrow"] = nc.dram_tensor("d_wrow", [3, 128, D], BF16,
                                            kind="ExternalOutput").ap()

    SB_BASE = 16768
    SB_LIMIT = 229344
    off = [SB_BASE]

    def salloc(name, shape, dtype, align=64):
        nbytes = int(np.prod(shape[1:])) * mybir.dt.size(dtype)
        off[0] = (off[0] + align - 1) // align * align
        t = nc.alloc_sbuf_tensor_at(name, list(shape), dtype, offset=off[0])
        off[0] += nbytes
        assert off[0] <= SB_LIMIT, f"SBUF overflow at {name}: {off[0]}"
        return t

    ident = salloc("ident", [128, 128], BF16)
    kones = salloc("kones", [1, 128], F32)
    epsT = salloc("epsT", [128, 1], F32)
    b75 = salloc("b75", [128, 1], F32)
    onec = salloc("onec", [128, 1], BF16)
    PERSIST_END = off[0]

    c_w0 = salloc("c_w0", [128, DC, MK], BF16)
    c_w1 = salloc("c_w1", [128, DC, MK + 1], BF16)
    c_w1T = salloc("c_w1T", [128, 2, D], BF16)
    c_b0 = salloc("c_b0", [128, 2, 1], F32)
    c_lrfg = salloc("c_lrfg", [128, DC, 2], BF16)
    c_lrb = salloc("c_lrb", [1, 1], F32)
    c_fgb = salloc("c_fgb", [1, 1], F32)
    c_fv = salloc("c_fv", [128, 1], F32)
    c_nl = salloc("c_nl", [128, 1], F32)
    c_b18 = salloc("c_b18", [128, DC, 1], BF16)
    c_xraw = salloc("c_xraw", [128, RC, D], F32)
    c_xn = [salloc(f"c_xn{i}", [128, RC, D], BF16) for i in range(2)]
    c_xnT = [salloc(f"c_xnT{i}", [128, DC, RW], BF16) for i in range(2)]
    c_h1T = salloc("c_h1T", [128, 2, RW], BF16)
    c_h1r = salloc("c_h1r", [128, RC, MK + 1], BF16)
    c_sgT = salloc("c_sgT", [128, 2, RW], BF16)
    c_sgr = salloc("c_sgr", [128, RC, MK], BF16)
    c_dpre = salloc("c_dpre", [128, RC, MK], BF16)
    c_pT = salloc("c_pT", [128, DC, RW], BF16)
    c_pr = salloc("c_pr", [128, RC, D], BF16)
    c_par = salloc("c_par", [128, DC, RW], BF16)
    c_p2 = salloc("c_p2", [128, DC, RW], BF16)
    c_st = salloc("c_st", [128, RC, 8, 3], F32)
    c_mv = salloc("c_mv", [128, RC, 2], F32)
    c_rs = salloc("c_rs", [128, RC, 1], F32)
    c_sc = salloc("c_sc", [2, RW], F32)
    c_u4 = salloc("c_u4", [1, 8], F32)
    c_fl = salloc("c_fl", [1, 2], F32)
    assert off[0] <= SB_LIMIT

    off[0] = PERSIST_END
    f_acc = salloc("f_acc", [128, TT // 128, D], F32)
    f_ynT = salloc("f_ynT", [128, DC, TT], BF16)
    f_int = salloc("f_int", [128, DC, TT], BF16)
    f_wp = [salloc(f"f_wp{i}", [128, DC, NGSZ], BF16) for i in range(3)]
    f_wd = salloc("f_wd", [128, DC, 1024], BF16)
    f_gel = salloc("f_gel", [128, RW], BF16)
    f_ay = salloc("f_ay", [128, D], BF16)
    f_st = salloc("f_st", [128, 8, 3], F32)
    f_mv = salloc("f_mv", [128, 2], F32)
    f_rs = salloc("f_rs", [128, 1], F32)
    FFN_END = off[0]
    assert FFN_END <= SB_LIMIT, FFN_END

    off[0] = PERSIST_END
    q_pk = salloc("q_pk", [128, 1024], I32)
    q_cd = salloc("q_cd", [128, 2048], BF16)
    q_s = salloc("q_s", [128, 2048], BF16)
    q_m = salloc("q_m", [128, 2048], BF16)
    q_a = salloc("q_a", [128, 2048], BF16)
    q_b = salloc("q_b", [128, 2048], BF16)
    q_c = salloc("q_c", [128, 2048], BF16)
    q_sc = salloc("q_sc", [128, 64], F32)
    q_out = salloc("q_out", [128, 2048], BF16)
    q_slab = [salloc(f"q_slab{i}", [128, NF], BF16) for i in range(2)]
    assert off[0] <= SB_LIMIT

    from contextlib import ExitStack
    with ExitStack() as ctx:
        S = Sched(nc)
        for name in ["sPE", "sACT", "sDVE", "sPOOL", "sDMA", "sGD",
                     "sCOLL", "sRS", "sST", "sLD", "sRB", "sXB0", "sXB1"]:
            S.sem(ctx, name)
        ps_a = ctx.enter_context(nc.psum_tensor("ps_a", [128, 2, RW], F32))
        ps_b = ctx.enter_context(nc.psum_tensor("ps_b", [128, 2, RW], F32))
        ps_d0 = ctx.enter_context(nc.psum_tensor("ps_d0", [128, 4, 128], BF16))
        ps_d1 = ctx.enter_context(nc.psum_tensor("ps_d1", [128, 4, 128], BF16))
        ps_s = ctx.enter_context(nc.psum_tensor("ps_s", [128, RW], F32))
        ps_f = ctx.enter_context(nc.psum_tensor("ps_f", [128, RW], F32))
        psd = [ps_d0, ps_d1]
        blk = ctx.enter_context(nc.Block())
        op = S.op

        op("gpsimd", lambda g: g.memset(ident[:], 0.0), inc="sPOOL")
        op("gpsimd", lambda g: g.affine_select(
            out=ident[:], in_=ident[:], compare_op=ALU.not_equal, fill=1.0,
            base=0, pattern=[[-1, 128]], channel_multiplier=1),
           waits=[("sPOOL", S.v("sPOOL"))], inc="sPOOL")
        op("gpsimd", lambda g: g.memset(kones[:], 1.0), inc="sPOOL")
        op("gpsimd", lambda g: g.memset(epsT[:], 1e-5), inc="sPOOL")
        op("gpsimd", lambda g: g.memset(b75[:], 7.5), inc="sPOOL")
        op("gpsimd", lambda g: g.memset(onec[:], 1.0), inc="sPOOL")
        IDENT_POOL = 6

        scrf = c_xraw[:].rearrange("p a d -> p (a d)")

        def stage(view, src_fn, dst, prev_dve):
            d_ = op("sync", lambda s: s.dma_start(out=view, in_=src_fn),
                    waits=[("sDVE", prev_dve)], inc="sDMA", n=16)
            return op("vector", lambda v: v.tensor_copy(dst, view),
                      waits=[("sDMA", d_)], inc="sDVE")

        cv = 0
        v16 = scrf[:, 0:DC * MK].rearrange("p (a m) -> p a m", m=MK)
        cv = stage(v16, w0kTd.rearrange("(a p) m -> p a m", p=128), c_w0[:], cv)
        cv = stage(v16, w1dmd.rearrange("(a p) m -> p a m", p=128), c_w1[:, :, 0:MK], cv)
        vb1 = scrf[:, 0:DC].rearrange("p (a o) -> p a o", o=1)
        cv = stage(vb1, b1d.rearrange("p (a o) -> p a o", o=1), c_w1[:, :, MK:MK + 1], cv)
        cv = op("vector", lambda v: v.tensor_scalar_mul(
            c_b18[:], c_w1[:, :, MK:MK + 1], 0.125), waits=[("sDVE", cv)], inc="sDVE")
        v2d = scrf[:, 0:2 * D].rearrange("p (a d) -> p a d", d=D)
        cv = stage(v2d, w1kTd.rearrange("(a p) d -> p a d", p=128), c_w1T[:], cv)
        vb0 = scrf[:, 0:2].rearrange("p (a o) -> p a o", o=1)
        cv = stage(vb0, b0kd.rearrange("p (a o) -> p a o", o=1), c_b0[:], cv)
        vlf = scrf[:, 0:2 * DC].rearrange("p (a t) -> p a t", t=2)
        cv = stage(vlf, lrfgd, c_lrfg[:], cv)
        dsm = op("sync", lambda s: s.dma_start(out=c_lrb[:], in_=lrbd.rearrange("(a o) -> a o", a=1)),
                 waits=[("sDVE", cv)], inc="sDMA", n=16)
        dsm = op("sync", lambda s: s.dma_start(out=c_fgb[:], in_=fgbd.rearrange("(a o) -> a o", a=1)),
                 waits=[("sDMA", dsm)], inc="sDMA", n=16)
        P0_DVE = cv
        P0_DMA = S.v("sDMA")

        psd_evac = [0, 0]

        def emit_prep(c, extra_load_waits=()):
            par = c % 2
            xsl = xfull.rearrange("b (n q) d -> n q b d", q=128)[c]
            dl = op("sync", lambda s: s.dma_start(out=c_xraw[:], in_=xsl),
                    waits=list(extra_load_waits), inc="sLD", n=16)
            last = 0
            for rc in range(RC):
                for sg in range(4):
                    last = op("vector",
                              lambda v, rc=rc, sg=sg: v.bn_stats(
                                  c_st[:, rc, 2 * sg:2 * sg + 2, :].rearrange("p a b -> p (a b)"),
                                  c_xraw[:, rc, 512 * sg:512 * (sg + 1)]),
                              waits=[("sLD", dl)] if (rc == 0 and sg == 0) else (),
                              inc="sDVE")
            ag = 0
            for rc in range(RC):
                ag = op("vector", lambda v, rc=rc: v.bn_aggr(
                    c_mv[:, rc, :], c_st[:, rc, :, :].rearrange("p a b -> p (a b)")),
                    waits=[("sDVE", last)] if rc == 0 else (), inc="sDVE")
            sq = op("scalar", lambda s: s.activation(
                c_rs[:].rearrange("p a o -> p (a o)"),
                c_mv[:, :, 1:2].rearrange("p a o -> p (a o)"),
                AF.Sqrt, bias=epsT[:], scale=1.0),
                waits=[("sDVE", ag), ("sPOOL", IDENT_POOL)], inc="sACT")
            rc_ = op("vector", lambda v: v.reciprocal(
                c_rs[:].rearrange("p a o -> p (a o)"),
                c_rs[:].rearrange("p a o -> p (a o)")),
                waits=[("sACT", sq)], inc="sDVE")
            nrm = 0
            for rc in range(RC):
                nrm = op("vector", lambda v, rc=rc, par=par: v.tensor_scalar(
                    c_xn[par][:, rc, :], c_xraw[:, rc, :],
                    c_mv[:, rc, 0:1], c_rs[:, rc, :], ALU.subtract, ALU.mult),
                    waits=[("sDVE", rc_)] if rc == 0 else (), inc="sDVE")
            for dc in range(DC):
                sl = dc % 2
                for rcc in range(RC):
                    w = []
                    if dc == 0 and rcc == 0:
                        w = [("sDVE", nrm), ("sPOOL", IDENT_POOL)]
                    if rcc == 0 and psd_evac[sl]:
                        w.append(("sDVE", psd_evac[sl]))
                    op("tensor", lambda t, dc=dc, rcc=rcc, sl=sl, par=par: t.transpose(
                        psd[sl][:, rcc, :], c_xn[par][:, rcc, 128 * dc:128 * (dc + 1)],
                        ident[:]), waits=w, inc="sPE")
                pe_b = S.v("sPE")
                psd_evac[sl] = op("vector", lambda v, dc=dc, sl=sl, par=par: v.tensor_copy(
                    c_xnT[par][:, dc, :], psd[sl][:].rearrange("p a b -> p (a b)")),
                    waits=[("sPE", pe_b)], inc="sDVE")
            return nrm, max(psd_evac)

        nrm0, prep0 = emit_prep(0, extra_load_waits=[("sDVE", P0_DVE)])
        prep_dve = {0: S.v("sDVE")}
        prev_nrm = {0: nrm0}

        psb_free = {0: None, 1: None}
        pending_rs = None
        st1_prev = 0
        gd_pair_prev = 0
        ev_p2_prev = 0
        ar_prev = 0
        rb_prev = 0

        for c in range(n_steps):
            par = c % 2
            xn = c_xn[par]
            xnT = c_xnT[par]

            w0_ = [("sDVE", S.v("sDVE")), ("sACT", ev_p2_prev)]
            if c == 0:
                w0_.append(("sDMA", P0_DMA))
            first = True
            for kc in range(DC):
                for ic in range(2):
                    op("tensor", lambda t, kc=kc, ic=ic, xnT=xnT: t.matmul(
                        ps_a[:, ic, :], c_w0[:, kc, 128 * ic:128 * (ic + 1)],
                        xnT[:, kc, :], start=(kc == 0), stop=(kc == DC - 1)),
                        waits=w0_ if first else (), inc="sPE")
                    first = False
                op("tensor", lambda t, kc=kc, xnT=xnT: t.matmul(
                    ps_s[0:1, :], c_lrfg[:, kc, 0:1], xnT[:, kc, :],
                    start=(kc == 0), stop=(kc == DC - 1)), inc="sPE")
                op("tensor", lambda t, kc=kc, xnT=xnT: t.matmul(
                    ps_s[32:33, :], c_lrfg[:, kc, 1:2], xnT[:, kc, :],
                    start=(kc == 0), stop=(kc == DC - 1)), inc="sPE")
            pe_fwd1 = S.v("sPE")

            act_h1 = 0
            for mc in range(2):
                op("scalar", lambda s, mc=mc: s.activation(
                    c_h1T[:, mc, :], ps_a[:, mc, :], AF.Silu,
                    bias=c_b0[:, mc, :], scale=1.0),
                    waits=[("sPE", pe_fwd1)] if mc == 0 else (), inc="sACT")
                act_h1 = op("scalar", lambda s, mc=mc: s.activation(
                    c_sgT[:, mc, :], ps_a[:, mc, :], AF.Derivative_silu,
                    bias=c_b0[:, mc, :], scale=1.0), inc="sACT")
            sv_ = op("scalar", lambda s: s.activation(
                c_sc[0:1, :], ps_s[0:1, :], AF.Sigmoid, bias=c_lrb[:], scale=1.0),
                waits=[("sPE", pe_fwd1)], inc="sACT")
            e1 = op("vector", lambda v: v.tensor_reduce(
                c_u4[0:1, 6:7].rearrange("p (a o) -> p a o", o=1),
                c_sc[0:1, :].rearrange("p (a r) -> p a r", a=1), AX.X, ALU.add),
                waits=[("sACT", sv_)], inc="sDVE")
            nl_ = op("vector", lambda v: v.tensor_scalar_mul(
                c_u4[0:1, 4:5], c_u4[0:1, 6:7], -0.01 / RW),
                waits=[("sDVE", e1)], inc="sDVE")
            u4_ = op("vector", lambda v: v.tensor_reduce(
                c_u4[0:1, 0:4].rearrange("p (a o) -> p a o", o=1),
                ps_s[32:33, :].rearrange("p (a r) -> p a r", a=4), AX.X, ALU.add),
                waits=[("sDVE", e1)], inc="sDVE")
            sg4 = op("scalar", lambda s: s.activation(
                c_sc[0:1, 0:4], c_u4[0:1, 0:4], AF.Sigmoid, bias=c_fgb[:],
                scale=1.0 / 128), waits=[("sDVE", u4_)], inc="sACT")
            f1 = op("vector", lambda v: v.tensor_reduce(
                c_u4[0:1, 5:6].rearrange("p (a o) -> p a o", o=1),
                c_sc[0:1, 0:4].rearrange("p (a r) -> p a r", a=1), AX.X, ALU.add),
                waits=[("sACT", sg4)], inc="sDVE")
            fl_ = op("vector", lambda v: v.tensor_scalar_mul(
                c_fl[0:1, 0:1], c_u4[0:1, 5:6], 0.25), waits=[("sDVE", f1)], inc="sDVE")
            fl2 = op("vector", lambda v: v.tensor_copy(c_fl[0:1, 1:2], c_u4[0:1, 4:5]),
                     waits=[("sDVE", nl_)], inc="sDVE")

            ev_par = 0
            ev_hist = {}
            for dc in range(DC):
                for kc in range(2):
                    w = []
                    if dc == 0 and kc == 0:
                        w = [("sACT", act_h1)]
                    if kc == 0 and dc >= 2:
                        w = [("sACT", ev_hist[dc - 2])]
                    if kc == 0 and dc < 2 and psb_free[dc % 2]:
                        w.append(psb_free[dc % 2])
                    op("tensor", lambda t, dc=dc, kc=kc: t.matmul(
                        ps_b[:, dc % 2, :], c_w1T[:, kc, 128 * dc:128 * (dc + 1)],
                        c_h1T[:, kc, :], start=(kc == 0), stop=(kc == 1)),
                        waits=w, inc="sPE")
                pe_pk = S.v("sPE")
                wex = [("sPE", pe_pk)]
                if dc == 0 and st1_prev:
                    wex.append(("sGD", st1_prev))
                ev_par = op("scalar", lambda s, dc=dc: s.activation(
                    c_par[:, dc, :], ps_b[:, dc % 2, :], AF.Identity,
                    bias=c_b18[:, dc, :], scale=1.0), waits=wex, inc="sACT")
                ev_hist[dc] = ev_par
                psb_free[dc % 2] = ("sACT", ev_par)

            bc = op("tensor", lambda t: t.matmul(
                ps_s[:, 64:66], kones[:], c_fl[0:1, :], start=True, stop=True),
                waits=[("sDVE", max(fl_, fl2))], inc="sPE")
            fv_ = op("vector", lambda v: v.tensor_copy(c_fv[:], ps_s[:, 64:65]),
                     waits=[("sPE", bc)], inc="sDVE")
            nv_ = op("vector", lambda v: v.tensor_copy(c_nl[:], ps_s[:, 65:66]),
                     inc="sDVE")
            sc_done = nv_

            st1 = op("gpsimd", lambda g: g.dma_start(
                out=car_in.rearrange("(a p) r -> p a r", p=128), in_=c_par[:]),
                waits=[("sACT", ev_par), ("sRB", rb_prev)], inc="sGD", n=16)
            ar = op("gpsimd", lambda g: g.collective_compute(
                "AllReduce", ALU.add, RG8, ins=[car_in.opt()], outs=[car_out.opt()]),
                waits=[("sGD", st1)], inc="sCOLL")
            st1_prev = st1
            if pending_rs is not None:
                op("gpsimd", pending_rs[0], waits=pending_rs[1], inc="sRS")
                pending_rs = None

            if c + 1 < n_steps:
                emit_prep(c + 1, extra_load_waits=[("sDVE", prev_nrm[c])])
                prev_nrm[c + 1] = S.v("sDVE")
                prep_dve[c + 1] = S.v("sDVE")

            for mc in range(2):
                sl = mc % 2
                w0t = []
                if mc == 0:
                    w0t = [("sACT", act_h1)]
                if psd_evac[sl]:
                    w0t.append(("sDVE", psd_evac[sl]))
                for rcc in range(RC):
                    op("tensor", lambda t, mc=mc, rcc=rcc, sl=sl: t.transpose(
                        psd[sl][:, rcc, :], c_sgT[:, mc, 128 * rcc:128 * (rcc + 1)],
                        ident[:]), waits=w0t if rcc == 0 else (), inc="sPE")
                pe_b = S.v("sPE")
                psd_evac[sl] = op("vector", lambda v, mc=mc, sl=sl: v.tensor_copy(
                    c_sgr[:, :, 128 * mc:128 * (mc + 1)], psd[sl][:]),
                    waits=[("sPE", pe_b)], inc="sDVE")
            sgr_done = S.v("sDVE")
            h1r_done = 0
            for mc in range(2):
                sl = mc % 2
                for rcc in range(RC):
                    w = [("sDVE", psd_evac[sl])] if rcc == 0 else ()
                    op("tensor", lambda t, mc=mc, rcc=rcc, sl=sl: t.transpose(
                        psd[sl][:, rcc, :], c_h1T[:, mc, 128 * rcc:128 * (rcc + 1)],
                        ident[:]), waits=w, inc="sPE")
                pe_b = S.v("sPE")
                psd_evac[sl] = op("vector", lambda v, mc=mc, sl=sl: v.tensor_scalar_mul(
                    c_h1r[:, :, 128 * mc:128 * (mc + 1)], psd[sl][:], c_nl[:]),
                    waits=[("sPE", pe_b), ("sDVE", sc_done)], inc="sDVE")
            for rcc in range(RC):
                h1r_done = op("vector", lambda v, rcc=rcc: v.tensor_copy(
                    c_h1r[:, rcc, MK:MK + 1], c_nl[:]),
                    waits=[("sDVE", sc_done)] if rcc == 0 else (), inc="sDVE")

            rb1 = op("sync", lambda s: s.dma_start(
                out=c_pT[:], in_=car_out.rearrange("(a p) r -> p a r", p=128)),
                waits=[("sCOLL", ar)], inc="sRB", n=16)
            if dbg.get("pred"):
                dbp = op("gpsimd", lambda g, c=c: g.dma_start(
                    out=dbg_outs["d_pred"][c].rearrange("(a p) r -> p a r", p=128),
                    in_=car_out.rearrange("(a p) r -> p a r", p=128)),
                    waits=[("sCOLL", ar)], inc="sGD", n=16)
            rbx = 0
            for rcc in range(RC):
                rbx = op("sync", lambda s, rcc=rcc: s.dma_start_transpose(
                    c_pr[:, rcc, :], car_out[:, 128 * rcc:128 * (rcc + 1)]),
                    inc="sRB", n=16)
            rb_prev = rbx

            dT1 = op("vector", lambda v, xnT=xnT: v.tensor_sub(
                c_pT[:].rearrange("p a r -> p (a r)"),
                c_pT[:].rearrange("p a r -> p (a r)"),
                xnT[:].rearrange("p a r -> p (a r)")),
                waits=[("sRB", rbx)], inc="sDVE")
            dT2 = op("vector", lambda v: v.tensor_scalar_mul(
                c_pT[:].rearrange("p a r -> p (a r)"),
                c_pT[:].rearrange("p a r -> p (a r)"), TWO_N),
                waits=[("sDVE", dT1)], inc="sDVE")
            dR1 = op("vector", lambda v, xn=xn: v.tensor_sub(
                c_pr[:].rearrange("p a r -> p (a r)"),
                c_pr[:].rearrange("p a r -> p (a r)"),
                xn[:].rearrange("p a r -> p (a r)")),
                waits=[("sRB", rbx)], inc="sDVE")
            dR2 = op("vector", lambda v: v.tensor_scalar_mul(
                c_pr[:].rearrange("p a r -> p (a r)"),
                c_pr[:].rearrange("p a r -> p (a r)"), TWO_N),
                waits=[("sDVE", dR1)], inc="sDVE")

            dhv = ps_b[:].rearrange("p a (c m) -> p (a c) m", m=MK)
            first = True
            for ic in range(RC):
                for kc in range(DC):
                    w = []
                    if first:
                        w = [("sDVE", dT2), ("sACT", ev_par)]
                        first = False
                    op("tensor", lambda t, ic=ic, kc=kc, dhv=dhv: t.matmul(
                        dhv[:, ic, :], c_pT[:, kc, 128 * ic:128 * (ic + 1)],
                        c_w1[:, kc, 0:MK], start=(kc == 0), stop=(kc == DC - 1)),
                        waits=w, inc="sPE")
            pe_dh = S.v("sPE")
            dpre_ = op("vector", lambda v, dhv=dhv: v.scalar_tensor_tensor(
                c_dpre[:].rearrange("p a m -> p (a m)"),
                dhv[:].rearrange("p a m -> p (a m)"), c_nl[:],
                c_sgr[:].rearrange("p a m -> p (a m)"), ALU.mult, ALU.mult),
                waits=[("sPE", pe_dh), ("sDVE", sgr_done)], inc="sDVE")
            psb_free[0] = psb_free[1] = ("sDVE", dpre_)

            upd_w1 = 0
            uw_hist = {}
            for dc in range(DC):
                for kc in range(RC):
                    w = []
                    if dc == 0 and kc == 0:
                        w = [("sDVE", max(dR2, h1r_done, dpre_))]
                    if kc == 0 and dc >= 2:
                        w.append(("sDVE", uw_hist[dc - 2]))
                    if kc == 0 and dc < 2 and psb_free[dc % 2]:
                        w.append(psb_free[dc % 2])
                    op("tensor", lambda t, dc=dc, kc=kc: t.matmul(
                        ps_b[:, dc % 2, 0:MK + 1], c_pr[:, kc, 128 * dc:128 * (dc + 1)],
                        c_h1r[:, kc, :], start=(kc == 0), stop=(kc == RC - 1)),
                        waits=w, inc="sPE")
                pe_g = S.v("sPE")
                upd_w1 = op("vector", lambda v, dc=dc: v.scalar_tensor_tensor(
                    c_w1[:, dc, :], c_w1[:, dc, :], c_fv[:],
                    ps_b[:, dc % 2, 0:MK + 1], ALU.mult, ALU.add),
                    waits=[("sPE", pe_g), ("sDVE", fv_)] if dc == 0 else [("sPE", pe_g)],
                    inc="sDVE")
                uw_hist[dc] = upd_w1
                psb_free[dc % 2] = ("sDVE", upd_w1)
            b18_ = op("vector", lambda v: v.tensor_scalar_mul(
                c_b18[:], c_w1[:, :, MK:MK + 1], 0.125),
                waits=[("sDVE", upd_w1)], inc="sDVE")

            upd_w0 = 0
            uw_hist = {}
            for dc in range(DC):
                for kc in range(RC):
                    w = []
                    if dc == 0 and kc == 0:
                        w = [("sDVE", dpre_)]
                    if kc == 0 and dc >= 2:
                        w.append(("sDVE", uw_hist[dc - 2]))
                    if kc == 0 and dc < 2 and psb_free[dc % 2]:
                        w.append(psb_free[dc % 2])
                    op("tensor", lambda t, dc=dc, kc=kc, xn=xn: t.matmul(
                        ps_b[:, dc % 2, 0:MK], xn[:, kc, 128 * dc:128 * (dc + 1)],
                        c_dpre[:, kc, :], start=(kc == 0), stop=(kc == RC - 1)),
                        waits=w, inc="sPE")
                pe_g = S.v("sPE")
                upd_w0 = op("vector", lambda v, dc=dc: v.scalar_tensor_tensor(
                    c_w0[:, dc, :], c_w0[:, dc, :], c_fv[:],
                    ps_b[:, dc % 2, 0:MK], ALU.mult, ALU.add),
                    waits=[("sPE", pe_g)], inc="sDVE")
                uw_hist[dc] = upd_w0
                psb_free[dc % 2] = ("sDVE", upd_w0)

            for mc in range(2):
                for kc in range(RC):
                    op("tensor", lambda t, mc=mc, kc=kc: t.matmul(
                        ps_s[:, 128 + mc:129 + mc],
                        c_dpre[:, kc, 128 * mc:128 * (mc + 1)],
                        onec[:], start=(kc == 0), stop=(kc == RC - 1)),
                        waits=[("sDVE", dpre_)] if (mc == 0 and kc == 0) else (),
                        inc="sPE")
            pe_b0 = S.v("sPE")
            upd_b0 = op("vector", lambda v: v.scalar_tensor_tensor(
                c_b0[:].rearrange("p a o -> p (a o)"),
                c_b0[:].rearrange("p a o -> p (a o)"), c_fv[:],
                ps_s[:, 128:130], ALU.mult, ALU.add),
                waits=[("sPE", pe_b0)], inc="sDVE")

            flip_done = 0
            for mc in range(2):
                for d4 in range(4):
                    sl = (2 * mc + d4) % 2
                    w = []
                    if mc == 0 and d4 == 0:
                        w = [("sDVE", max(upd_w1, b18_))]
                    if psd_evac[sl]:
                        w.append(("sDVE", psd_evac[sl]))
                    for j in range(4):
                        dc = 4 * d4 + j
                        op("tensor", lambda t, dc=dc, mc=mc, sl=sl, j=j: t.transpose(
                            psd[sl][:, j, :], c_w1[:, dc, 128 * mc:128 * (mc + 1)],
                            ident[:]), waits=w if j == 0 else (), inc="sPE")
                    pe_b = S.v("sPE")
                    flip_done = op("vector", lambda v, mc=mc, d4=d4, sl=sl: v.tensor_copy(
                        c_w1T[:, mc, 512 * d4:512 * (d4 + 1)],
                        psd[sl][:].rearrange("p a b -> p (a b)")),
                        waits=[("sPE", pe_b)], inc="sDVE")
                    psd_evac[sl] = flip_done

            first = True
            for kc in range(DC):
                for ic in range(2):
                    w = []
                    if first:
                        w = [("sDVE", max(upd_w0, upd_b0)), ("sACT", S.v("sACT"))]
                        first = False
                    op("tensor", lambda t, kc=kc, ic=ic, xnT=xnT: t.matmul(
                        ps_a[:, ic, :], c_w0[:, kc, 128 * ic:128 * (ic + 1)],
                        xnT[:, kc, :], start=(kc == 0), stop=(kc == DC - 1)),
                        waits=w, inc="sPE")
            pe_fwd2 = S.v("sPE")
            a2 = 0
            for mc in range(2):
                a2 = op("scalar", lambda s, mc=mc: s.activation(
                    c_h1T[:, mc, :], ps_a[:, mc, :], AF.Silu,
                    bias=c_b0[:, mc, :], scale=1.0),
                    waits=[("sPE", pe_fwd2)] if mc == 0 else (), inc="sACT")
            ev_p2 = 0
            ev_hist = {}
            for dc in range(DC):
                for kc in range(2):
                    w = []
                    if dc == 0 and kc == 0:
                        w = [("sACT", a2), ("sDVE", flip_done)]
                    if kc == 0 and dc >= 2:
                        w.append(("sACT", ev_hist[dc - 2]))
                    if kc == 0 and dc < 2 and psb_free[dc % 2]:
                        w.append(psb_free[dc % 2])
                    op("tensor", lambda t, dc=dc, kc=kc: t.matmul(
                        ps_b[:, dc % 2, :], c_w1T[:, kc, 128 * dc:128 * (dc + 1)],
                        c_h1T[:, kc, :], start=(kc == 0), stop=(kc == 1)),
                        waits=w, inc="sPE")
                pe_p2 = S.v("sPE")
                wex = [("sPE", pe_p2)]
                if dc == 0:
                    wex += [("sDVE", b18_), ("sGD", S.v("sGD"))]
                ev_p2 = op("scalar", lambda s, dc=dc: s.activation(
                    c_p2[:, dc, :], ps_b[:, dc % 2, :], AF.Identity,
                    bias=c_b18[:, dc, :], scale=1.0), waits=wex, inc="sACT")
                ev_hist[dc] = ev_p2
                psb_free[dc % 2] = ("sACT", ev_p2)
            ev_p2_prev = ev_p2

            half = 32 if n_steps == NCH else max(1, n_steps // 2)
            h = (c // half) if full_run else 0
            p = (c % half) if full_run else c % npair
            gd = 0
            for b in range(4):
                gd = op("gpsimd", lambda g, b=b, p=p, h=h: g.dma_start(
                    out=pairbuf[p, 2 * b + h].rearrange("(a q) r -> q a r", q=128),
                    in_=c_p2[:, :, 128 * b:128 * (b + 1)]),
                    waits=[("sACT", ev_p2)] if b == 0 else (), inc="sGD", n=16)
            gd_pair_prev = gd
            if dbg.get("p2st"):
                gd_pair_prev = op("gpsimd", lambda g, c=c: g.dma_start(
                    out=dbg_outs["d_p2st"][c].rearrange("(a q) r -> q a r", q=128),
                    in_=c_p2[:]), waits=[("sACT", ev_p2)], inc="sGD", n=16)
            if full_run and h == 1:
                pending_rs = (lambda g, p=p: g.collective_compute(
                    "ReduceScatter", ALU.add, RG8,
                    ins=[pairbuf[p].rearrange("s d r -> (s d) r").opt()],
                    outs=[myattn[p].opt()]),
                    [("sGD", S.v("sGD"))])

        if pending_rs is not None:
            op("gpsimd", pending_rs[0], waits=pending_rs[1], inc="sRS")
            pending_rs = None
        if dbg.get("myattn"):
            op("gpsimd", lambda g: g.dma_start(
                out=dbg_outs["d_myattn"].rearrange("n d r -> (n d r)").rearrange("(a b) -> a b", a=1),
                in_=myattn.rearrange("n d r -> (n d r)").rearrange("(a b) -> a b", a=1)),
                waits=[("sRS", S.v("sRS"))], inc="sGD", n=16)

        barrier = [("sDVE", S.v("sDVE")), ("sACT", S.v("sACT")),
                   ("sPE", S.v("sPE")), ("sGD", S.v("sGD")),
                   ("sDMA", S.v("sDMA")), ("sPOOL", S.v("sPOOL"))]

        if do_ffn:
            dq_gd, dq_dma = emit_dequant(
                S, dict(gwp=gwp, gsc=gsc, uwp=uwp, usc=usc, dwp=dwp, dsc=dsc,
                        wg_nm=wg_nm, wu_nm=wu_nm, wd_nm=wd_nm, wgT=wgT, wuT=wuT,
                        wdT=wdT, q_pk=q_pk, q_cd=q_cd, q_s=q_s, q_m=q_m, q_a=q_a,
                        q_b=q_b, q_c=q_c, q_sc=q_sc, q_out=q_out, q_slab=q_slab,
                        b75=b75, barrier=barrier,
                        dbg_outs=dbg_outs if dbg.get("wrow") else None))
            emit_ffn(S, dict(
                xloc=xloc, myattn=myattn, ynbuf=ynbuf, out=out,
                wgT=wgT, wuT=wuT, wdT=wdT,
                f_acc=f_acc, f_ynT=f_ynT, f_int=f_int, f_wp=f_wp, f_wd=f_wd,
                f_gel=f_gel, f_ay=f_ay, f_st=f_st, f_mv=f_mv, f_rs=f_rs,
                epsT=epsT, ps_a=ps_a, ps_b=ps_b, ps_f=ps_f, ps_s=ps_s,
                n_rs=S.v("sRS"), full_run=full_run,
                dq_barrier=[("sGD", dq_gd), ("sDMA", dq_dma)]))
        op("gpsimd", lambda g: g.engine_nop(),
           waits=[("sST", S.v("sST")), ("sGD", S.v("sGD")),
                  ("sDMA", S.v("sDMA")), ("sDVE", S.v("sDVE")),
                  ("sACT", S.v("sACT")), ("sPE", S.v("sPE")),
                  ("sRS", S.v("sRS")), ("sCOLL", S.v("sCOLL"))])

        @blk.sync
        def _(e):
            S.replay("sync", e)

        @blk.vector
        def _(e):
            S.replay("vector", e)

        @blk.scalar
        def _(e):
            S.replay("scalar", e)

        @blk.tensor
        def _(e):
            S.replay("tensor", e)

        @blk.gpsimd
        def _(e):
            S.replay("gpsimd", e)

    return nc


def emit_dequant(S, E):
    op = S.op
    bar = list(E["barrier"])
    q_pk, q_cd, q_s, q_m = E["q_pk"], E["q_cd"], E["q_s"], E["q_m"]
    q_a, q_b, q_c, q_sc, q_out = E["q_a"], E["q_b"], E["q_c"], E["q_sc"], E["q_out"]
    qi = q_m[:].bitcast(I32)[:, 0:1024]
    jobs = [(E["gwp"], E["gsc"], E["wg_nm"], NF, D),
            (E["uwp"], E["usc"], E["wu_nm"], NF, D),
            (E["dwp"], E["dsc"], E["wd_nm"], D, NF)]
    first = True
    so_prev = 0
    nat_gd = 0
    for (pk, sc, nat, rows, K) in jobs:
        for r0 in range(0, rows, 128):
            for k0 in range(0, K, 2048):
                w = (bar + [("sGD", S.v("sGD"))]) if first else \
                    [("sDVE", so_prev), ("sGD", S.v("sGD"))]
                first = False
                dl = op("sync", lambda s, pk=pk, r0=r0, k0=k0: s.dma_start(
                    out=q_pk[:], in_=pk[r0:r0 + 128, k0 // 2:k0 // 2 + 1024]),
                    waits=w, inc="sDMA", n=16)
                ds = op("sync", lambda s, sc=sc, r0=r0, k0=k0: s.dma_start(
                    out=q_sc[:], in_=sc[r0:r0 + 128, k0 // 32:k0 // 32 + 64]),
                    inc="sDMA", n=16)
                lo = op("vector", lambda v: v.tensor_single_scalar(
                    qi, q_pk[:], 15, ALU.bitwise_and),
                    waits=[("sDMA", ds)], inc="sDVE")
                cl = op("vector", lambda v: v.tensor_copy(
                    q_cd[:].rearrange("p (a t) -> p a t", t=2)[:, :, 0:1],
                    qi.rearrange("p (a o) -> p a o", o=1)),
                    waits=[("sDVE", lo)], inc="sDVE")
                hi = op("vector", lambda v: v.tensor_single_scalar(
                    qi, q_pk[:], 4, ALU.logical_shift_right),
                    waits=[("sDVE", cl)], inc="sDVE")
                ch = op("vector", lambda v: v.tensor_copy(
                    q_cd[:].rearrange("p (a t) -> p a t", t=2)[:, :, 1:2],
                    qi.rearrange("p (a o) -> p a o", o=1)),
                    waits=[("sDVE", hi)], inc="sDVE")
                sg = op("scalar", lambda s: s.activation(
                    q_s[:], q_cd[:], AF.Sign, bias=E["b75"][:], scale=-1.0),
                    waits=[("sDVE", ch)], inc="sACT")
                mm = op("vector", lambda v: v.scalar_tensor_tensor(
                    q_m[:], q_s[:], 4.0, q_cd[:], ALU.mult, ALU.add),
                    waits=[("sACT", sg)], inc="sDVE")
                aa = op("gpsimd", lambda g: g.tensor_scalar(
                    q_a[:], q_m[:], 0.5, -2.0, ALU.mult, ALU.add),
                    waits=[("sDVE", mm)], inc="sPOOL")
                bb = op("gpsimd", lambda g: g.tensor_scalar(
                    q_b[:], q_m[:], 1.0, -6.0, ALU.mult, ALU.add), inc="sPOOL")
                cc2 = op("gpsimd", lambda g: g.tensor_scalar(
                    q_c[:], q_m[:], 2.0, -16.0, ALU.mult, ALU.add), inc="sPOOL")
                m1 = op("vector", lambda v: v.tensor_max(q_a[:], q_a[:], q_b[:]),
                        waits=[("sPOOL", bb)], inc="sDVE")
                m2 = op("vector", lambda v: v.tensor_max(q_a[:], q_a[:], q_c[:]),
                        waits=[("sPOOL", cc2), ("sDVE", m1)], inc="sDVE")
                vv = op("vector", lambda v: v.tensor_mul(q_a[:], q_a[:], q_s[:]),
                        waits=[("sDVE", m2)], inc="sDVE")
                so = op("vector", lambda v: v.tensor_tensor(
                    q_out[:].rearrange("p (a t) -> p a t", t=32),
                    q_a[:].rearrange("p (a t) -> p a t", t=32),
                    q_sc[:].rearrange("p (a o) -> p a o", o=1).broadcast_to([128, 64, 32]),
                    ALU.mult), waits=[("sDVE", vv), ("sDMA", ds)], inc="sDVE")
                so_prev = so
                nat_gd = op("gpsimd", lambda g, nat=nat, r0=r0, k0=k0: g.dma_start(
                    out=nat[r0:r0 + 128, k0:k0 + 2048], in_=q_out[:]),
                    waits=[("sDVE", so)], inc="sGD", n=16)
    q_slab = E["q_slab"]
    tj = [(E["wg_nm"], E["wgT"], NF, 0), (E["wu_nm"], E["wuT"], NF, 1),
          (E["wd_nm"], E["wdT"], D, 2)]
    slab_wr = [0, 0]
    xbsem = ["sXB0", "sXB1"]
    i = 0
    last_gd = nat_gd
    for (nat, tout, rows, mi) in tj:
        K = tout.shape[0]
        for cc in range(K // 128):
            sl = i % 2
            w = [("sGD", S.v("sGD"))]
            i += 1
            xr = op("sync", lambda s, nat=nat, cc=cc, rows=rows, sl=sl: s.dma_start_transpose(
                q_slab[sl][:, 0:rows], nat[:, 128 * cc:128 * (cc + 1)]),
                waits=w, inc=xbsem[sl], n=16)
            last_gd = op("gpsimd", lambda g, tout=tout, cc=cc, rows=rows, sl=sl: g.dma_start(
                out=tout[128 * cc:128 * (cc + 1), :], in_=q_slab[sl][:, 0:rows]),
                waits=[(xbsem[sl], xr)], inc="sGD", n=16)
            slab_wr[sl] = last_gd
            if E.get("dbg_outs") and cc == 1:
                last_gd = op("gpsimd", lambda g, mi=mi, rows=rows, sl=sl: g.dma_start(
                    out=E["dbg_outs"]["d_wrow"][mi][:, 0:D],
                    in_=q_slab[sl][:, 0:D]),
                    waits=[(xbsem[sl], xr)], inc="sGD", n=16)
                slab_wr[sl] = last_gd
    return last_gd, S.v("sDMA")


def emit_ffn(S, E):
    op = S.op
    yn_gd = 0
    f_acc, f_ynT, f_int = E["f_acc"], E["f_ynT"], E["f_int"]
    f_wp, f_wd = E["f_wp"], E["f_wd"]
    f_gel, f_ay = E["f_gel"], E["f_ay"]
    f_st, f_mv, f_rs = E["f_st"], E["f_mv"], E["f_rs"]
    ps_a, ps_b, ps_f, ps_s = E["ps_a"], E["ps_b"], E["ps_f"], E["ps_s"]
    xloc, myattn, ynbuf, out = E["xloc"], E["myattn"], E["ynbuf"], E["out"]
    wgT, wuT, wdT = E["wgT"], E["wuT"], E["wdT"]
    dqb = list(E["dq_barrier"])
    wgTv = wgT.rearrange("(a p) n -> p a n", p=128)
    wuTv = wuT.rearrange("(a p) n -> p a n", p=128)
    wdTv = wdT.rearrange("(a p) h -> p a h", p=128)

    wp_free = [0, 0, 0]
    st_prev = 0
    gate_ev = {}
    up_ev = {}
    dn_ev = {}
    wd_free = 0

    for t4 in range(NTILE):
        for cc in range(8):
            ch = 8 * t4 + cc
            wat = list(dqb) if (t4 == 0 and cc == 0) else []
            if E["full_run"]:
                wat.append(("sRS", ch + 1))
            wat.append(("sDVE", S.v("sDVE")))
            wat.append(("sGD", S.v("sGD")))
            xa = op("sync", lambda s, ch=ch: s.dma_start_transpose(
                f_ay[:], myattn[ch]), waits=wat, inc="sDMA", n=16)
            wl = [("sDMA", xa)]
            if t4 > 0 and cc == 0:
                wl.append(("sST", st_prev))
            xl = op("sync", lambda s, ch=ch, cc=cc: s.dma_start(
                out=f_acc[:, cc, :], in_=xloc[128 * ch:128 * (ch + 1), :]),
                waits=[("sST", st_prev)] if (t4 > 0 and cc == 0) else [],
                inc="sDMA", n=16)
            ya = op("vector", lambda v, cc=cc: v.tensor_add(
                f_acc[:, cc, :], f_acc[:, cc, :], f_ay[:]),
                waits=[("sDMA", S.v("sDMA"))], inc="sDVE")
            last = 0
            for sg in range(4):
                last = op("vector", lambda v, cc=cc, sg=sg: v.bn_stats(
                    f_st[:, 2 * sg:2 * sg + 2, :].rearrange("p a b -> p (a b)"),
                    f_acc[:, cc, 512 * sg:512 * (sg + 1)]),
                    waits=[("sDVE", ya)] if sg == 0 else (), inc="sDVE")
            ag = op("vector", lambda v: v.bn_aggr(
                f_mv[:], f_st[:].rearrange("p a b -> p (a b)")),
                waits=[("sDVE", last)], inc="sDVE")
            sq = op("scalar", lambda s: s.activation(
                f_rs[:], f_mv[:, 1:2], AF.Sqrt, bias=E["epsT"][:], scale=1.0),
                waits=[("sDVE", ag)], inc="sACT")
            rc_ = op("vector", lambda v: v.reciprocal(f_rs[:], f_rs[:]),
                     waits=[("sACT", sq)], inc="sDVE")
            nm = op("vector", lambda v, cc=cc: v.tensor_scalar(
                f_ay[:], f_acc[:, cc, :], f_mv[:, 0:1], f_rs[:],
                ALU.subtract, ALU.mult), waits=[("sDVE", rc_)], inc="sDVE")
            yn_gd = op("gpsimd", lambda g, cc=cc: g.dma_start(
                out=ynbuf[128 * cc:128 * (cc + 1), :], in_=f_ay[:]),
                waits=[("sDVE", nm)], inc="sGD", n=16)
        ynT_d = 0
        for dc in range(DC):
            w = [("sGD", yn_gd)]
            if t4 > 0 and dc == 0:
                w.append(("sPE", S.v("sPE")))
            ynT_d = op("sync", lambda s, dc=dc: s.dma_start_transpose(
                f_ynT[:, dc, :], ynbuf[:, 128 * dc:128 * (dc + 1)]),
                waits=w if dc == 0 else [("sGD", yn_gd)], inc="sDMA", n=16)

        for g in range(NGRP):
            iv_last = 0
            wd_l0 = op("gpsimd", lambda gg, g=g: gg.dma_start(
                out=f_wd[:], in_=wdTv[:, 16 * g:16 * g + 16, 0:1024]),
                waits=[("sPE", wd_free)], inc="sGD", n=16)
            for sb in range(NSUB):
                n0 = 2048 * g + NGSZ * sb
                bg = (2 * sb) % 3
                bu = (2 * sb + 1) % 3
                lg = op("sync", lambda s, n0=n0, bg=bg: s.dma_start(
                    out=f_wp[bg][:], in_=wgTv[:, :, n0:n0 + NGSZ]),
                    waits=[("sPE", wp_free[bg])], inc="sDMA", n=16)
                lu = op("sync", lambda s, n0=n0, bu=bu: s.dma_start(
                    out=f_wp[bu][:], in_=wuTv[:, :, n0:n0 + NGSZ]),
                    waits=[("sPE", wp_free[bu])], inc="sDMA", n=16)
                for ic in range(2):
                    for tj in range(TT // 512):
                        rot = tj % 2
                        for kc in range(DC):
                            w = []
                            if kc == 0:
                                w = [("sDMA", lu)]
                                if gate_ev.get(rot):
                                    w.append(("sACT", gate_ev[rot]))
                            op("tensor", lambda t, ic=ic, tj=tj, kc=kc, rot=rot, bg=bg: t.matmul(
                                ps_a[:, rot, :],
                                f_wp[bg][:, kc, 128 * ic:128 * (ic + 1)],
                                f_ynT[:, kc, 512 * tj:512 * (tj + 1)],
                                start=(kc == 0), stop=(kc == DC - 1)),
                                waits=w, inc="sPE")
                        peg = S.v("sPE")
                        gate_ev[rot] = op("scalar", lambda s, rot=rot: s.activation(
                            f_gel[:], ps_a[:, rot, :], AF.Gelu_apprx_tanh),
                            waits=[("sPE", peg)], inc="sACT")
                        for kc in range(DC):
                            w = []
                            if kc == 0:
                                w = [("sDMA", lu)]
                                if up_ev.get(rot):
                                    w.append(("sDVE", up_ev[rot]))
                            op("tensor", lambda t, ic=ic, tj=tj, kc=kc, rot=rot, bu=bu: t.matmul(
                                ps_b[:, rot, :],
                                f_wp[bu][:, kc, 128 * ic:128 * (ic + 1)],
                                f_ynT[:, kc, 512 * tj:512 * (tj + 1)],
                                start=(kc == 0), stop=(kc == DC - 1)),
                                waits=w, inc="sPE")
                        peu = S.v("sPE")
                        iv_last = op("vector", lambda v, sb=sb, ic=ic, tj=tj, rot=rot: v.tensor_mul(
                            f_int[:, 2 * sb + ic, 512 * tj:512 * (tj + 1)],
                            f_gel[:], ps_b[:, rot, :]),
                            waits=[("sACT", gate_ev[rot]), ("sPE", peu)], inc="sDVE")
                        up_ev[rot] = iv_last
                wp_free[bg] = S.v("sPE")
                wp_free[bu] = S.v("sPE")
            for hh in range(2):
                if hh == 1:
                    wd_l0 = op("gpsimd", lambda gg, g=g: gg.dma_start(
                        out=f_wd[:], in_=wdTv[:, 16 * g:16 * g + 16, 1024:2048]),
                        waits=[("sPE", S.v("sPE"))], inc="sGD", n=16)
                for hq in range(2):
                    for tc in range(TT // 128):
                        rot = (hq * (TT // 128) + tc) % 2
                        pst = [ps_f, ps_s][rot]
                        for kc in range(DC):
                            w = []
                            if kc == 0:
                                w = [("sGD", wd_l0)]
                                if hh == 0 and hq == 0 and tc == 0:
                                    w.append(("sDVE", iv_last))
                                if dn_ev.get(rot):
                                    w.append(("sDVE", dn_ev[rot]))
                            op("tensor", lambda t, hq=hq, tc=tc, kc=kc, pst=pst: t.matmul(
                                pst[:], f_int[:, kc, 128 * tc:128 * (tc + 1)],
                                f_wd[:, kc, 512 * hq:512 * (hq + 1)],
                                start=(kc == 0), stop=(kc == DC - 1)),
                                waits=w, inc="sPE")
                        ped = S.v("sPE")
                        dn_ev[rot] = op("vector", lambda v, hh=hh, hq=hq, tc=tc, pst=pst: v.tensor_add(
                            f_acc[:, tc, 1024 * hh + 512 * hq:1024 * hh + 512 * (hq + 1)],
                            f_acc[:, tc, 1024 * hh + 512 * hq:1024 * hh + 512 * (hq + 1)],
                            pst[:]), waits=[("sPE", ped)], inc="sDVE")
                wd_free = S.v("sPE")
        st_prev = op("gpsimd", lambda g, t4=t4: g.dma_start(
            out=out[TT * t4:TT * (t4 + 1), :].rearrange("(a p) d -> p a d", p=128),
            in_=f_acc[:]), waits=[("sDVE", S.v("sDVE"))], inc="sST", n=16)


def make_inmaps(inp):
    x = np.ascontiguousarray(np.asarray(inp["L0_attn_x"], np.float32))
    W0 = np.asarray(inp["L0_attn_W0"], np.float32)
    W1 = np.asarray(inp["L0_attn_W1"], np.float32)
    b0 = np.asarray(inp["L0_attn_b0"], np.float32)
    b1 = np.asarray(inp["L0_attn_b1"], np.float32)
    lr_w = np.asarray(inp["L0_attn_lr_w"], np.float32)
    fg_w = np.asarray(inp["L0_attn_fg_w"], np.float32)
    lrfg = np.ascontiguousarray(np.stack([lr_w[0], fg_w[0]], axis=1))
    xflat = x.reshape(4 * 8192, D)
    maps = []
    for k in range(8):
        mk = slice(MK * k, MK * (k + 1))
        maps.append({
            "xfull": x,
            "xloc": np.ascontiguousarray(xflat[TLOC * k:TLOC * (k + 1)]),
            "w0kT": np.ascontiguousarray(W0[mk].T),
            "w1dm": np.ascontiguousarray(W1[:, mk]),
            "w1kT": np.ascontiguousarray(W1[:, mk].T),
            "b0k": np.ascontiguousarray(b0[mk].reshape(2, 128).T),
            "b1f": np.ascontiguousarray(b1.reshape(DC, 128).T),
            "lrfg": np.ascontiguousarray(lrfg.reshape(DC, 128, 2).transpose(1, 0, 2)),
            "lrb": np.asarray(inp["L0_attn_lr_b"], np.float32),
            "fgb": np.asarray(inp["L0_attn_fg_b"], np.float32),
            "gwp": np.asarray(inp["L0_ffn_gate_W_packed"], np.int32),
            "gsc": np.asarray(inp["L0_ffn_gate_scales"], np.float32),
            "uwp": np.asarray(inp["L0_ffn_up_W_packed"], np.int32),
            "usc": np.asarray(inp["L0_ffn_up_scales"], np.float32),
            "dwp": np.asarray(inp["L0_ffn_down_W_packed"], np.int32),
            "dsc": np.asarray(inp["L0_ffn_down_scales"], np.float32),
        })
    return maps


def assemble(results):
    return np.concatenate([r["out"] for r in results], axis=0).reshape(4, 8192, D)


def kernel(**inputs):
    maps = make_inmaps(inputs)
    nc = build()
    res = run_bass_kernel_spmd(nc, maps, core_ids=list(range(8)))
    return assemble(res.results)


# version 02
# speedup vs baseline: 1.1571x; 1.1571x over previous
from concourse.bass_utils import run_bass_kernel_spmd

import numpy as np
import concourse.bass as bass
import concourse.mybir as mybir

F32 = mybir.dt.float32
BF16 = mybir.dt.bfloat16
I32 = mybir.dt.int32
AF = mybir.ActivationFunctionType
ALU = mybir.AluOpType
AX = mybir.AxisListType
RG8 = [[0, 1, 2, 3, 4, 5, 6, 7]]

D = 2048
MK = 256
RW = 512
DC = D // 128
RC = RW // 128
NCH = 64
NF = 16384
TLOC = 4096
TT = 1024
NTILE = TLOC // TT
NGRP = 8
NSUB = 8
NGSZ = 256
TWO_N = 2.0 / (RW * D)


class Sched:
    def __init__(self, nc):
        self.nc = nc
        self.items = []
        self.cnt = {}
        self.sems = {}

    def sem(self, ctx, name):
        h = ctx.enter_context(self.nc.semaphore(name))
        self.sems[name] = h
        self.cnt[name] = 0

    def op(self, eng, fn, waits=(), inc=None, n=1):
        v = None
        if inc is not None:
            self.cnt[inc] += n
            v = self.cnt[inc]
        self.items.append((eng, fn, tuple(waits), inc, n))
        return v

    def v(self, name):
        return self.cnt[name]

    def replay(self, ename, eng):
        for (e, fn, waits, inc, n) in self.items:
            if e != ename:
                continue
            for (s, val) in waits:
                if val and val > 0:
                    eng.wait_ge(self.sems[s], val)
            inst = fn(eng)
            if inc is not None:
                inst.then_inc(self.sems[inc], n)


def build(n_steps=NCH, do_ffn=True, dbg=None, force_rs=False, dq_on=True):
    dbg = dbg or {}
    nc = bass.Bass("TRN2", target_bir_lowering=False)
    full_run = (n_steps == NCH) or force_rs

    xfull = nc.dram_tensor("xfull", [4, 8192, D], F32, kind="ExternalInput").ap()
    xloc = nc.dram_tensor("xloc", [TLOC, D], F32, kind="ExternalInput").ap()
    w0kTd = nc.dram_tensor("w0kT", [D, MK], F32, kind="ExternalInput").ap()
    w1dmd = nc.dram_tensor("w1dm", [D, MK], F32, kind="ExternalInput").ap()
    w1kTd = nc.dram_tensor("w1kT", [MK, D], F32, kind="ExternalInput").ap()
    b0kd = nc.dram_tensor("b0k", [128, 2], F32, kind="ExternalInput").ap()
    b1d = nc.dram_tensor("b1f", [128, DC], F32, kind="ExternalInput").ap()
    lrfgd = nc.dram_tensor("lrfg", [128, DC, 2], F32, kind="ExternalInput").ap()
    lrbd = nc.dram_tensor("lrb", [1], F32, kind="ExternalInput").ap()
    fgbd = nc.dram_tensor("fgb", [1], F32, kind="ExternalInput").ap()
    gwp = nc.dram_tensor("gwp", [NF, D // 2], I32, kind="ExternalInput").ap()
    gsc = nc.dram_tensor("gsc", [NF, D // 32], F32, kind="ExternalInput").ap()
    uwp = nc.dram_tensor("uwp", [NF, D // 2], I32, kind="ExternalInput").ap()
    usc = nc.dram_tensor("usc", [NF, D // 32], F32, kind="ExternalInput").ap()
    dwp = nc.dram_tensor("dwp", [D, NF // 2], I32, kind="ExternalInput").ap()
    dsc = nc.dram_tensor("dsc", [D, NF // 32], F32, kind="ExternalInput").ap()
    out = nc.dram_tensor("out", [TLOC, D], F32, kind="ExternalOutput").ap()

    car_in = nc.dram_tensor("car_in", [D, RW], BF16, kind="Internal").ap()
    car_out = nc.dram_tensor("car_out", [D, RW], BF16, kind="Internal").ap()
    npair = 32
    pairbuf = nc.dram_tensor("pairbuf", [npair, 8, D, 128], BF16, kind="Internal").ap()
    myattn = nc.dram_tensor("myattn", [npair, D, 128], BF16, kind="Internal").ap()
    wg_nm = nc.dram_tensor("wg_nm", [NF, D], BF16, kind="Internal").ap()
    wu_nm = nc.dram_tensor("wu_nm", [NF, D], BF16, kind="Internal").ap()
    wd_nm = nc.dram_tensor("wd_nm", [D, NF], BF16, kind="Internal").ap()
    wgT = nc.dram_tensor("wgT", [D, NF], BF16, kind="Internal").ap()
    wuT = nc.dram_tensor("wuT", [D, NF], BF16, kind="Internal").ap()
    wdT = nc.dram_tensor("wdT", [NF, D], BF16, kind="Internal").ap()
    ynbuf = nc.dram_tensor("ynbuf", [TT, D], BF16, kind="Internal").ap()

    dbg_outs = {}
    if dbg.get("p2st"):
        dbg_outs["d_p2st"] = nc.dram_tensor("d_p2st", [n_steps, D, RW], BF16,
                                            kind="ExternalOutput").ap()
    if dbg.get("pred"):
        dbg_outs["d_pred"] = nc.dram_tensor("d_pred", [n_steps, D, RW], BF16,
                                            kind="ExternalOutput").ap()
    if dbg.get("myattn"):
        dbg_outs["d_myattn"] = nc.dram_tensor("d_myattn", [npair, D, 128], BF16,
                                              kind="ExternalOutput").ap()
    if dbg.get("wrow"):
        dbg_outs["d_wrow"] = nc.dram_tensor("d_wrow", [3, 128, D], BF16,
                                            kind="ExternalOutput").ap()

    SB_BASE = 16768
    SB_LIMIT = 229344
    off = [SB_BASE]

    def salloc(name, shape, dtype, align=64):
        nbytes = int(np.prod(shape[1:])) * mybir.dt.size(dtype)
        off[0] = (off[0] + align - 1) // align * align
        t = nc.alloc_sbuf_tensor_at(name, list(shape), dtype, offset=off[0])
        off[0] += nbytes
        assert off[0] <= SB_LIMIT, f"SBUF overflow at {name}: {off[0]}"
        return t

    ident = salloc("ident", [128, 128], BF16)
    kones = salloc("kones", [1, 128], F32)
    epsT = salloc("epsT", [128, 1], F32)
    b75 = salloc("b75", [128, 1], F32)
    onec = salloc("onec", [128, 1], BF16)
    bm2 = salloc("bm2", [128, 1], F32)
    bm6 = salloc("bm6", [128, 1], F32)
    bm16 = salloc("bm16", [128, 1], F32)
    PERSIST_END = off[0]

    c_w0 = salloc("c_w0", [128, DC, MK], BF16)
    c_w1 = salloc("c_w1", [128, DC, MK + 1], BF16)
    c_w1T = salloc("c_w1T", [128, 2, D], BF16)
    c_b0 = salloc("c_b0", [128, 2, 1], F32)
    c_lrfg = salloc("c_lrfg", [128, DC, 2], BF16)
    c_lrb = salloc("c_lrb", [1, 1], F32)
    c_fgb = salloc("c_fgb", [1, 1], F32)
    c_fv = salloc("c_fv", [128, 1], F32)
    c_nl = salloc("c_nl", [128, 1], F32)
    c_b18 = salloc("c_b18", [128, DC, 1], BF16)
    c_xraw = salloc("c_xraw", [128, 2, D], F32)
    c_xn = [salloc(f"c_xn{i}", [128, RC, D], BF16) for i in range(2)]
    c_xnT = [salloc(f"c_xnT{i}", [128, DC, RW], BF16) for i in range(2)]
    c_h1T = salloc("c_h1T", [128, 2, RW], BF16)
    c_h2T = salloc("c_h2T", [128, 2, RW], BF16)
    c_h1r = salloc("c_h1r", [128, RC, MK + 1], BF16)
    c_sgT = salloc("c_sgT", [128, 2, RW], BF16)
    c_sgr = salloc("c_sgr", [128, RC, MK], BF16)
    c_dpre = salloc("c_dpre", [128, RC, MK], BF16)
    c_pT = salloc("c_pT", [128, DC, RW], BF16)
    c_pr = salloc("c_pr", [128, RC, D], BF16)
    c_par = salloc("c_par", [128, DC, RW], BF16)
    c_p2 = salloc("c_p2", [128, DC, RW], BF16)
    c_st = salloc("c_st", [128, 2, 8, 3], F32)
    c_mv = salloc("c_mv", [128, RC, 2], F32)
    c_rs = salloc("c_rs", [128, RC, 1], F32)
    c_sc = salloc("c_sc", [2, RW], F32)
    c_u4 = salloc("c_u4", [1, 8], F32)
    c_fl = salloc("c_fl", [1, 2], F32)
    TTT_END = off[0]
    assert TTT_END <= SB_LIMIT

    off[0] = PERSIST_END
    f_acc = salloc("f_acc", [128, TT // 128, D], F32)
    f_ynT = salloc("f_ynT", [128, DC, TT], BF16)
    f_int = salloc("f_int", [128, DC, TT], BF16)
    f_wp = [salloc(f"f_wp{i}", [128, DC, NGSZ], BF16) for i in range(3)]
    f_wd = [salloc(f"f_wd{i}", [128, DC, 512], BF16) for i in range(2)]
    f_ds = [salloc(f"f_ds{i}", [128, D], BF16) for i in range(2)]
    f_gel = salloc("f_gel", [128, RW], BF16)
    f_ay = salloc("f_ay", [128, D], BF16)
    f_st = salloc("f_st", [128, 8, 3], F32)
    f_mv = salloc("f_mv", [128, 2], F32)
    f_rs = salloc("f_rs", [128, 1], F32)
    FFN_END = off[0]
    assert FFN_END <= SB_LIMIT, FFN_END
    off[0] = PERSIST_END
    g_slab = [salloc(f"g_slab{i}", [128, NF], BF16) for i in range(2)]
    assert off[0] <= SB_LIMIT
    NW = 8
    q_pk, q_sc, q_cd, q_s, q_m, q_a, q_b, q_out = [], [], [], [], [], [], [], []
    for i in range(NW):
        q_pk.append(salloc(f"q_pk{i}", [128, 512], I32))
        q_sc.append(salloc(f"q_sc{i}", [128, 32], F32))
        q_cd.append(salloc(f"q_cd{i}", [128, 1024], BF16))
        q_s.append(salloc(f"q_s{i}", [128, 1024], BF16))
        q_m.append(salloc(f"q_m{i}", [128, 1024], BF16))
        q_a.append(salloc(f"q_a{i}", [128, 1024], BF16))
        q_b.append(salloc(f"q_b{i}", [128, 1024], BF16))
        q_out.append(salloc(f"q_out{i}", [128, 1024], BF16))
    assert off[0] <= SB_LIMIT, off[0]

    from contextlib import ExitStack
    with ExitStack() as ctx:
        S = Sched(nc)
        for name in ["sPE", "sACT", "sDVE", "sPOOL", "sDMA", "sGD",
                     "sCOLL", "sRS", "sST", "sLD", "sRB", "sXB0", "sXB1",
                     "sQ0", "sQ1", "sQ2", "sQ3", "sQ4", "sQ5", "sQ6", "sQ7"]:
            S.sem(ctx, name)
        ps_a = ctx.enter_context(nc.psum_tensor("ps_a", [128, 2, RW], F32))
        ps_b = ctx.enter_context(nc.psum_tensor("ps_b", [128, 2, RW], F32))
        ps_d0 = ctx.enter_context(nc.psum_tensor("ps_d0", [128, 4, 128], BF16))
        ps_d1 = ctx.enter_context(nc.psum_tensor("ps_d1", [128, 4, 128], BF16))
        ps_s = ctx.enter_context(nc.psum_tensor("ps_s", [128, RW], F32))
        ps_f = ctx.enter_context(nc.psum_tensor("ps_f", [128, RW], F32))
        psd = [ps_d0, ps_d1]
        blk = ctx.enter_context(nc.Block())
        op = S.op

        op("gpsimd", lambda g: g.memset(ident[:], 0.0), inc="sPOOL")
        op("gpsimd", lambda g: g.affine_select(
            out=ident[:], in_=ident[:], compare_op=ALU.not_equal, fill=1.0,
            base=0, pattern=[[-1, 128]], channel_multiplier=1),
           waits=[("sPOOL", S.v("sPOOL"))], inc="sPOOL")
        op("gpsimd", lambda g: g.memset(kones[:], 1.0), inc="sPOOL")
        op("gpsimd", lambda g: g.memset(epsT[:], 1e-5), inc="sPOOL")
        op("gpsimd", lambda g: g.memset(b75[:], 7.5), inc="sPOOL")
        op("gpsimd", lambda g: g.memset(onec[:], 1.0), inc="sPOOL")
        op("gpsimd", lambda g: g.memset(bm2[:], -2.0), inc="sPOOL")
        op("gpsimd", lambda g: g.memset(bm6[:], -6.0), inc="sPOOL")
        op("gpsimd", lambda g: g.memset(bm16[:], -16.0), inc="sPOOL")
        IDENT_POOL = S.v("sPOOL")

        scrf = c_xraw[:].rearrange("p a d -> p (a d)")

        def stage(view, src, dst, prev_dve):
            d_ = op("sync", lambda s, view=view, src=src: s.dma_start(out=view, in_=src),
                    waits=[("sDVE", prev_dve)], inc="sDMA", n=16)
            return op("vector", lambda v, dst=dst, view=view: v.tensor_copy(dst, view),
                      waits=[("sDMA", d_)], inc="sDVE")

        cv = 0
        v16 = scrf[:, 0:DC * MK].rearrange("p (a m) -> p a m", m=MK)
        cv = stage(v16, w0kTd.rearrange("(a p) m -> p a m", p=128), c_w0[:], cv)
        cv = stage(v16, w1dmd.rearrange("(a p) m -> p a m", p=128), c_w1[:, :, 0:MK], cv)
        vb1 = scrf[:, 0:DC].rearrange("p (a o) -> p a o", o=1)
        cv = stage(vb1, b1d.rearrange("p (a o) -> p a o", o=1), c_w1[:, :, MK:MK + 1], cv)
        cv = op("vector", lambda v: v.tensor_scalar_mul(
            c_b18[:], c_w1[:, :, MK:MK + 1], 0.125), waits=[("sDVE", cv)], inc="sDVE")
        v2d = scrf[:, 0:2 * D].rearrange("p (a d) -> p a d", d=D)
        cv = stage(v2d, w1kTd.rearrange("(a p) d -> p a d", p=128), c_w1T[:], cv)
        vb0 = scrf[:, 0:2].rearrange("p (a o) -> p a o", o=1)
        cv = stage(vb0, b0kd.rearrange("p (a o) -> p a o", o=1), c_b0[:], cv)
        vlf = scrf[:, 0:2 * DC].rearrange("p (a t) -> p a t", t=2)
        cv = stage(vlf, lrfgd, c_lrfg[:], cv)
        dsm = op("sync", lambda s: s.dma_start(out=c_lrb[:], in_=lrbd.rearrange("(a o) -> a o", a=1)),
                 waits=[("sDVE", cv)], inc="sDMA", n=16)
        dsm = op("sync", lambda s: s.dma_start(out=c_fgb[:], in_=fgbd.rearrange("(a o) -> a o", a=1)),
                 waits=[("sDMA", dsm)], inc="sDMA", n=16)
        P0_DVE = cv
        P0_DMA = S.v("sDMA")

        dq_jobs = []
        if do_ffn and dq_on:
            for (pk, sc, nat, rows, K) in [(gwp, gsc, wg_nm, NF, D),
                                           (uwp, usc, wu_nm, NF, D),
                                           (dwp, dsc, wd_nm, D, NF)]:
                for r0 in range(0, rows, 128):
                    for k0 in range(0, K, 1024):
                        dq_jobs.append((pk, sc, nat, r0, k0))

        def dq_stages(j, barrier_w, last, wr_last):
            (pk, sc, nat, r0, k0) = dq_jobs[j]
            st = j % NW
            qsem = f"sQ{st}"
            qi = q_m[st][:].bitcast(I32)
            o = {}

            def s0():
                w0 = list(barrier_w) if last is None else [("sDVE", last[0]), (qsem, last[1])]
                o["dl"] = op("scalar", lambda s: s.dma_start(
                    out=q_pk[st][:], in_=pk[r0:r0 + 128, k0 // 2:k0 // 2 + 512]),
                    waits=w0, inc=qsem, n=16)
                o["ds"] = op("scalar", lambda s: s.dma_start(
                    out=q_sc[st][:], in_=sc[r0:r0 + 128, k0 // 32:k0 // 32 + 32]),
                    inc=qsem, n=16)

            def s1():
                o["lo"] = op("vector", lambda v: v.tensor_single_scalar(
                    qi, q_pk[st][:], 15, ALU.bitwise_and),
                    waits=[(qsem, o["ds"])], inc="sDVE")

            def s2():
                o["cl"] = op("gpsimd", lambda g: g.tensor_copy(
                    q_cd[st][:].rearrange("p (a t) -> p a t", t=2)[:, :, 0:1],
                    qi.rearrange("p (a o) -> p a o", o=1)),
                    waits=[("sDVE", o["lo"])], inc="sPOOL")

            def s3():
                o["hi"] = op("vector", lambda v: v.tensor_single_scalar(
                    qi, q_pk[st][:], 4, ALU.logical_shift_right),
                    waits=[("sPOOL", o["cl"])], inc="sDVE")

            def s4():
                o["ch"] = op("gpsimd", lambda g: g.tensor_copy(
                    q_cd[st][:].rearrange("p (a t) -> p a t", t=2)[:, :, 1:2],
                    qi.rearrange("p (a o) -> p a o", o=1)),
                    waits=[("sDVE", o["hi"])], inc="sPOOL")

            def s5():
                o["sg"] = op("scalar", lambda s: s.activation(
                    q_s[st][:], q_cd[st][:], AF.Sign, bias=b75[:], scale=-1.0),
                    waits=[("sPOOL", o["ch"])], inc="sACT")

            def s6():
                o["mm"] = op("vector", lambda v: v.scalar_tensor_tensor(
                    q_m[st][:], q_s[st][:], 4.0, q_cd[st][:], ALU.mult, ALU.add),
                    waits=[("sACT", o["sg"])], inc="sDVE")

            def s7():
                o["aa"] = op("vector", lambda v: v.tensor_scalar(
                    q_a[st][:], q_m[st][:], 0.5, -2.0, ALU.mult, ALU.add),
                    waits=[("sDVE", o["mm"])], inc="sDVE")
                o["bb"] = op("scalar", lambda s: s.activation(
                    q_b[st][:], q_m[st][:], AF.Identity, bias=bm6[:], scale=1.0),
                    waits=[("sDVE", o["mm"])], inc="sACT")

            def s8():
                o["m1"] = op("vector", lambda v: v.tensor_max(
                    q_a[st][:], q_a[st][:], q_b[st][:]),
                    waits=[("sACT", o["bb"])], inc="sDVE")

            def s9():
                o["cc"] = op("scalar", lambda s: s.activation(
                    q_b[st][:], q_m[st][:], AF.Identity, bias=bm16[:], scale=2.0),
                    waits=[("sDVE", o["m1"])], inc="sACT")

            def s10():
                o["m2"] = op("vector", lambda v: v.tensor_max(
                    q_a[st][:], q_a[st][:], q_b[st][:]),
                    waits=[("sACT", o["cc"])], inc="sDVE")

            def s11():
                o["vv"] = op("vector", lambda v: v.tensor_mul(
                    q_a[st][:], q_a[st][:], q_s[st][:]),
                    waits=[("sDVE", o["m2"])], inc="sDVE")

            def s12():
                o["so"] = op("vector", lambda v: v.tensor_tensor(
                    q_out[st][:].rearrange("p (a t) -> p a t", t=32),
                    q_a[st][:].rearrange("p (a t) -> p a t", t=32),
                    q_sc[st][:].rearrange("p (a o) -> p a o", o=1).broadcast_to([128, 32, 32]),
                    ALU.mult),
                    waits=[("sDVE", o["vv"]), (qsem, wr_last)], inc="sDVE")

            def s13():
                o["wr"] = op("gpsimd", lambda g: g.dma_start(
                    out=nat[r0:r0 + 128, k0:k0 + 1024], in_=q_out[st][:]),
                    waits=[("sDVE", o["so"])], inc=qsem, n=16)

            return [s0, s1, s2, s3, s4, s5, s6, s7, s8, s9, s10, s11, s12, s13], o

        def emit_dq_wave_phase(barrier_w):
            NS = 14
            last = [None] * NW
            wr_last = [0] * NW
            lane_stage = [None] * NW
            lane_pos = [0] * NW
            nxt = [l for l in range(NW)]

            def start_lane(l):
                if nxt[l] >= len(dq_jobs):
                    return False
                j = nxt[l]
                nxt[l] += NW
                stages, o = dq_stages(j, barrier_w, last[l], wr_last[l])
                lane_stage[l] = (stages, o, l)
                lane_pos[l] = 0
                return True

            for l in range(NW):
                if not start_lane(l):
                    lane_stage[l] = None
            active = sum(1 for x in lane_stage if x)
            while active:
                for l in range(NW):
                    if lane_stage[l] is None:
                        continue
                    stages, o, st_ = lane_stage[l]
                    stages[lane_pos[l]]()
                    lane_pos[l] += 1
                    if lane_pos[l] == NS:
                        last[st_] = (o["so"], o["wr"])
                        wr_last[st_] = o["wr"]
                        lane_stage[l] = None
                        if start_lane(l):
                            pass
                active = sum(1 for x in lane_stage if x)

        psd_evac = [0, 0]

        def emit_prep(c, extra_load_waits=()):
            par = c % 2
            xv = xfull.rearrange("b (n q) d -> n q b d", q=128)[c]
            nrm = 0
            dl = 0
            for hf in range(2):
                w = list(extra_load_waits) if hf == 0 else [("sDVE", nrm)]
                if dl:
                    w.append(("sLD", dl))
                dl = op("sync", lambda s, hf=hf, xv=xv: s.dma_start(
                    out=c_xraw[:], in_=xv[:, 2 * hf:2 * hf + 2, :]),
                    waits=w, inc="sLD", n=16)
                last = 0
                for rc2 in range(2):
                    for sgm in range(4):
                        last = op("vector", lambda v, rc2=rc2, sgm=sgm: v.bn_stats(
                            c_st[:, rc2, 2 * sgm:2 * sgm + 2, :].rearrange("p a b -> p (a b)"),
                            c_xraw[:, rc2, 512 * sgm:512 * (sgm + 1)]),
                            waits=[("sLD", dl)] if (rc2 == 0 and sgm == 0) else (),
                            inc="sDVE")
                for rc2 in range(2):
                    last = op("vector", lambda v, rc2=rc2, hf=hf: v.bn_aggr(
                        c_mv[:, 2 * hf + rc2, :], c_st[:, rc2, :, :].rearrange("p a b -> p (a b)")),
                        waits=[("sDVE", last)] if rc2 == 0 else (), inc="sDVE")
                sq = op("scalar", lambda s, hf=hf: s.activation(
                    c_rs[:, 2 * hf:2 * hf + 2, :].rearrange("p a o -> p (a o)"),
                    c_mv[:, 2 * hf:2 * hf + 2, 1:2].rearrange("p a o -> p (a o)"),
                    AF.Sqrt, bias=epsT[:], scale=1.0),
                    waits=[("sDVE", last), ("sPOOL", IDENT_POOL)], inc="sACT")
                rc_ = op("vector", lambda v, hf=hf: v.reciprocal(
                    c_rs[:, 2 * hf:2 * hf + 2, :].rearrange("p a o -> p (a o)"),
                    c_rs[:, 2 * hf:2 * hf + 2, :].rearrange("p a o -> p (a o)")),
                    waits=[("sACT", sq)], inc="sDVE")
                for rc2 in range(2):
                    nrm = op("vector", lambda v, rc2=rc2, hf=hf, par=par: v.tensor_scalar(
                        c_xn[par][:, 2 * hf + rc2, :], c_xraw[:, rc2, :],
                        c_mv[:, 2 * hf + rc2, 0:1], c_rs[:, 2 * hf + rc2, :],
                        ALU.subtract, ALU.mult),
                        waits=[("sDVE", rc_)] if rc2 == 0 else (), inc="sDVE")
            for dc in range(DC):
                sl = dc % 2
                for rcc in range(RC):
                    w = []
                    if dc == 0 and rcc == 0:
                        w = [("sDVE", nrm), ("sPOOL", IDENT_POOL)]
                    if rcc == 0 and psd_evac[sl]:
                        w.append(("sDVE", psd_evac[sl]))
                    op("tensor", lambda t, dc=dc, rcc=rcc, sl=sl, par=par: t.transpose(
                        psd[sl][:, rcc, :], c_xn[par][:, rcc, 128 * dc:128 * (dc + 1)],
                        ident[:]), waits=w, inc="sPE")
                pe_b = S.v("sPE")
                psd_evac[sl] = op("vector", lambda v, dc=dc, sl=sl, par=par: v.tensor_copy(
                    c_xnT[par][:, dc, :], psd[sl][:].rearrange("p a b -> p (a b)")),
                    waits=[("sPE", pe_b)], inc="sDVE")
            return nrm

        nrm0 = emit_prep(0, extra_load_waits=[("sDVE", P0_DVE)])
        prev_nrm = {0: nrm0}

        psb_free = {0: None, 1: None}
        psa_free = 0
        pending_rs = None
        deferred = None
        st1_prev = 0
        rb_prev = 0

        def emit_deferred(dctx):
            c1 = dctx["c"]
            xnT1 = dctx["xnT"]
            first = True
            for kc in range(DC):
                for ic in range(2):
                    w = []
                    if first:
                        w = [("sDVE", dctx["upd"]), ("sACT", psa_free)]
                        first = False
                    op("tensor", lambda t, kc=kc, ic=ic, xnT1=xnT1: t.matmul(
                        ps_a[:, ic, :], c_w0[:, kc, 128 * ic:128 * (ic + 1)],
                        xnT1[:, kc, :], start=(kc == 0), stop=(kc == DC - 1)),
                        waits=w, inc="sPE")
            pe_fwd2 = S.v("sPE")
            a2 = 0
            for mc in range(2):
                a2 = op("scalar", lambda s, mc=mc: s.activation(
                    c_h2T[:, mc, :], ps_a[:, mc, :], AF.Silu,
                    bias=c_b0[:, mc, :], scale=1.0),
                    waits=[("sPE", pe_fwd2)] if mc == 0 else (), inc="sACT")
            ev_p2 = 0
            ev_hist = {}
            for dc in range(DC):
                for kc in range(2):
                    w = []
                    if dc == 0 and kc == 0:
                        w = [("sACT", a2)]
                    if kc == 0 and dc >= 2:
                        w.append(("sACT", ev_hist[dc - 2]))
                    if kc == 0 and dc < 2 and psb_free[dc % 2]:
                        w.append(psb_free[dc % 2])
                    op("tensor", lambda t, dc=dc, kc=kc: t.matmul(
                        ps_b[:, dc % 2, :], c_w1T[:, kc, 128 * dc:128 * (dc + 1)],
                        c_h2T[:, kc, :], start=(kc == 0), stop=(kc == 1)),
                        waits=w, inc="sPE")
                pe_p2 = S.v("sPE")
                wex = [("sPE", pe_p2)]
                if dc == 0:
                    wex += [("sGD", S.v("sGD"))]
                ev_p2 = op("scalar", lambda s, dc=dc: s.activation(
                    c_p2[:, dc, :], ps_b[:, dc % 2, :], AF.Identity,
                    bias=c_b18[:, dc, :], scale=1.0), waits=wex, inc="sACT")
                ev_hist[dc] = ev_p2
                psb_free[dc % 2] = ("sACT", ev_p2)
            half = 32 if n_steps == NCH else max(1, n_steps // 2)
            h = (c1 // half) if full_run else 0
            p = (c1 % half) if full_run else c1 % npair
            gd = 0
            for b in range(4):
                gd = op("gpsimd", lambda g, b=b, p=p, h=h: g.dma_start(
                    out=pairbuf[p, 2 * b + h].rearrange("(a q) r -> q a r", q=128),
                    in_=c_p2[:, :, 128 * b:128 * (b + 1)]),
                    waits=[("sACT", ev_p2)] if b == 0 else (), inc="sGD", n=16)
            if dbg.get("p2st"):
                op("gpsimd", lambda g, c1=c1: g.dma_start(
                    out=dbg_outs["d_p2st"][c1].rearrange("(a q) r -> q a r", q=128),
                    in_=c_p2[:]), waits=[("sACT", ev_p2)], inc="sGD", n=16)
            rs = None
            if full_run and h == 1:
                rs = (lambda g, p=p: g.collective_compute(
                    "ReduceScatter", ALU.add, RG8,
                    ins=[pairbuf[p].rearrange("s d r -> (s d) r").opt()],
                    outs=[myattn[p].opt()]),
                    [("sGD", S.v("sGD"))])
            return a2, ev_p2, rs

        for c in range(n_steps):
            par = c % 2
            xn = c_xn[par]
            xnT = c_xnT[par]

            w0_ = [("sDVE", S.v("sDVE"))]
            if psa_free:
                w0_.append(("sACT", psa_free))
            if c == 0:
                w0_.append(("sDMA", P0_DMA))
            first = True
            for kc in range(DC):
                for ic in range(2):
                    op("tensor", lambda t, kc=kc, ic=ic, xnT=xnT: t.matmul(
                        ps_a[:, ic, :], c_w0[:, kc, 128 * ic:128 * (ic + 1)],
                        xnT[:, kc, :], start=(kc == 0), stop=(kc == DC - 1)),
                        waits=w0_ if first else (), inc="sPE")
                    first = False
                op("tensor", lambda t, kc=kc, xnT=xnT: t.matmul(
                    ps_s[0:1, :], c_lrfg[:, kc, 0:1], xnT[:, kc, :],
                    start=(kc == 0), stop=(kc == DC - 1)), inc="sPE")
                op("tensor", lambda t, kc=kc, xnT=xnT: t.matmul(
                    ps_s[32:33, :], c_lrfg[:, kc, 1:2], xnT[:, kc, :],
                    start=(kc == 0), stop=(kc == DC - 1)), inc="sPE")
            pe_fwd1 = S.v("sPE")

            act_h1 = 0
            for mc in range(2):
                op("scalar", lambda s, mc=mc: s.activation(
                    c_h1T[:, mc, :], ps_a[:, mc, :], AF.Silu,
                    bias=c_b0[:, mc, :], scale=1.0),
                    waits=[("sPE", pe_fwd1)] if mc == 0 else (), inc="sACT")
                act_h1 = op("scalar", lambda s, mc=mc: s.activation(
                    c_sgT[:, mc, :], ps_a[:, mc, :], AF.Derivative_silu,
                    bias=c_b0[:, mc, :], scale=1.0), inc="sACT")
            psa_free = act_h1
            sv_ = op("scalar", lambda s: s.activation(
                c_sc[0:1, :], ps_s[0:1, :], AF.Sigmoid, bias=c_lrb[:], scale=1.0),
                waits=[("sPE", pe_fwd1)], inc="sACT")
            e1 = op("vector", lambda v: v.tensor_reduce(
                c_u4[0:1, 6:7].rearrange("p (a o) -> p a o", o=1),
                c_sc[0:1, :].rearrange("p (a r) -> p a r", a=1), AX.X, ALU.add),
                waits=[("sACT", sv_)], inc="sDVE")
            nl_ = op("vector", lambda v: v.tensor_scalar_mul(
                c_u4[0:1, 4:5], c_u4[0:1, 6:7], -0.01 / RW),
                waits=[("sDVE", e1)], inc="sDVE")
            u4_ = op("vector", lambda v: v.tensor_reduce(
                c_u4[0:1, 0:4].rearrange("p (a o) -> p a o", o=1),
                ps_s[32:33, :].rearrange("p (a r) -> p a r", a=4), AX.X, ALU.add),
                waits=[("sDVE", e1)], inc="sDVE")
            sg4 = op("scalar", lambda s: s.activation(
                c_sc[0:1, 0:4], c_u4[0:1, 0:4], AF.Sigmoid, bias=c_fgb[:],
                scale=1.0 / 128), waits=[("sDVE", u4_)], inc="sACT")
            f1 = op("vector", lambda v: v.tensor_reduce(
                c_u4[0:1, 5:6].rearrange("p (a o) -> p a o", o=1),
                c_sc[0:1, 0:4].rearrange("p (a r) -> p a r", a=1), AX.X, ALU.add),
                waits=[("sACT", sg4)], inc="sDVE")
            fl_ = op("vector", lambda v: v.tensor_scalar_mul(
                c_fl[0:1, 0:1], c_u4[0:1, 5:6], 0.25), waits=[("sDVE", f1)], inc="sDVE")
            fl2 = op("vector", lambda v: v.tensor_copy(c_fl[0:1, 1:2], c_u4[0:1, 4:5]),
                     waits=[("sDVE", nl_)], inc="sDVE")

            ev_par = 0
            ev_hist = {}
            for dc in range(DC):
                for kc in range(2):
                    w = []
                    if dc == 0 and kc == 0:
                        w = [("sACT", act_h1)]
                    if kc == 0 and dc >= 2:
                        w = [("sACT", ev_hist[dc - 2])]
                    if kc == 0 and dc < 2 and psb_free[dc % 2]:
                        w.append(psb_free[dc % 2])
                    op("tensor", lambda t, dc=dc, kc=kc: t.matmul(
                        ps_b[:, dc % 2, :], c_w1T[:, kc, 128 * dc:128 * (dc + 1)],
                        c_h1T[:, kc, :], start=(kc == 0), stop=(kc == 1)),
                        waits=w, inc="sPE")
                pe_pk = S.v("sPE")
                wex = [("sPE", pe_pk)]
                if dc == 0 and st1_prev:
                    wex.append(("sGD", st1_prev))
                ev_par = op("scalar", lambda s, dc=dc: s.activation(
                    c_par[:, dc, :], ps_b[:, dc % 2, :], AF.Identity,
                    bias=c_b18[:, dc, :], scale=1.0), waits=wex, inc="sACT")
                ev_hist[dc] = ev_par
                psb_free[dc % 2] = ("sACT", ev_par)

            bc = op("tensor", lambda t: t.matmul(
                ps_s[:, 64:66], kones[:], c_fl[0:1, :], start=True, stop=True),
                waits=[("sDVE", max(fl_, fl2))], inc="sPE")
            fv_ = op("vector", lambda v: v.tensor_copy(c_fv[:], ps_s[:, 64:65]),
                     waits=[("sPE", bc)], inc="sDVE")
            nv_ = op("vector", lambda v: v.tensor_copy(c_nl[:], ps_s[:, 65:66]),
                     inc="sDVE")
            sc_done = nv_

            st1 = op("gpsimd", lambda g: g.dma_start(
                out=car_in.rearrange("(a p) r -> p a r", p=128), in_=c_par[:]),
                waits=[("sACT", ev_par), ("sRB", rb_prev)], inc="sGD", n=16)
            ar = op("gpsimd", lambda g: g.collective_compute(
                "AllReduce", ALU.add, RG8, ins=[car_in.opt()], outs=[car_out.opt()]),
                waits=[("sGD", st1)], inc="sCOLL")
            st1_prev = st1
            if pending_rs is not None:
                op("gpsimd", pending_rs[0], waits=pending_rs[1], inc="sRS")
                pending_rs = None

            last_evp2 = 0
            if deferred is not None:
                a2v, last_evp2, pending_rs = emit_deferred(deferred)
                psa_free = a2v
                deferred = None

            for mc in range(2):
                sl = mc % 2
                w0t = []
                if mc == 0:
                    w0t = [("sACT", act_h1)]
                if psd_evac[sl]:
                    w0t.append(("sDVE", psd_evac[sl]))
                for rcc in range(RC):
                    op("tensor", lambda t, mc=mc, rcc=rcc, sl=sl: t.transpose(
                        psd[sl][:, rcc, :], c_sgT[:, mc, 128 * rcc:128 * (rcc + 1)],
                        ident[:]), waits=w0t if rcc == 0 else (), inc="sPE")
                pe_b = S.v("sPE")
                psd_evac[sl] = op("vector", lambda v, mc=mc, sl=sl: v.tensor_copy(
                    c_sgr[:, :, 128 * mc:128 * (mc + 1)], psd[sl][:]),
                    waits=[("sPE", pe_b)], inc="sDVE")
            sgr_done = S.v("sDVE")
            h1r_done = 0
            for mc in range(2):
                sl = mc % 2
                for rcc in range(RC):
                    w = [("sDVE", psd_evac[sl])] if rcc == 0 else ()
                    op("tensor", lambda t, mc=mc, rcc=rcc, sl=sl: t.transpose(
                        psd[sl][:, rcc, :], c_h1T[:, mc, 128 * rcc:128 * (rcc + 1)],
                        ident[:]), waits=w, inc="sPE")
                pe_b = S.v("sPE")
                psd_evac[sl] = op("vector", lambda v, mc=mc, sl=sl: v.tensor_scalar_mul(
                    c_h1r[:, :, 128 * mc:128 * (mc + 1)], psd[sl][:], c_nl[:]),
                    waits=[("sPE", pe_b), ("sDVE", sc_done)], inc="sDVE")
            for rcc in range(RC):
                h1r_done = op("vector", lambda v, rcc=rcc: v.tensor_copy(
                    c_h1r[:, rcc, MK:MK + 1], c_nl[:]),
                    waits=[("sDVE", sc_done)] if rcc == 0 else (), inc="sDVE")

            if c + 1 < n_steps:
                emit_prep(c + 1, extra_load_waits=[("sDVE", prev_nrm[c])])
                prev_nrm[c + 1] = S.v("sDVE")

            rb1 = op("sync", lambda s: s.dma_start(
                out=c_pT[:], in_=car_out.rearrange("(a p) r -> p a r", p=128)),
                waits=[("sCOLL", ar)], inc="sRB", n=16)
            if dbg.get("pred"):
                op("gpsimd", lambda g, c=c: g.dma_start(
                    out=dbg_outs["d_pred"][c].rearrange("(a p) r -> p a r", p=128),
                    in_=car_out.rearrange("(a p) r -> p a r", p=128)),
                    waits=[("sCOLL", ar)], inc="sGD", n=16)
            rbx = 0
            for rcc in range(RC):
                qn = "sync" if rcc % 2 == 0 else "scalar"
                wq = [("sCOLL", ar)] if rcc < 2 else ()
                rbx = op(qn, lambda s, rcc=rcc: s.dma_start_transpose(
                    c_pr[:, rcc, :], car_out[:, 128 * rcc:128 * (rcc + 1)]),
                    waits=wq, inc="sRB", n=16)
            rb_prev = rbx


            dT1 = op("vector", lambda v, xnT=xnT: v.tensor_sub(
                c_pT[:].rearrange("p a r -> p (a r)"),
                c_pT[:].rearrange("p a r -> p (a r)"),
                xnT[:].rearrange("p a r -> p (a r)")),
                waits=[("sRB", rbx)], inc="sDVE")
            dT2 = op("vector", lambda v: v.tensor_scalar_mul(
                c_pT[:].rearrange("p a r -> p (a r)"),
                c_pT[:].rearrange("p a r -> p (a r)"), TWO_N),
                waits=[("sDVE", dT1)], inc="sDVE")
            dR1 = op("vector", lambda v, xn=xn: v.tensor_sub(
                c_pr[:].rearrange("p a r -> p (a r)"),
                c_pr[:].rearrange("p a r -> p (a r)"),
                xn[:].rearrange("p a r -> p (a r)")),
                waits=[("sRB", rbx)], inc="sDVE")
            dR2 = op("vector", lambda v: v.tensor_scalar_mul(
                c_pr[:].rearrange("p a r -> p (a r)"),
                c_pr[:].rearrange("p a r -> p (a r)"), TWO_N),
                waits=[("sDVE", dR1)], inc="sDVE")

            dhv = ps_b[:].rearrange("p a (c m) -> p (a c) m", m=MK)
            first = True
            for ic in range(RC):
                for kc in range(DC):
                    w = []
                    if first:
                        w = [("sDVE", dT2)]
                        if psb_free[0]:
                            w.append(psb_free[0])
                        if psb_free[1]:
                            w.append(psb_free[1])
                        first = False
                    op("tensor", lambda t, ic=ic, kc=kc, dhv=dhv: t.matmul(
                        dhv[:, ic, :], c_pT[:, kc, 128 * ic:128 * (ic + 1)],
                        c_w1[:, kc, 0:MK], start=(kc == 0), stop=(kc == DC - 1)),
                        waits=w, inc="sPE")
            pe_dh = S.v("sPE")
            dpre_ = op("vector", lambda v, dhv=dhv: v.scalar_tensor_tensor(
                c_dpre[:].rearrange("p a m -> p (a m)"),
                dhv[:].rearrange("p a m -> p (a m)"), c_nl[:],
                c_sgr[:].rearrange("p a m -> p (a m)"), ALU.mult, ALU.mult),
                waits=[("sPE", pe_dh), ("sDVE", sgr_done)], inc="sDVE")
            psb_free[0] = psb_free[1] = ("sDVE", dpre_)

            upd_w1 = 0
            uw_hist = {}
            for dc in range(DC):
                for kc in range(RC):
                    w = []
                    if dc == 0 and kc == 0:
                        w = [("sDVE", max(dR2, h1r_done, dpre_))]
                    if kc == 0 and dc >= 2:
                        w.append(("sDVE", uw_hist[dc - 2]))
                    if kc == 0 and dc < 2 and psb_free[dc % 2]:
                        w.append(psb_free[dc % 2])
                    op("tensor", lambda t, dc=dc, kc=kc: t.matmul(
                        ps_b[:, dc % 2, 0:MK + 1], c_pr[:, kc, 128 * dc:128 * (dc + 1)],
                        c_h1r[:, kc, :], start=(kc == 0), stop=(kc == RC - 1)),
                        waits=w, inc="sPE")
                pe_g = S.v("sPE")
                upd_w1 = op("vector", lambda v, dc=dc: v.scalar_tensor_tensor(
                    c_w1[:, dc, :], c_w1[:, dc, :], c_fv[:],
                    ps_b[:, dc % 2, 0:MK + 1], ALU.mult, ALU.add),
                    waits=[("sPE", pe_g), ("sDVE", fv_)] if dc == 0 else [("sPE", pe_g)],
                    inc="sDVE")
                uw_hist[dc] = upd_w1
                psb_free[dc % 2] = ("sDVE", upd_w1)
            b18_ = op("vector", lambda v: v.tensor_scalar_mul(
                c_b18[:], c_w1[:, :, MK:MK + 1], 0.125),
                waits=[("sDVE", upd_w1), ("sACT", last_evp2)], inc="sDVE")
            upd_w0 = 0
            uw_hist = {}
            for dc in range(DC):
                for kc in range(RC):
                    w = []
                    if dc == 0 and kc == 0:
                        w = [("sDVE", dpre_)]
                    if kc == 0 and dc >= 2:
                        w.append(("sDVE", uw_hist[dc - 2]))
                    if kc == 0 and dc < 2 and psb_free[dc % 2]:
                        w.append(psb_free[dc % 2])
                    op("tensor", lambda t, dc=dc, kc=kc, xn=xn: t.matmul(
                        ps_b[:, dc % 2, 0:MK], xn[:, kc, 128 * dc:128 * (dc + 1)],
                        c_dpre[:, kc, :], start=(kc == 0), stop=(kc == RC - 1)),
                        waits=w, inc="sPE")
                pe_g = S.v("sPE")
                upd_w0 = op("vector", lambda v, dc=dc: v.scalar_tensor_tensor(
                    c_w0[:, dc, :], c_w0[:, dc, :], c_fv[:],
                    ps_b[:, dc % 2, 0:MK], ALU.mult, ALU.add),
                    waits=[("sPE", pe_g)], inc="sDVE")
                uw_hist[dc] = upd_w0
                psb_free[dc % 2] = ("sDVE", upd_w0)
            for mc in range(2):
                for kc in range(RC):
                    op("tensor", lambda t, mc=mc, kc=kc: t.matmul(
                        ps_s[:, 128 + mc:129 + mc],
                        c_dpre[:, kc, 128 * mc:128 * (mc + 1)],
                        onec[:], start=(kc == 0), stop=(kc == RC - 1)),
                        waits=[("sDVE", dpre_)] if (mc == 0 and kc == 0) else (),
                        inc="sPE")
            pe_b0 = S.v("sPE")
            upd_b0 = op("vector", lambda v: v.scalar_tensor_tensor(
                c_b0[:].rearrange("p a o -> p (a o)"),
                c_b0[:].rearrange("p a o -> p (a o)"), c_fv[:],
                ps_s[:, 128:130], ALU.mult, ALU.add),
                waits=[("sPE", pe_b0)], inc="sDVE")

            flip_done = 0
            for mc in range(2):
                for d4 in range(4):
                    sl = (2 * mc + d4) % 2
                    w = []
                    if mc == 0 and d4 == 0:
                        w = [("sDVE", max(upd_w1, b18_))]
                    if psd_evac[sl]:
                        w.append(("sDVE", psd_evac[sl]))
                    for j in range(4):
                        dc = 4 * d4 + j
                        op("tensor", lambda t, dc=dc, mc=mc, sl=sl: t.transpose(
                            psd[sl][:, dc % 4, :], c_w1[:, dc, 128 * mc:128 * (mc + 1)],
                            ident[:]), waits=w if j == 0 else (), inc="sPE")
                    pe_b = S.v("sPE")
                    flip_done = op("vector", lambda v, mc=mc, d4=d4, sl=sl: v.tensor_copy(
                        c_w1T[:, mc, 512 * d4:512 * (d4 + 1)],
                        psd[sl][:].rearrange("p a b -> p (a b)")),
                        waits=[("sPE", pe_b)], inc="sDVE")
                    psd_evac[sl] = flip_done

            deferred = {"c": c, "xnT": xnT,
                        "upd": max(upd_w0, upd_b0, flip_done)}

        if deferred is not None:
            _, _, pending_rs2 = emit_deferred(deferred)
            deferred = None
            if pending_rs is not None:
                op("gpsimd", pending_rs[0], waits=pending_rs[1], inc="sRS")
            pending_rs = pending_rs2
        if pending_rs is not None:
            op("gpsimd", pending_rs[0], waits=pending_rs[1], inc="sRS")
            pending_rs = None

        if dbg.get("myattn"):
            op("gpsimd", lambda g: g.dma_start(
                out=dbg_outs["d_myattn"].rearrange("n d r -> (n d r)").rearrange("(a b) -> a b", a=1),
                in_=myattn.rearrange("n d r -> (n d r)").rearrange("(a b) -> a b", a=1)),
                waits=[("sRS", S.v("sRS"))], inc="sGD", n=16)

        barrier = [("sDVE", S.v("sDVE")), ("sACT", S.v("sACT")),
                   ("sPE", S.v("sPE")), ("sGD", S.v("sGD")),
                   ("sDMA", S.v("sDMA")), ("sPOOL", S.v("sPOOL")),
                   ("sLD", S.v("sLD")), ("sRB", S.v("sRB"))]

        if do_ffn:
            wd_slab_gd = {}

            def emit_gu_slabs():
                first = True
                i = 0
                xbsem = ["sXB0", "sXB1"]
                for (nat, tout) in [(wg_nm, wgT), (wu_nm, wuT)]:
                    for cc3 in range(DC):
                        sl = i % 2
                        w = list(barrier) if first else [("sGD", S.v("sGD"))]
                        first = False
                        i += 1
                        xr = op("sync", lambda s, nat=nat, cc3=cc3, sl=sl: s.dma_start_transpose(
                            g_slab[sl][:, 0:NF], nat[:, 128 * cc3:128 * (cc3 + 1)]),
                            waits=w, inc=xbsem[sl], n=16)
                        op("gpsimd", lambda g, tout=tout, cc3=cc3, sl=sl: g.dma_start(
                            out=tout[128 * cc3:128 * (cc3 + 1), :], in_=g_slab[sl][:, 0:NF]),
                            waits=[(xbsem[sl], xr)], inc="sGD", n=16)
                        if dbg.get("wrow") and cc3 == 1:
                            mi = 0 if tout is wgT else 1
                            op("gpsimd", lambda g, mi=mi, sl=sl: g.dma_start(
                                out=dbg_outs["d_wrow"][mi][:, 0:D],
                                in_=g_slab[sl][:, 0:D]),
                                waits=[(xbsem[sl], xr)], inc="sGD", n=16)

            def emit_d_slabs(grp):
                for j in range(DC):
                    cc3 = 16 * grp + j
                    sl = j % 2
                    w_ = [("sGD", S.v("sGD"))]
                    if j == 0:
                        w_ += [(f"sQ{i}", S.v(f"sQ{i}")) for i in range(8)]
                    xr = op("scalar", lambda s, cc3=cc3, sl=sl: s.dma_start_transpose(
                        f_ds[sl][:], wd_nm[:, 128 * cc3:128 * (cc3 + 1)]),
                        waits=w_, inc=["sXB0", "sXB1"][sl], n=16)
                    op("gpsimd", lambda g, cc3=cc3, sl=sl: g.dma_start(
                        out=wdT[128 * cc3:128 * (cc3 + 1), :], in_=f_ds[sl][:]),
                        waits=[(["sXB0", "sXB1"][sl], xr)], inc="sGD", n=16)
                    if dbg.get("wrow") and cc3 == 1:
                        op("gpsimd", lambda g, sl=sl: g.dma_start(
                            out=dbg_outs["d_wrow"][2][:, 0:D], in_=f_ds[sl][:]),
                            waits=[(["sXB0", "sXB1"][sl], xr)], inc="sGD", n=16)
                wd_slab_gd[grp] = S.v("sGD")

            emit_dq_wave_phase(barrier)
            barrier = barrier + [(f"sQ{i}", S.v(f"sQ{i}")) for i in range(8)]
            emit_gu_slabs()
            emit_d_slabs(0)
            emit_ffn(S, dict(
                xloc=xloc, myattn=myattn, ynbuf=ynbuf, out=out,
                wgT=wgT, wuT=wuT, wdT=wdT,
                f_acc=f_acc, f_ynT=f_ynT, f_int=f_int, f_wp=f_wp, f_wd=f_wd,
                f_gel=f_gel, f_ay=f_ay, f_st=f_st, f_mv=f_mv, f_rs=f_rs,
                epsT=epsT, ps_a=ps_a, ps_b=ps_b, ps_f=ps_f, ps_s=ps_s,
                n_rs=S.v("sRS"), full_run=full_run,
                emit_d_slabs=emit_d_slabs, wd_slab_gd=wd_slab_gd,
                dq_barrier=[("sGD", S.v("sGD")), ("sDMA", S.v("sDMA")),
                            ("sXB0", S.v("sXB0")), ("sXB1", S.v("sXB1"))]))
        op("gpsimd", lambda g: g.engine_nop(),
           waits=[("sST", S.v("sST")), ("sGD", S.v("sGD")),
                  ("sDMA", S.v("sDMA")), ("sDVE", S.v("sDVE")),
                  ("sACT", S.v("sACT")), ("sPE", S.v("sPE")),
                  ("sRS", S.v("sRS")), ("sCOLL", S.v("sCOLL")),
                  ("sLD", S.v("sLD")), ("sRB", S.v("sRB")),
                  ("sXB0", S.v("sXB0")), ("sXB1", S.v("sXB1"))]
                 + [(f"sQ{i}", S.v(f"sQ{i}")) for i in range(8)])

        @blk.sync
        def _(e):
            S.replay("sync", e)

        @blk.vector
        def _(e):
            S.replay("vector", e)

        @blk.scalar
        def _(e):
            S.replay("scalar", e)

        @blk.tensor
        def _(e):
            S.replay("tensor", e)

        @blk.gpsimd
        def _(e):
            S.replay("gpsimd", e)

    return nc


def emit_ffn(S, E):
    op = S.op
    f_acc, f_ynT, f_int = E["f_acc"], E["f_ynT"], E["f_int"]
    f_wp, f_wd = E["f_wp"], E["f_wd"]
    f_gel, f_ay = E["f_gel"], E["f_ay"]
    f_st, f_mv, f_rs = E["f_st"], E["f_mv"], E["f_rs"]
    ps_a, ps_b, ps_f, ps_s = E["ps_a"], E["ps_b"], E["ps_f"], E["ps_s"]
    xloc, myattn, ynbuf, out = E["xloc"], E["myattn"], E["ynbuf"], E["out"]
    wgT, wuT, wdT = E["wgT"], E["wuT"], E["wdT"]
    dqb = list(E["dq_barrier"])
    wgTv = wgT.rearrange("(a p) n -> p a n", p=128)
    wuTv = wuT.rearrange("(a p) n -> p a n", p=128)
    wdTv = wdT.rearrange("(a p) h -> p a h", p=128)
    emit_d_slabs = E["emit_d_slabs"]
    wd_slab_gd = E["wd_slab_gd"]

    wp_free = [0, 0, 0]
    wd_freepe = [0, 0]
    st_prev = 0
    gate_ev = {}
    up_ev = {}
    dn_ev = {}
    yn_gd = 0

    for t4 in range(NTILE):
        for cc in range(8):
            ch = 8 * t4 + cc
            wat = list(dqb) if (t4 == 0 and cc == 0) else []
            if E["full_run"]:
                wat.append(("sRS", min(ch + 1, E["n_rs"])))
            wat.append(("sDVE", S.v("sDVE")))
            wat.append(("sGD", yn_gd))
            xa = op("sync", lambda s, ch=ch: s.dma_start_transpose(
                f_ay[:], myattn[ch]), waits=wat, inc="sDMA", n=16)
            xl = op("sync", lambda s, ch=ch, cc=cc: s.dma_start(
                out=f_acc[:, cc, :], in_=xloc[128 * ch:128 * (ch + 1), :]),
                waits=[("sST", st_prev)] if (t4 > 0 and cc == 0) else [],
                inc="sDMA", n=16)
            ya = op("vector", lambda v, cc=cc: v.tensor_add(
                f_acc[:, cc, :], f_acc[:, cc, :], f_ay[:]),
                waits=[("sDMA", S.v("sDMA"))], inc="sDVE")
            last = 0
            for sgm in range(4):
                last = op("vector", lambda v, cc=cc, sgm=sgm: v.bn_stats(
                    f_st[:, 2 * sgm:2 * sgm + 2, :].rearrange("p a b -> p (a b)"),
                    f_acc[:, cc, 512 * sgm:512 * (sgm + 1)]),
                    waits=[("sDVE", ya)] if sgm == 0 else (), inc="sDVE")
            ag = op("vector", lambda v: v.bn_aggr(
                f_mv[:], f_st[:].rearrange("p a b -> p (a b)")),
                waits=[("sDVE", last)], inc="sDVE")
            sq = op("scalar", lambda s: s.activation(
                f_rs[:], f_mv[:, 1:2], AF.Sqrt, bias=E["epsT"][:], scale=1.0),
                waits=[("sDVE", ag)], inc="sACT")
            rc_ = op("vector", lambda v: v.reciprocal(f_rs[:], f_rs[:]),
                     waits=[("sACT", sq)], inc="sDVE")
            nm = op("vector", lambda v, cc=cc: v.tensor_scalar(
                f_ay[:], f_acc[:, cc, :], f_mv[:, 0:1], f_rs[:],
                ALU.subtract, ALU.mult), waits=[("sDVE", rc_)], inc="sDVE")
            yn_gd = op("gpsimd", lambda g, cc=cc: g.dma_start(
                out=ynbuf[128 * cc:128 * (cc + 1), :], in_=f_ay[:]),
                waits=[("sDVE", nm)], inc="sGD", n=16)
        ynT_d = 0
        for dc in range(DC):
            w = [("sGD", yn_gd)]
            if t4 > 0 and dc == 0:
                w.append(("sPE", S.v("sPE")))
            ynT_d = op("sync", lambda s, dc=dc: s.dma_start_transpose(
                f_ynT[:, dc, :], ynbuf[:, 128 * dc:128 * (dc + 1)]),
                waits=w if dc == 0 else [("sGD", yn_gd)], inc="sDMA", n=16)

        for g in range(NGRP):
            iv_last = 0
            for sb in range(NSUB):
                n0 = 2048 * g + NGSZ * sb
                bg = (2 * sb) % 3
                bu = (2 * sb + 1) % 3
                lg = op("sync", lambda s, n0=n0, bg=bg: s.dma_start(
                    out=f_wp[bg][:], in_=wgTv[:, :, n0:n0 + NGSZ]),
                    waits=[("sPE", wp_free[bg])], inc="sDMA", n=16)
                lu = op("sync", lambda s, n0=n0, bu=bu: s.dma_start(
                    out=f_wp[bu][:], in_=wuTv[:, :, n0:n0 + NGSZ]),
                    waits=[("sPE", wp_free[bu])], inc="sDMA", n=16)
                for ic in range(2):
                    for tj in range(TT // 512):
                        rot = tj % 2
                        for kc in range(DC):
                            w = []
                            if kc == 0:
                                w = [("sDMA", max(lu, ynT_d))]
                                if gate_ev.get(rot):
                                    w.append(("sACT", gate_ev[rot]))
                            op("tensor", lambda t, ic=ic, tj=tj, kc=kc, rot=rot, bg=bg: t.matmul(
                                ps_a[:, rot, :],
                                f_wp[bg][:, kc, 128 * ic:128 * (ic + 1)],
                                f_ynT[:, kc, 512 * tj:512 * (tj + 1)],
                                start=(kc == 0), stop=(kc == DC - 1)),
                                waits=w, inc="sPE")
                        peg = S.v("sPE")
                        gate_ev[rot] = op("scalar", lambda s, rot=rot: s.activation(
                            f_gel[:], ps_a[:, rot, :], AF.Gelu_apprx_tanh),
                            waits=[("sPE", peg)], inc="sACT")
                        for kc in range(DC):
                            w = []
                            if kc == 0:
                                w = [("sDMA", lu)]
                                if up_ev.get(rot):
                                    w.append(("sDVE", up_ev[rot]))
                            op("tensor", lambda t, ic=ic, tj=tj, kc=kc, rot=rot, bu=bu: t.matmul(
                                ps_b[:, rot, :],
                                f_wp[bu][:, kc, 128 * ic:128 * (ic + 1)],
                                f_ynT[:, kc, 512 * tj:512 * (tj + 1)],
                                start=(kc == 0), stop=(kc == DC - 1)),
                                waits=w, inc="sPE")
                        peu = S.v("sPE")
                        iv_last = op("vector", lambda v, sb=sb, ic=ic, tj=tj, rot=rot: v.tensor_mul(
                            f_int[:, 2 * sb + ic, 512 * tj:512 * (tj + 1)],
                            f_gel[:], ps_b[:, rot, :]),
                            waits=[("sACT", gate_ev[rot]), ("sPE", peu)], inc="sDVE")
                        up_ev[rot] = iv_last
                wp_free[bg] = S.v("sPE")
                wp_free[bu] = S.v("sPE")
            if t4 == 0 and g + 1 < NGRP:
                emit_d_slabs(g + 1)
            for hh in range(2):
                for hq in range(2):
                    qi_ = 2 * hh + hq
                    wb = qi_ % 2
                    wl = op("gpsimd", lambda gg, g=g, hh=hh, hq=hq, wb=wb: gg.dma_start(
                        out=f_wd[wb][:],
                        in_=wdTv[:, 16 * g:16 * g + 16,
                                 1024 * hh + 512 * hq:1024 * hh + 512 * (hq + 1)]),
                        waits=[("sPE", wd_freepe[wb]), ("sGD", wd_slab_gd[g])],
                        inc="sGD", n=16)
                    for tc in range(TT // 128):
                        rot = tc % 2
                        pst = [ps_f, ps_s][rot]
                        for kc in range(DC):
                            w = []
                            if kc == 0:
                                w = [("sGD", wl)]
                                if qi_ == 0 and tc == 0:
                                    w.append(("sDVE", iv_last))
                                if dn_ev.get(rot):
                                    w.append(("sDVE", dn_ev[rot]))
                            op("tensor", lambda t, tc=tc, kc=kc, pst=pst, wb=wb: t.matmul(
                                pst[:], f_int[:, kc, 128 * tc:128 * (tc + 1)],
                                f_wd[wb][:, kc, :],
                                start=(kc == 0), stop=(kc == DC - 1)),
                                waits=w, inc="sPE")
                        ped = S.v("sPE")
                        dn_ev[rot] = op("vector", lambda v, hh=hh, hq=hq, tc=tc, pst=pst: v.tensor_add(
                            f_acc[:, tc, 1024 * hh + 512 * hq:1024 * hh + 512 * (hq + 1)],
                            f_acc[:, tc, 1024 * hh + 512 * hq:1024 * hh + 512 * (hq + 1)],
                            pst[:]), waits=[("sPE", ped)], inc="sDVE")
                    wd_freepe[wb] = S.v("sPE")
        st_prev = op("gpsimd", lambda g, t4=t4: g.dma_start(
            out=out[TT * t4:TT * (t4 + 1), :].rearrange("(a p) d -> p a d", p=128),
            in_=f_acc[:]), waits=[("sDVE", S.v("sDVE"))], inc="sST", n=16)


def make_inmaps(inp):
    x = np.ascontiguousarray(np.asarray(inp["L0_attn_x"], np.float32))
    W0 = np.asarray(inp["L0_attn_W0"], np.float32)
    W1 = np.asarray(inp["L0_attn_W1"], np.float32)
    b0 = np.asarray(inp["L0_attn_b0"], np.float32)
    b1 = np.asarray(inp["L0_attn_b1"], np.float32)
    lr_w = np.asarray(inp["L0_attn_lr_w"], np.float32)
    fg_w = np.asarray(inp["L0_attn_fg_w"], np.float32)
    lrfg = np.ascontiguousarray(np.stack([lr_w[0], fg_w[0]], axis=1))
    xflat = x.reshape(4 * 8192, D)
    maps = []
    for k in range(8):
        mk = slice(MK * k, MK * (k + 1))
        maps.append({
            "xfull": x,
            "xloc": np.ascontiguousarray(xflat[TLOC * k:TLOC * (k + 1)]),
            "w0kT": np.ascontiguousarray(W0[mk].T),
            "w1dm": np.ascontiguousarray(W1[:, mk]),
            "w1kT": np.ascontiguousarray(W1[:, mk].T),
            "b0k": np.ascontiguousarray(b0[mk].reshape(2, 128).T),
            "b1f": np.ascontiguousarray(b1.reshape(DC, 128).T),
            "lrfg": np.ascontiguousarray(lrfg.reshape(DC, 128, 2).transpose(1, 0, 2)),
            "lrb": np.asarray(inp["L0_attn_lr_b"], np.float32),
            "fgb": np.asarray(inp["L0_attn_fg_b"], np.float32),
            "gwp": np.asarray(inp["L0_ffn_gate_W_packed"], np.int32),
            "gsc": np.asarray(inp["L0_ffn_gate_scales"], np.float32),
            "uwp": np.asarray(inp["L0_ffn_up_W_packed"], np.int32),
            "usc": np.asarray(inp["L0_ffn_up_scales"], np.float32),
            "dwp": np.asarray(inp["L0_ffn_down_W_packed"], np.int32),
            "dsc": np.asarray(inp["L0_ffn_down_scales"], np.float32),
        })
    return maps


def assemble(results):
    return np.concatenate([r["out"] for r in results], axis=0).reshape(4, 8192, D)


def kernel(**inputs):
    maps = make_inmaps(inputs)
    nc = build()
    res = run_bass_kernel_spmd(nc, maps, core_ids=list(range(8)))
    return assemble(res.results)


# version 03 (final)
# speedup vs baseline: 1.2282x; 1.0615x over previous
from concourse.bass_utils import run_bass_kernel_spmd

import numpy as np
import concourse.bass as bass
import concourse.mybir as mybir

F32 = mybir.dt.float32
BF16 = mybir.dt.bfloat16
I32 = mybir.dt.int32
AF = mybir.ActivationFunctionType
ALU = mybir.AluOpType
AX = mybir.AxisListType
RG8 = [[0, 1, 2, 3, 4, 5, 6, 7]]

D = 2048
MK = 256
RW = 512
DC = D // 128
RC = RW // 128
NCH = 64
NF = 16384
TLOC = 4096
TT = 1024
NTILE = TLOC // TT
NGRP = 8
NSUB = 8
NGSZ = 256
TWO_N = 2.0 / (RW * D)


class Sched:
    def __init__(self, nc):
        self.nc = nc
        self.items = []
        self.cnt = {}
        self.sems = {}

    def sem(self, ctx, name):
        h = ctx.enter_context(self.nc.semaphore(name))
        self.sems[name] = h
        self.cnt[name] = 0

    def op(self, eng, fn, waits=(), inc=None, n=1):
        v = None
        if inc is not None:
            self.cnt[inc] += n
            v = self.cnt[inc]
        self.items.append((eng, fn, tuple(waits), inc, n))
        return v

    def v(self, name):
        return self.cnt[name]

    def replay(self, ename, eng):
        for (e, fn, waits, inc, n) in self.items:
            if e != ename:
                continue
            for (s, val) in waits:
                if val and val > 0:
                    eng.wait_ge(self.sems[s], val)
            inst = fn(eng)
            if inc is not None:
                inst.then_inc(self.sems[inc], n)


def build(n_steps=NCH, do_ffn=True, dbg=None, force_rs=False, dq_on=True):
    dbg = dbg or {}
    nc = bass.Bass("TRN2", target_bir_lowering=False)
    full_run = (n_steps == NCH) or force_rs

    xfull = nc.dram_tensor("xfull", [4, 8192, D], F32, kind="ExternalInput").ap()
    xloc = nc.dram_tensor("xloc", [TLOC, D], F32, kind="ExternalInput").ap()
    w0kTd = nc.dram_tensor("w0kT", [D, MK], F32, kind="ExternalInput").ap()
    w1dmd = nc.dram_tensor("w1dm", [D, MK], F32, kind="ExternalInput").ap()
    w1kTd = nc.dram_tensor("w1kT", [MK, D], F32, kind="ExternalInput").ap()
    b0kd = nc.dram_tensor("b0k", [128, 2], F32, kind="ExternalInput").ap()
    b1d = nc.dram_tensor("b1f", [128, DC], F32, kind="ExternalInput").ap()
    lrfgd = nc.dram_tensor("lrfg", [128, DC, 2], F32, kind="ExternalInput").ap()
    lrbd = nc.dram_tensor("lrb", [1], F32, kind="ExternalInput").ap()
    fgbd = nc.dram_tensor("fgb", [1], F32, kind="ExternalInput").ap()
    gwp = nc.dram_tensor("gwp", [NF, D // 2], I32, kind="ExternalInput").ap()
    gsc = nc.dram_tensor("gsc", [NF, D // 32], F32, kind="ExternalInput").ap()
    uwp = nc.dram_tensor("uwp", [NF, D // 2], I32, kind="ExternalInput").ap()
    usc = nc.dram_tensor("usc", [NF, D // 32], F32, kind="ExternalInput").ap()
    dwp = nc.dram_tensor("dwp", [D, NF // 2], I32, kind="ExternalInput").ap()
    dsc = nc.dram_tensor("dsc", [D, NF // 32], F32, kind="ExternalInput").ap()
    out = nc.dram_tensor("out", [TLOC, D], F32, kind="ExternalOutput").ap()

    car_in = nc.dram_tensor("car_in", [D, RW], BF16, kind="Internal").ap()
    car_out = nc.dram_tensor("car_out", [D, RW], BF16, kind="Internal").ap()
    npair = 32
    pairbuf = nc.dram_tensor("pairbuf", [npair, 8, D, 128], BF16, kind="Internal").ap()
    myattn = nc.dram_tensor("myattn", [npair, D, 128], BF16, kind="Internal").ap()
    wg_nm = nc.dram_tensor("wg_nm", [NF, D], BF16, kind="Internal").ap()
    wu_nm = nc.dram_tensor("wu_nm", [NF, D], BF16, kind="Internal").ap()
    wd_nm = nc.dram_tensor("wd_nm", [D, NF], BF16, kind="Internal").ap()
    wgT = nc.dram_tensor("wgT", [D, NF], BF16, kind="Internal").ap()
    wuT = nc.dram_tensor("wuT", [D, NF], BF16, kind="Internal").ap()
    wdT = nc.dram_tensor("wdT", [NF, D], BF16, kind="Internal").ap()
    ynbuf = nc.dram_tensor("ynbuf", [TT, D], BF16, kind="Internal").ap()

    dbg_outs = {}
    if dbg.get("p2st"):
        dbg_outs["d_p2st"] = nc.dram_tensor("d_p2st", [n_steps, D, RW], BF16,
                                            kind="ExternalOutput").ap()
    if dbg.get("pred"):
        dbg_outs["d_pred"] = nc.dram_tensor("d_pred", [n_steps, D, RW], BF16,
                                            kind="ExternalOutput").ap()
    if dbg.get("myattn"):
        dbg_outs["d_myattn"] = nc.dram_tensor("d_myattn", [npair, D, 128], BF16,
                                              kind="ExternalOutput").ap()
    if dbg.get("wrow"):
        dbg_outs["d_wrow"] = nc.dram_tensor("d_wrow", [3, 128, D], BF16,
                                            kind="ExternalOutput").ap()

    SB_BASE = 16768
    SB_LIMIT = 229344
    off = [SB_BASE]

    def salloc(name, shape, dtype, align=64):
        nbytes = int(np.prod(shape[1:])) * mybir.dt.size(dtype)
        off[0] = (off[0] + align - 1) // align * align
        t = nc.alloc_sbuf_tensor_at(name, list(shape), dtype, offset=off[0])
        off[0] += nbytes
        assert off[0] <= SB_LIMIT, f"SBUF overflow at {name}: {off[0]}"
        return t

    ident = salloc("ident", [128, 128], BF16)
    kones = salloc("kones", [1, 128], F32)
    epsT = salloc("epsT", [128, 1], F32)
    bq_sg = salloc("bq_sg", [128, 1], F32)
    onec = salloc("onec", [128, 1], BF16)
    bq_th = salloc("bq_th", [128, 1], F32)
    bq_bb = salloc("bq_bb", [128, 1], F32)
    bq_cc = salloc("bq_cc", [128, 1], F32)
    PERSIST_END = off[0]

    c_w0 = salloc("c_w0", [128, DC, MK], BF16)
    c_w1 = salloc("c_w1", [128, DC, MK + 1], BF16)
    c_w1T = salloc("c_w1T", [128, 2, D], BF16)
    c_b0 = salloc("c_b0", [128, 2, 1], F32)
    c_lrfg = salloc("c_lrfg", [128, DC, 2], BF16)
    c_lrb = salloc("c_lrb", [1, 1], F32)
    c_fgb = salloc("c_fgb", [1, 1], F32)
    c_fv = salloc("c_fv", [128, 1], F32)
    c_nl = salloc("c_nl", [128, 1], F32)
    c_b18 = salloc("c_b18", [128, DC, 1], BF16)
    c_xraw = salloc("c_xraw", [128, 2, D], F32)
    c_xn = [salloc(f"c_xn{i}", [128, RC, D], BF16) for i in range(2)]
    c_xnT = [salloc(f"c_xnT{i}", [128, DC, RW], BF16) for i in range(2)]
    c_h1T = salloc("c_h1T", [128, 2, RW], BF16)
    c_h2T = salloc("c_h2T", [128, 2, RW], BF16)
    c_h1r = salloc("c_h1r", [128, RC, MK + 1], BF16)
    c_sgT = salloc("c_sgT", [128, 2, RW], BF16)
    c_sgr = salloc("c_sgr", [128, RC, MK], BF16)
    c_dpre = salloc("c_dpre", [128, RC, MK], BF16)
    c_pT = salloc("c_pT", [128, DC, RW], BF16)
    c_pr = salloc("c_pr", [128, RC, D], BF16)
    c_par = salloc("c_par", [128, DC, RW], BF16)
    c_p2 = salloc("c_p2", [128, DC, RW], BF16)
    c_st = salloc("c_st", [128, 2, 8, 3], F32)
    c_mv = salloc("c_mv", [128, RC, 2], F32)
    c_rs = salloc("c_rs", [128, RC, 1], F32)
    c_sc = salloc("c_sc", [2, RW], F32)
    c_u4 = salloc("c_u4", [1, 8], F32)
    c_fl = salloc("c_fl", [1, 2], F32)
    TTT_END = off[0]
    assert TTT_END <= SB_LIMIT

    off[0] = PERSIST_END
    f_acc = salloc("f_acc", [128, TT // 128, D], F32)
    f_ynT = salloc("f_ynT", [128, DC, TT], BF16)
    f_int = salloc("f_int", [128, DC, TT], BF16)
    f_wp = [salloc(f"f_wp{i}", [128, DC, NGSZ], BF16) for i in range(3)]
    f_wd = [salloc(f"f_wd{i}", [128, DC, 512], BF16) for i in range(2)]
    f_ds = [salloc(f"f_ds{i}", [128, D], BF16) for i in range(2)]
    f_gel = salloc("f_gel", [128, RW], BF16)
    f_ay = salloc("f_ay", [128, D], BF16)
    f_st = salloc("f_st", [128, 8, 3], F32)
    f_mv = salloc("f_mv", [128, 2], F32)
    f_rs = salloc("f_rs", [128, 1], F32)
    FFN_END = off[0]
    assert FFN_END <= SB_LIMIT, FFN_END
    off[0] = PERSIST_END
    g_slab = [salloc(f"g_slab{i}", [128, NF], BF16) for i in range(2)]
    assert off[0] <= SB_LIMIT
    NW = 8
    q_pk, q_sc, q_wb, q_cd, q_s, q_m, q_a, q_b, q_out = \
        [], [], [], [], [], [], [], [], []
    for i in range(NW):
        q_pk.append(salloc(f"q_pk{i}", [128, 512], I32))
        q_sc.append(salloc(f"q_sc{i}", [128, 32], F32))
        q_wb.append(salloc(f"q_wb{i}", [128, 512], F32))
        q_cd.append(salloc(f"q_cd{i}", [128, 1024], BF16))
        q_s.append(salloc(f"q_s{i}", [128, 1024], BF16))
        q_m.append(salloc(f"q_m{i}", [128, 1024], BF16))
        q_a.append(salloc(f"q_a{i}", [128, 1024], BF16))
        q_b.append(salloc(f"q_b{i}", [128, 1024], BF16))
        q_out.append(salloc(f"q_out{i}", [128, 1024], BF16))
    assert off[0] <= SB_LIMIT, off[0]

    from contextlib import ExitStack
    with ExitStack() as ctx:
        S = Sched(nc)
        for name in ["sPE", "sACT", "sDVE", "sPOOL", "sDMA", "sGD",
                     "sCOLL", "sRS", "sST", "sLD", "sRB", "sXB0", "sXB1", "sSL",
                     "sQ0", "sQ1", "sQ2", "sQ3", "sQ4", "sQ5", "sQ6", "sQ7"]:
            S.sem(ctx, name)
        ps_a = ctx.enter_context(nc.psum_tensor("ps_a", [128, 2, RW], F32))
        ps_b = ctx.enter_context(nc.psum_tensor("ps_b", [128, 2, RW], F32))
        ps_d0 = ctx.enter_context(nc.psum_tensor("ps_d0", [128, 4, 128], BF16))
        ps_d1 = ctx.enter_context(nc.psum_tensor("ps_d1", [128, 4, 128], BF16))
        ps_s = ctx.enter_context(nc.psum_tensor("ps_s", [128, RW], F32))
        ps_f = ctx.enter_context(nc.psum_tensor("ps_f", [128, RW], F32))
        psd = [ps_d0, ps_d1]
        blk = ctx.enter_context(nc.Block())
        op = S.op

        op("gpsimd", lambda g: g.memset(ident[:], 0.0), inc="sPOOL")
        op("gpsimd", lambda g: g.affine_select(
            out=ident[:], in_=ident[:], compare_op=ALU.not_equal, fill=1.0,
            base=0, pattern=[[-1, 128]], channel_multiplier=1),
           waits=[("sPOOL", S.v("sPOOL"))], inc="sPOOL")
        op("gpsimd", lambda g: g.memset(kones[:], 1.0), inc="sPOOL")
        op("gpsimd", lambda g: g.memset(epsT[:], 1e-5), inc="sPOOL")
        op("gpsimd", lambda g: g.memset(bq_sg[:], 143.5), inc="sPOOL")
        op("gpsimd", lambda g: g.memset(onec[:], 1.0), inc="sPOOL")
        op("gpsimd", lambda g: g.memset(bq_th[:], -8.96875), inc="sPOOL")
        op("gpsimd", lambda g: g.memset(bq_bb[:], -142.0), inc="sPOOL")
        op("gpsimd", lambda g: g.memset(bq_cc[:], -288.0), inc="sPOOL")
        IDENT_POOL = S.v("sPOOL")

        scrf = c_xraw[:].rearrange("p a d -> p (a d)")

        def stage(view, src, dst, prev_dve):
            d_ = op("sync", lambda s, view=view, src=src: s.dma_start(out=view, in_=src),
                    waits=[("sDVE", prev_dve)], inc="sDMA", n=16)
            return op("vector", lambda v, dst=dst, view=view: v.tensor_copy(dst, view),
                      waits=[("sDMA", d_)], inc="sDVE")

        cv = 0
        v16 = scrf[:, 0:DC * MK].rearrange("p (a m) -> p a m", m=MK)
        cv = stage(v16, w0kTd.rearrange("(a p) m -> p a m", p=128), c_w0[:], cv)
        cv = stage(v16, w1dmd.rearrange("(a p) m -> p a m", p=128), c_w1[:, :, 0:MK], cv)
        vb1 = scrf[:, 0:DC].rearrange("p (a o) -> p a o", o=1)
        cv = stage(vb1, b1d.rearrange("p (a o) -> p a o", o=1), c_w1[:, :, MK:MK + 1], cv)
        cv = op("vector", lambda v: v.tensor_scalar_mul(
            c_b18[:], c_w1[:, :, MK:MK + 1], 0.125), waits=[("sDVE", cv)], inc="sDVE")
        v2d = scrf[:, 0:2 * D].rearrange("p (a d) -> p a d", d=D)
        cv = stage(v2d, w1kTd.rearrange("(a p) d -> p a d", p=128), c_w1T[:], cv)
        vb0 = scrf[:, 0:2].rearrange("p (a o) -> p a o", o=1)
        cv = stage(vb0, b0kd.rearrange("p (a o) -> p a o", o=1), c_b0[:], cv)
        vlf = scrf[:, 0:2 * DC].rearrange("p (a t) -> p a t", t=2)
        cv = stage(vlf, lrfgd, c_lrfg[:], cv)
        dsm = op("sync", lambda s: s.dma_start(out=c_lrb[:], in_=lrbd.rearrange("(a o) -> a o", a=1)),
                 waits=[("sDVE", cv)], inc="sDMA", n=16)
        dsm = op("sync", lambda s: s.dma_start(out=c_fgb[:], in_=fgbd.rearrange("(a o) -> a o", a=1)),
                 waits=[("sDMA", dsm)], inc="sDMA", n=16)
        P0_DVE = cv
        P0_DMA = S.v("sDMA")

        dq_jobs = []
        if do_ffn and dq_on:
            for (pk, sc, nat, rows, K) in [(gwp, gsc, wg_nm, NF, D),
                                           (uwp, usc, wu_nm, NF, D),
                                           (dwp, dsc, wd_nm, D, NF)]:
                for k0 in range(0, K, 1024):
                    for r0 in range(0, rows, 128):
                        dq_jobs.append((pk, sc, nat, r0, k0))

        def dq_stages(j, barrier_w, last, wr_last):
            (pk, sc, nat, r0, k0) = dq_jobs[j]
            st = j % NW
            qsem = f"sQ{st}"
            cd_hi = q_cd[st][:, 512:1024]
            cd_lo = q_cd[st][:, 0:512]
            o = {}

            def s0():
                w0 = list(barrier_w) if last is None else [("sDVE", last[0]), (qsem, last[1])]
                o["dl"] = op("scalar", lambda s: s.dma_start(
                    out=q_pk[st][:], in_=pk[r0:r0 + 128, k0 // 2:k0 // 2 + 512]),
                    waits=w0, inc=qsem, n=16)
                o["ds"] = op("scalar", lambda s: s.dma_start(
                    out=q_sc[st][:], in_=sc[r0:r0 + 128, k0 // 32:k0 // 32 + 32]),
                    inc=qsem, n=16)

            def s1():
                o["wb"] = op("vector", lambda v: v.tensor_scalar(
                    q_wb[st][:], q_pk[st][:], 2312.0, 0.0, ALU.add, ALU.bypass),
                    waits=[(qsem, o["ds"])], inc="sDVE")

            def s2():
                o["th"] = op("scalar", lambda s: s.activation(
                    cd_hi, q_wb[st][:], AF.Identity, bias=bq_th[:], scale=0.0625),
                    waits=[("sDVE", o["wb"])], inc="sACT")

            def s3():
                o["tl"] = op("vector", lambda v: v.scalar_tensor_tensor(
                    cd_lo, cd_hi, -16.0, q_wb[st][:], ALU.mult, ALU.add),
                    waits=[("sACT", o["th"])], inc="sDVE")

            def s4():
                o["sg"] = op("scalar", lambda s: s.activation(
                    q_s[st][:], q_cd[st][:], AF.Sign, bias=bq_sg[:], scale=-1.0),
                    waits=[("sDVE", o["tl"])], inc="sACT")

            def s5():
                o["mm"] = op("vector", lambda v: v.scalar_tensor_tensor(
                    q_m[st][:], q_s[st][:], 4.0, q_cd[st][:], ALU.mult, ALU.add),
                    waits=[("sACT", o["sg"])], inc="sDVE")

            def s6():
                o["aa"] = op("gpsimd", lambda g: g.tensor_scalar(
                    q_a[st][:], q_m[st][:], 0.5, -70.0, ALU.mult, ALU.add),
                    waits=[("sDVE", o["mm"])], inc="sPOOL")
                o["bb"] = op("scalar", lambda s: s.activation(
                    q_b[st][:], q_m[st][:], AF.Identity, bias=bq_bb[:], scale=1.0),
                    waits=[("sDVE", o["mm"])], inc="sACT")

            def s7():
                o["m1"] = op("vector", lambda v: v.tensor_max(
                    q_a[st][:], q_a[st][:], q_b[st][:]),
                    waits=[("sPOOL", o["aa"]), ("sACT", o["bb"])], inc="sDVE")

            def s8():
                o["cc"] = op("scalar", lambda s: s.activation(
                    q_b[st][:], q_m[st][:], AF.Identity, bias=bq_cc[:], scale=2.0),
                    waits=[("sDVE", o["m1"])], inc="sACT")

            def s9():
                o["m2"] = op("vector", lambda v: v.tensor_max(
                    q_a[st][:], q_a[st][:], q_b[st][:]),
                    waits=[("sACT", o["cc"])], inc="sDVE")

            def s10():
                o["vv"] = op("vector", lambda v: v.tensor_mul(
                    q_a[st][:], q_a[st][:], q_s[st][:]),
                    waits=[("sDVE", o["m2"])], inc="sDVE")

            def s11():
                scb = q_sc[st][:].rearrange(
                    "p (g o) -> p g o", o=1).broadcast_to([128, 32, 16])
                op("vector", lambda v, scb=scb: v.tensor_tensor(
                    q_out[st][:, 0:512].rearrange("p (g e) -> p g e", g=32),
                    q_a[st][:, 0:512].rearrange("p (g e) -> p g e", g=32),
                    scb, ALU.mult),
                    waits=[("sDVE", o["vv"]), (qsem, wr_last)], inc="sDVE")
                o["so"] = op("vector", lambda v, scb=scb: v.tensor_tensor(
                    q_out[st][:, 512:1024].rearrange("p (g e) -> p g e", g=32),
                    q_a[st][:, 512:1024].rearrange("p (g e) -> p g e", g=32),
                    scb, ALU.mult), inc="sDVE")

            def s12():
                o["wr"] = op("gpsimd", lambda g: g.dma_start(
                    out=nat[r0:r0 + 128, :].rearrange(
                        "r (t a) -> r t a", t=2)[:, :, k0 // 2:k0 // 2 + 512],
                    in_=q_out[st][:].rearrange("p (t a) -> p t a", t=2)),
                    waits=[("sDVE", o["so"])], inc=qsem, n=16)

            return [s0, s1, s2, s3, s4, s5, s6, s7, s8, s9, s10, s11, s12], o

        def emit_dq_wave_phase(barrier_w, boundary_cb=None):
            NS = 13
            last = [None] * NW
            wr_last = [0] * NW
            lane_stage = [None] * NW
            lane_pos = [0] * NW
            nxt = [l for l in range(NW)]
            ncomp = 0

            def start_lane(l):
                if nxt[l] >= len(dq_jobs):
                    return False
                j = nxt[l]
                nxt[l] += NW
                stages, o = dq_stages(j, barrier_w, last[l], wr_last[l])
                lane_stage[l] = (stages, o, l)
                lane_pos[l] = 0
                return True

            for l in range(NW):
                if not start_lane(l):
                    lane_stage[l] = None
            active = sum(1 for x in lane_stage if x)
            while active:
                for l in range(NW):
                    if lane_stage[l] is None:
                        continue
                    stages, o, st_ = lane_stage[l]
                    stages[lane_pos[l]]()
                    lane_pos[l] += 1
                    if lane_pos[l] == NS:
                        last[st_] = (o["so"], o["wr"])
                        wr_last[st_] = o["wr"]
                        lane_stage[l] = None
                        ncomp += 1
                        if boundary_cb and ncomp % NW == 0:
                            boundary_cb(ncomp)
                        start_lane(l)
                active = sum(1 for x in lane_stage if x)

        psd_evac = [0, 0]

        def emit_prep(c, extra_load_waits=()):
            par = c % 2
            xv = xfull.rearrange("b (n q) d -> n q b d", q=128)[c]
            nrm = 0
            dl = 0
            for hf in range(2):
                w = list(extra_load_waits) if hf == 0 else [("sDVE", nrm)]
                if dl:
                    w.append(("sLD", dl))
                dl = op("sync", lambda s, hf=hf, xv=xv: s.dma_start(
                    out=c_xraw[:], in_=xv[:, 2 * hf:2 * hf + 2, :]),
                    waits=w, inc="sLD", n=16)
                last = 0
                for rc2 in range(2):
                    for sgm in range(4):
                        last = op("vector", lambda v, rc2=rc2, sgm=sgm: v.bn_stats(
                            c_st[:, rc2, 2 * sgm:2 * sgm + 2, :].rearrange("p a b -> p (a b)"),
                            c_xraw[:, rc2, 512 * sgm:512 * (sgm + 1)]),
                            waits=[("sLD", dl)] if (rc2 == 0 and sgm == 0) else (),
                            inc="sDVE")
                for rc2 in range(2):
                    last = op("vector", lambda v, rc2=rc2, hf=hf: v.bn_aggr(
                        c_mv[:, 2 * hf + rc2, :], c_st[:, rc2, :, :].rearrange("p a b -> p (a b)")),
                        waits=[("sDVE", last)] if rc2 == 0 else (), inc="sDVE")
                sq = op("scalar", lambda s, hf=hf: s.activation(
                    c_rs[:, 2 * hf:2 * hf + 2, :].rearrange("p a o -> p (a o)"),
                    c_mv[:, 2 * hf:2 * hf + 2, 1:2].rearrange("p a o -> p (a o)"),
                    AF.Sqrt, bias=epsT[:], scale=1.0),
                    waits=[("sDVE", last), ("sPOOL", IDENT_POOL)], inc="sACT")
                rc_ = op("vector", lambda v, hf=hf: v.reciprocal(
                    c_rs[:, 2 * hf:2 * hf + 2, :].rearrange("p a o -> p (a o)"),
                    c_rs[:, 2 * hf:2 * hf + 2, :].rearrange("p a o -> p (a o)")),
                    waits=[("sACT", sq)], inc="sDVE")
                for rc2 in range(2):
                    nrm = op("vector", lambda v, rc2=rc2, hf=hf, par=par: v.tensor_scalar(
                        c_xn[par][:, 2 * hf + rc2, :], c_xraw[:, rc2, :],
                        c_mv[:, 2 * hf + rc2, 0:1], c_rs[:, 2 * hf + rc2, :],
                        ALU.subtract, ALU.mult),
                        waits=[("sDVE", rc_)] if rc2 == 0 else (), inc="sDVE")
            for dc in range(DC):
                sl = dc % 2
                for rcc in range(RC):
                    w = []
                    if dc == 0 and rcc == 0:
                        w = [("sDVE", nrm), ("sPOOL", IDENT_POOL)]
                    if rcc == 0 and psd_evac[sl]:
                        w.append(("sDVE", psd_evac[sl]))
                    op("tensor", lambda t, dc=dc, rcc=rcc, sl=sl, par=par: t.transpose(
                        psd[sl][:, rcc, :], c_xn[par][:, rcc, 128 * dc:128 * (dc + 1)],
                        ident[:]), waits=w, inc="sPE")
                pe_b = S.v("sPE")
                psd_evac[sl] = op("vector", lambda v, dc=dc, sl=sl, par=par: v.tensor_copy(
                    c_xnT[par][:, dc, :], psd[sl][:].rearrange("p a b -> p (a b)")),
                    waits=[("sPE", pe_b)], inc="sDVE")
            return nrm

        nrm0 = emit_prep(0, extra_load_waits=[("sDVE", P0_DVE)])
        prev_nrm = {0: nrm0}

        psb_free = {0: None, 1: None}
        psa_free = 0
        pending_rs = None
        deferred = None
        st1_prev = 0
        rb_prev = 0

        def emit_deferred(dctx):
            c1 = dctx["c"]
            xnT1 = dctx["xnT"]
            first = True
            for kc in range(DC):
                for ic in range(2):
                    w = []
                    if first:
                        w = [("sDVE", dctx["upd"]), ("sACT", psa_free)]
                        first = False
                    op("tensor", lambda t, kc=kc, ic=ic, xnT1=xnT1: t.matmul(
                        ps_a[:, ic, :], c_w0[:, kc, 128 * ic:128 * (ic + 1)],
                        xnT1[:, kc, :], start=(kc == 0), stop=(kc == DC - 1)),
                        waits=w, inc="sPE")
            pe_fwd2 = S.v("sPE")
            a2 = 0
            for mc in range(2):
                a2 = op("scalar", lambda s, mc=mc: s.activation(
                    c_h2T[:, mc, :], ps_a[:, mc, :], AF.Silu,
                    bias=c_b0[:, mc, :], scale=1.0),
                    waits=[("sPE", pe_fwd2)] if mc == 0 else (), inc="sACT")
            ev_p2 = 0
            ev_hist = {}
            for dc in range(DC):
                for kc in range(2):
                    w = []
                    if dc == 0 and kc == 0:
                        w = [("sACT", a2)]
                    if kc == 0 and dc >= 2:
                        w.append(("sACT", ev_hist[dc - 2]))
                    if kc == 0 and dc < 2 and psb_free[dc % 2]:
                        w.append(psb_free[dc % 2])
                    op("tensor", lambda t, dc=dc, kc=kc: t.matmul(
                        ps_b[:, dc % 2, :], c_w1T[:, kc, 128 * dc:128 * (dc + 1)],
                        c_h2T[:, kc, :], start=(kc == 0), stop=(kc == 1)),
                        waits=w, inc="sPE")
                pe_p2 = S.v("sPE")
                wex = [("sPE", pe_p2)]
                if dc == 0:
                    wex += [("sGD", S.v("sGD"))]
                ev_p2 = op("scalar", lambda s, dc=dc: s.activation(
                    c_p2[:, dc, :], ps_b[:, dc % 2, :], AF.Identity,
                    bias=c_b18[:, dc, :], scale=1.0), waits=wex, inc="sACT")
                ev_hist[dc] = ev_p2
                psb_free[dc % 2] = ("sACT", ev_p2)
            half = 32 if n_steps == NCH else max(1, n_steps // 2)
            h = (c1 // half) if full_run else 0
            p = (c1 % half) if full_run else c1 % npair
            gd = 0
            for b in range(4):
                gd = op("gpsimd", lambda g, b=b, p=p, h=h: g.dma_start(
                    out=pairbuf[p, 2 * b + h].rearrange("(a q) r -> q a r", q=128),
                    in_=c_p2[:, :, 128 * b:128 * (b + 1)]),
                    waits=[("sACT", ev_p2)] if b == 0 else (), inc="sGD", n=16)
            if dbg.get("p2st"):
                op("gpsimd", lambda g, c1=c1: g.dma_start(
                    out=dbg_outs["d_p2st"][c1].rearrange("(a q) r -> q a r", q=128),
                    in_=c_p2[:]), waits=[("sACT", ev_p2)], inc="sGD", n=16)
            rs = None
            if full_run and h == 1:
                rs = (lambda g, p=p: g.collective_compute(
                    "ReduceScatter", ALU.add, RG8,
                    ins=[pairbuf[p].rearrange("s d r -> (s d) r").opt()],
                    outs=[myattn[p].opt()]),
                    [("sGD", S.v("sGD"))])
            return a2, ev_p2, rs

        for c in range(n_steps):
            par = c % 2
            xn = c_xn[par]
            xnT = c_xnT[par]

            w0_ = [("sDVE", S.v("sDVE"))]
            if psa_free:
                w0_.append(("sACT", psa_free))
            if c == 0:
                w0_.append(("sDMA", P0_DMA))
            first = True
            for kc in range(DC):
                for ic in range(2):
                    op("tensor", lambda t, kc=kc, ic=ic, xnT=xnT: t.matmul(
                        ps_a[:, ic, :], c_w0[:, kc, 128 * ic:128 * (ic + 1)],
                        xnT[:, kc, :], start=(kc == 0), stop=(kc == DC - 1)),
                        waits=w0_ if first else (), inc="sPE")
                    first = False
                op("tensor", lambda t, kc=kc, xnT=xnT: t.matmul(
                    ps_s[0:1, :], c_lrfg[:, kc, 0:1], xnT[:, kc, :],
                    start=(kc == 0), stop=(kc == DC - 1)), inc="sPE")
                op("tensor", lambda t, kc=kc, xnT=xnT: t.matmul(
                    ps_s[32:33, :], c_lrfg[:, kc, 1:2], xnT[:, kc, :],
                    start=(kc == 0), stop=(kc == DC - 1)), inc="sPE")
            pe_fwd1 = S.v("sPE")

            act_h1 = 0
            for mc in range(2):
                op("scalar", lambda s, mc=mc: s.activation(
                    c_h1T[:, mc, :], ps_a[:, mc, :], AF.Silu,
                    bias=c_b0[:, mc, :], scale=1.0),
                    waits=[("sPE", pe_fwd1)] if mc == 0 else (), inc="sACT")
                act_h1 = op("scalar", lambda s, mc=mc: s.activation(
                    c_sgT[:, mc, :], ps_a[:, mc, :], AF.Derivative_silu,
                    bias=c_b0[:, mc, :], scale=1.0), inc="sACT")
            psa_free = act_h1
            sv_ = op("scalar", lambda s: s.activation(
                c_sc[0:1, :], ps_s[0:1, :], AF.Sigmoid, bias=c_lrb[:], scale=1.0),
                waits=[("sPE", pe_fwd1)], inc="sACT")
            e1 = op("vector", lambda v: v.tensor_reduce(
                c_u4[0:1, 6:7].rearrange("p (a o) -> p a o", o=1),
                c_sc[0:1, :].rearrange("p (a r) -> p a r", a=1), AX.X, ALU.add),
                waits=[("sACT", sv_)], inc="sDVE")
            nl_ = op("vector", lambda v: v.tensor_scalar_mul(
                c_u4[0:1, 4:5], c_u4[0:1, 6:7], -0.01 / RW),
                waits=[("sDVE", e1)], inc="sDVE")
            u4_ = op("vector", lambda v: v.tensor_reduce(
                c_u4[0:1, 0:4].rearrange("p (a o) -> p a o", o=1),
                ps_s[32:33, :].rearrange("p (a r) -> p a r", a=4), AX.X, ALU.add),
                waits=[("sDVE", e1)], inc="sDVE")
            sg4 = op("scalar", lambda s: s.activation(
                c_sc[0:1, 0:4], c_u4[0:1, 0:4], AF.Sigmoid, bias=c_fgb[:],
                scale=1.0 / 128), waits=[("sDVE", u4_)], inc="sACT")
            f1 = op("vector", lambda v: v.tensor_reduce(
                c_u4[0:1, 5:6].rearrange("p (a o) -> p a o", o=1),
                c_sc[0:1, 0:4].rearrange("p (a r) -> p a r", a=1), AX.X, ALU.add),
                waits=[("sACT", sg4)], inc="sDVE")
            fl_ = op("vector", lambda v: v.tensor_scalar_mul(
                c_fl[0:1, 0:1], c_u4[0:1, 5:6], 0.25), waits=[("sDVE", f1)], inc="sDVE")
            fl2 = op("vector", lambda v: v.tensor_copy(c_fl[0:1, 1:2], c_u4[0:1, 4:5]),
                     waits=[("sDVE", nl_)], inc="sDVE")

            ev_par = 0
            ev_hist = {}
            for dc in range(DC):
                for kc in range(2):
                    w = []
                    if dc == 0 and kc == 0:
                        w = [("sACT", act_h1)]
                    if kc == 0 and dc >= 2:
                        w = [("sACT", ev_hist[dc - 2])]
                    if kc == 0 and dc < 2 and psb_free[dc % 2]:
                        w.append(psb_free[dc % 2])
                    op("tensor", lambda t, dc=dc, kc=kc: t.matmul(
                        ps_b[:, dc % 2, :], c_w1T[:, kc, 128 * dc:128 * (dc + 1)],
                        c_h1T[:, kc, :], start=(kc == 0), stop=(kc == 1)),
                        waits=w, inc="sPE")
                pe_pk = S.v("sPE")
                wex = [("sPE", pe_pk)]
                if dc == 0 and st1_prev:
                    wex.append(("sGD", st1_prev))
                ev_par = op("scalar", lambda s, dc=dc: s.activation(
                    c_par[:, dc, :], ps_b[:, dc % 2, :], AF.Identity,
                    bias=c_b18[:, dc, :], scale=1.0), waits=wex, inc="sACT")
                ev_hist[dc] = ev_par
                psb_free[dc % 2] = ("sACT", ev_par)

            bc = op("tensor", lambda t: t.matmul(
                ps_s[:, 64:66], kones[:], c_fl[0:1, :], start=True, stop=True),
                waits=[("sDVE", max(fl_, fl2))], inc="sPE")
            fv_ = op("vector", lambda v: v.tensor_copy(c_fv[:], ps_s[:, 64:65]),
                     waits=[("sPE", bc)], inc="sDVE")
            nv_ = op("vector", lambda v: v.tensor_copy(c_nl[:], ps_s[:, 65:66]),
                     inc="sDVE")
            sc_done = nv_

            st1 = op("gpsimd", lambda g: g.dma_start(
                out=car_in.rearrange("(a p) r -> p a r", p=128), in_=c_par[:]),
                waits=[("sACT", ev_par), ("sRB", rb_prev)], inc="sGD", n=16)
            ar = op("gpsimd", lambda g: g.collective_compute(
                "AllReduce", ALU.add, RG8, ins=[car_in.opt()], outs=[car_out.opt()]),
                waits=[("sGD", st1)], inc="sCOLL")
            st1_prev = st1
            if pending_rs is not None:
                op("gpsimd", pending_rs[0], waits=pending_rs[1], inc="sRS")
                pending_rs = None

            last_evp2 = 0
            if deferred is not None:
                a2v, last_evp2, pending_rs = emit_deferred(deferred)
                psa_free = a2v
                deferred = None

            for mc in range(2):
                sl = mc % 2
                w0t = []
                if mc == 0:
                    w0t = [("sACT", act_h1)]
                if psd_evac[sl]:
                    w0t.append(("sDVE", psd_evac[sl]))
                for rcc in range(RC):
                    op("tensor", lambda t, mc=mc, rcc=rcc, sl=sl: t.transpose(
                        psd[sl][:, rcc, :], c_sgT[:, mc, 128 * rcc:128 * (rcc + 1)],
                        ident[:]), waits=w0t if rcc == 0 else (), inc="sPE")
                pe_b = S.v("sPE")
                psd_evac[sl] = op("vector", lambda v, mc=mc, sl=sl: v.tensor_copy(
                    c_sgr[:, :, 128 * mc:128 * (mc + 1)], psd[sl][:]),
                    waits=[("sPE", pe_b)], inc="sDVE")
            sgr_done = S.v("sDVE")
            h1r_done = 0
            for mc in range(2):
                sl = mc % 2
                for rcc in range(RC):
                    w = [("sDVE", psd_evac[sl])] if rcc == 0 else ()
                    op("tensor", lambda t, mc=mc, rcc=rcc, sl=sl: t.transpose(
                        psd[sl][:, rcc, :], c_h1T[:, mc, 128 * rcc:128 * (rcc + 1)],
                        ident[:]), waits=w, inc="sPE")
                pe_b = S.v("sPE")
                psd_evac[sl] = op("vector", lambda v, mc=mc, sl=sl: v.tensor_scalar_mul(
                    c_h1r[:, :, 128 * mc:128 * (mc + 1)], psd[sl][:], c_nl[:]),
                    waits=[("sPE", pe_b), ("sDVE", sc_done)], inc="sDVE")
            for rcc in range(RC):
                h1r_done = op("vector", lambda v, rcc=rcc: v.tensor_copy(
                    c_h1r[:, rcc, MK:MK + 1], c_nl[:]),
                    waits=[("sDVE", sc_done)] if rcc == 0 else (), inc="sDVE")

            if c + 1 < n_steps:
                emit_prep(c + 1, extra_load_waits=[("sDVE", prev_nrm[c])])
                prev_nrm[c + 1] = S.v("sDVE")

            rb1 = op("sync", lambda s: s.dma_start(
                out=c_pT[:], in_=car_out.rearrange("(a p) r -> p a r", p=128)),
                waits=[("sCOLL", ar)], inc="sRB", n=16)
            if dbg.get("pred"):
                op("gpsimd", lambda g, c=c: g.dma_start(
                    out=dbg_outs["d_pred"][c].rearrange("(a p) r -> p a r", p=128),
                    in_=car_out.rearrange("(a p) r -> p a r", p=128)),
                    waits=[("sCOLL", ar)], inc="sGD", n=16)
            rbx = 0
            for rcc in range(RC):
                qn = "sync" if rcc % 2 == 0 else "scalar"
                wq = [("sCOLL", ar)] if rcc < 2 else ()
                rbx = op(qn, lambda s, rcc=rcc: s.dma_start_transpose(
                    c_pr[:, rcc, :], car_out[:, 128 * rcc:128 * (rcc + 1)]),
                    waits=wq, inc="sRB", n=16)
            rb_prev = rbx


            dT1 = op("vector", lambda v, xnT=xnT: v.tensor_sub(
                c_pT[:].rearrange("p a r -> p (a r)"),
                c_pT[:].rearrange("p a r -> p (a r)"),
                xnT[:].rearrange("p a r -> p (a r)")),
                waits=[("sRB", rbx)], inc="sDVE")
            dT2 = op("vector", lambda v: v.tensor_scalar_mul(
                c_pT[:].rearrange("p a r -> p (a r)"),
                c_pT[:].rearrange("p a r -> p (a r)"), TWO_N),
                waits=[("sDVE", dT1)], inc="sDVE")
            dR1 = op("vector", lambda v, xn=xn: v.tensor_sub(
                c_pr[:].rearrange("p a r -> p (a r)"),
                c_pr[:].rearrange("p a r -> p (a r)"),
                xn[:].rearrange("p a r -> p (a r)")),
                waits=[("sRB", rbx)], inc="sDVE")
            dR2 = op("vector", lambda v: v.tensor_scalar_mul(
                c_pr[:].rearrange("p a r -> p (a r)"),
                c_pr[:].rearrange("p a r -> p (a r)"), TWO_N),
                waits=[("sDVE", dR1)], inc="sDVE")

            dhv = ps_b[:].rearrange("p a (c m) -> p (a c) m", m=MK)
            first = True
            for ic in range(RC):
                for kc in range(DC):
                    w = []
                    if first:
                        w = [("sDVE", dT2)]
                        if psb_free[0]:
                            w.append(psb_free[0])
                        if psb_free[1]:
                            w.append(psb_free[1])
                        first = False
                    op("tensor", lambda t, ic=ic, kc=kc, dhv=dhv: t.matmul(
                        dhv[:, ic, :], c_pT[:, kc, 128 * ic:128 * (ic + 1)],
                        c_w1[:, kc, 0:MK], start=(kc == 0), stop=(kc == DC - 1)),
                        waits=w, inc="sPE")
            pe_dh = S.v("sPE")
            dpre_ = op("vector", lambda v, dhv=dhv: v.scalar_tensor_tensor(
                c_dpre[:].rearrange("p a m -> p (a m)"),
                dhv[:].rearrange("p a m -> p (a m)"), c_nl[:],
                c_sgr[:].rearrange("p a m -> p (a m)"), ALU.mult, ALU.mult),
                waits=[("sPE", pe_dh), ("sDVE", sgr_done)], inc="sDVE")
            psb_free[0] = psb_free[1] = ("sDVE", dpre_)

            upd_w1 = 0
            uw_hist = {}
            for dc in range(DC):
                for kc in range(RC):
                    w = []
                    if dc == 0 and kc == 0:
                        w = [("sDVE", max(dR2, h1r_done, dpre_))]
                    if kc == 0 and dc >= 2:
                        w.append(("sDVE", uw_hist[dc - 2]))
                    if kc == 0 and dc < 2 and psb_free[dc % 2]:
                        w.append(psb_free[dc % 2])
                    op("tensor", lambda t, dc=dc, kc=kc: t.matmul(
                        ps_b[:, dc % 2, 0:MK + 1], c_pr[:, kc, 128 * dc:128 * (dc + 1)],
                        c_h1r[:, kc, :], start=(kc == 0), stop=(kc == RC - 1)),
                        waits=w, inc="sPE")
                pe_g = S.v("sPE")
                upd_w1 = op("vector", lambda v, dc=dc: v.scalar_tensor_tensor(
                    c_w1[:, dc, :], c_w1[:, dc, :], c_fv[:],
                    ps_b[:, dc % 2, 0:MK + 1], ALU.mult, ALU.add),
                    waits=[("sPE", pe_g), ("sDVE", fv_)] if dc == 0 else [("sPE", pe_g)],
                    inc="sDVE")
                uw_hist[dc] = upd_w1
                psb_free[dc % 2] = ("sDVE", upd_w1)
            b18_ = op("vector", lambda v: v.tensor_scalar_mul(
                c_b18[:], c_w1[:, :, MK:MK + 1], 0.125),
                waits=[("sDVE", upd_w1), ("sACT", last_evp2)], inc="sDVE")
            upd_w0 = 0
            uw_hist = {}
            for dc in range(DC):
                for kc in range(RC):
                    w = []
                    if dc == 0 and kc == 0:
                        w = [("sDVE", dpre_)]
                    if kc == 0 and dc >= 2:
                        w.append(("sDVE", uw_hist[dc - 2]))
                    if kc == 0 and dc < 2 and psb_free[dc % 2]:
                        w.append(psb_free[dc % 2])
                    op("tensor", lambda t, dc=dc, kc=kc, xn=xn: t.matmul(
                        ps_b[:, dc % 2, 0:MK], xn[:, kc, 128 * dc:128 * (dc + 1)],
                        c_dpre[:, kc, :], start=(kc == 0), stop=(kc == RC - 1)),
                        waits=w, inc="sPE")
                pe_g = S.v("sPE")
                upd_w0 = op("vector", lambda v, dc=dc: v.scalar_tensor_tensor(
                    c_w0[:, dc, :], c_w0[:, dc, :], c_fv[:],
                    ps_b[:, dc % 2, 0:MK], ALU.mult, ALU.add),
                    waits=[("sPE", pe_g)], inc="sDVE")
                uw_hist[dc] = upd_w0
                psb_free[dc % 2] = ("sDVE", upd_w0)
            for mc in range(2):
                for kc in range(RC):
                    op("tensor", lambda t, mc=mc, kc=kc: t.matmul(
                        ps_s[:, 128 + mc:129 + mc],
                        c_dpre[:, kc, 128 * mc:128 * (mc + 1)],
                        onec[:], start=(kc == 0), stop=(kc == RC - 1)),
                        waits=[("sDVE", dpre_)] if (mc == 0 and kc == 0) else (),
                        inc="sPE")
            pe_b0 = S.v("sPE")
            upd_b0 = op("vector", lambda v: v.scalar_tensor_tensor(
                c_b0[:].rearrange("p a o -> p (a o)"),
                c_b0[:].rearrange("p a o -> p (a o)"), c_fv[:],
                ps_s[:, 128:130], ALU.mult, ALU.add),
                waits=[("sPE", pe_b0)], inc="sDVE")

            flip_done = 0
            for mc in range(2):
                for d4 in range(4):
                    sl = (2 * mc + d4) % 2
                    w = []
                    if mc == 0 and d4 == 0:
                        w = [("sDVE", max(upd_w1, b18_))]
                    if psd_evac[sl]:
                        w.append(("sDVE", psd_evac[sl]))
                    for j in range(4):
                        dc = 4 * d4 + j
                        op("tensor", lambda t, dc=dc, mc=mc, sl=sl: t.transpose(
                            psd[sl][:, dc % 4, :], c_w1[:, dc, 128 * mc:128 * (mc + 1)],
                            ident[:]), waits=w if j == 0 else (), inc="sPE")
                    pe_b = S.v("sPE")
                    flip_done = op("vector", lambda v, mc=mc, d4=d4, sl=sl: v.tensor_copy(
                        c_w1T[:, mc, 512 * d4:512 * (d4 + 1)],
                        psd[sl][:].rearrange("p a b -> p (a b)")),
                        waits=[("sPE", pe_b)], inc="sDVE")
                    psd_evac[sl] = flip_done

            deferred = {"c": c, "xnT": xnT,
                        "upd": max(upd_w0, upd_b0, flip_done)}

        if deferred is not None:
            _, _, pending_rs2 = emit_deferred(deferred)
            deferred = None
            if pending_rs is not None:
                op("gpsimd", pending_rs[0], waits=pending_rs[1], inc="sRS")
            pending_rs = pending_rs2
        if pending_rs is not None:
            op("gpsimd", pending_rs[0], waits=pending_rs[1], inc="sRS")
            pending_rs = None

        if dbg.get("myattn"):
            op("gpsimd", lambda g: g.dma_start(
                out=dbg_outs["d_myattn"].rearrange("n d r -> (n d r)").rearrange("(a b) -> a b", a=1),
                in_=myattn.rearrange("n d r -> (n d r)").rearrange("(a b) -> a b", a=1)),
                waits=[("sRS", S.v("sRS"))], inc="sGD", n=16)

        barrier = [("sDVE", S.v("sDVE")), ("sACT", S.v("sACT")),
                   ("sPE", S.v("sPE")), ("sGD", S.v("sGD")),
                   ("sDMA", S.v("sDMA")), ("sPOOL", S.v("sPOOL")),
                   ("sLD", S.v("sLD")), ("sRB", S.v("sRB"))]

        if do_ffn:
            wd_slab_gd = {}
            slab_sl_free = {0: 0, 1: 0}

            def emit_gu_slab_pair(nat, tout, q, lane_w):
                w0 = list(lane_w)
                if slab_sl_free[0]:
                    w0.append(("sSL", slab_sl_free[0]))
                w1 = [("sSL", slab_sl_free[1])] if slab_sl_free[1] else []
                xl = op("sync", lambda s, nat=nat, q=q: s.dma_start_transpose(
                    g_slab[0][:, 0:NF], nat[:, 128 * q:128 * q + 128]),
                    waits=w0, inc="sXB0", n=16)
                xh = op("sync", lambda s, nat=nat, q=q: s.dma_start_transpose(
                    g_slab[1][:, 0:NF], nat[:, 1024 + 128 * q:1024 + 128 * q + 128]),
                    waits=w1, inc="sXB1", n=16)
                for ti in range(2):
                    ov = tout[256 * q:256 * q + 256, :].rearrange(
                        "(a t) n -> t a n", t=2)[ti:ti + 1].rearrange(
                        "o a n -> (o a) n")
                    slab_sl_free[ti] = op(
                        "sync", lambda s, ov=ov, ti=ti: s.dma_start(
                            out=ov, in_=g_slab[ti][:, 0:NF]),
                        waits=[(["sXB0", "sXB1"][ti], [xl, xh][ti])],
                        inc="sSL", n=16)

            slab_sched = {
                128: [(wg_nm, wgT, q) for q in range(4)],
                256: [(wg_nm, wgT, q) for q in range(4, 8)],
                384: [(wu_nm, wuT, q) for q in range(4)],
                512: [(wu_nm, wuT, q) for q in range(4, 8)],
            }

            def dq_boundary(ncomp):
                for (nat, tout, q) in slab_sched.pop(ncomp, []):
                    lane_w = [(f"sQ{l}", 48 * (ncomp // NW)) for l in range(NW)]
                    emit_gu_slab_pair(nat, tout, q, lane_w)

            def emit_d_slabs(grp):
                for j in range(8):
                    qg = 8 * grp + j
                    w_ = [("sGD", S.v("sGD"))]
                    if j == 0:
                        w_ += [(f"sQ{i}", S.v(f"sQ{i}")) for i in range(8)]
                    xl = op("scalar", lambda s, qg=qg: s.dma_start_transpose(
                        f_ds[0][:], wd_nm[:, 128 * qg:128 * qg + 128]),
                        waits=w_, inc="sXB0", n=16)
                    xh = op("scalar", lambda s, qg=qg: s.dma_start_transpose(
                        f_ds[1][:], wd_nm[:, 8192 + 128 * qg:8192 + 128 * qg + 128]),
                        inc="sXB1", n=16)
                    for ti in range(2):
                        ov = wdT[256 * qg:256 * qg + 256, :].rearrange(
                            "(a t) d -> t a d", t=2)[ti:ti + 1].rearrange(
                            "o a d -> (o a) d")
                        op("gpsimd", lambda g, ov=ov, ti=ti: g.dma_start(
                            out=ov, in_=f_ds[ti][:]),
                            waits=[(["sXB0", "sXB1"][ti], [xl, xh][ti])],
                            inc="sGD", n=16)
                wd_slab_gd[grp] = S.v("sGD")

            emit_dq_wave_phase(barrier, dq_boundary)
            if dq_on:
                assert not slab_sched, f"unemitted gu slabs: {list(slab_sched)}"
            else:
                for B in sorted(slab_sched):
                    for (nat, tout, q) in slab_sched.pop(B):
                        emit_gu_slab_pair(nat, tout, q, list(barrier))
            barrier = barrier + [(f"sQ{i}", S.v(f"sQ{i}")) for i in range(8)]
            emit_d_slabs(0)
            emit_ffn(S, dict(
                xloc=xloc, myattn=myattn, ynbuf=ynbuf, out=out,
                wgT=wgT, wuT=wuT, wdT=wdT,
                f_acc=f_acc, f_ynT=f_ynT, f_int=f_int, f_wp=f_wp, f_wd=f_wd,
                f_gel=f_gel, f_ay=f_ay, f_st=f_st, f_mv=f_mv, f_rs=f_rs,
                epsT=epsT, ps_a=ps_a, ps_b=ps_b, ps_f=ps_f, ps_s=ps_s,
                n_rs=S.v("sRS"), full_run=full_run,
                emit_d_slabs=emit_d_slabs, wd_slab_gd=wd_slab_gd,
                dq_barrier=[("sGD", S.v("sGD")), ("sDMA", S.v("sDMA")),
                            ("sXB0", S.v("sXB0")), ("sXB1", S.v("sXB1")),
                            ("sSL", S.v("sSL"))]))
        op("gpsimd", lambda g: g.engine_nop(),
           waits=[("sST", S.v("sST")), ("sGD", S.v("sGD")),
                  ("sDMA", S.v("sDMA")), ("sDVE", S.v("sDVE")),
                  ("sACT", S.v("sACT")), ("sPE", S.v("sPE")),
                  ("sRS", S.v("sRS")), ("sCOLL", S.v("sCOLL")),
                  ("sLD", S.v("sLD")), ("sRB", S.v("sRB")),
                  ("sXB0", S.v("sXB0")), ("sXB1", S.v("sXB1")),
                  ("sSL", S.v("sSL"))]
                 + [(f"sQ{i}", S.v(f"sQ{i}")) for i in range(8)])

        @blk.sync
        def _(e):
            S.replay("sync", e)

        @blk.vector
        def _(e):
            S.replay("vector", e)

        @blk.scalar
        def _(e):
            S.replay("scalar", e)

        @blk.tensor
        def _(e):
            S.replay("tensor", e)

        @blk.gpsimd
        def _(e):
            S.replay("gpsimd", e)

    return nc


def emit_ffn(S, E):
    op = S.op
    f_acc, f_ynT, f_int = E["f_acc"], E["f_ynT"], E["f_int"]
    f_wp, f_wd = E["f_wp"], E["f_wd"]
    f_gel, f_ay = E["f_gel"], E["f_ay"]
    f_st, f_mv, f_rs = E["f_st"], E["f_mv"], E["f_rs"]
    ps_a, ps_b, ps_f, ps_s = E["ps_a"], E["ps_b"], E["ps_f"], E["ps_s"]
    xloc, myattn, ynbuf, out = E["xloc"], E["myattn"], E["ynbuf"], E["out"]
    wgT, wuT, wdT = E["wgT"], E["wuT"], E["wdT"]
    dqb = list(E["dq_barrier"])
    wgTv = wgT.rearrange("(a p) n -> p a n", p=128)
    wuTv = wuT.rearrange("(a p) n -> p a n", p=128)
    wdTv = wdT.rearrange("(a p) h -> p a h", p=128)
    emit_d_slabs = E["emit_d_slabs"]
    wd_slab_gd = E["wd_slab_gd"]

    wp_free = [0, 0, 0]
    wd_freepe = [0, 0]
    st_prev = 0
    gate_ev = {}
    up_ev = {}
    dn_ev = {}
    yn_gd = 0

    for t4 in range(NTILE):
        for cc in range(8):
            ch = 8 * t4 + cc
            wat = list(dqb) if (t4 == 0 and cc == 0) else []
            if E["full_run"]:
                wat.append(("sRS", min(ch + 1, E["n_rs"])))
            wat.append(("sDVE", S.v("sDVE")))
            wat.append(("sGD", yn_gd))
            xa = op("sync", lambda s, ch=ch: s.dma_start_transpose(
                f_ay[:], myattn[ch]), waits=wat, inc="sDMA", n=16)
            xl = op("sync", lambda s, ch=ch, cc=cc: s.dma_start(
                out=f_acc[:, cc, :], in_=xloc[128 * ch:128 * (ch + 1), :]),
                waits=[("sST", st_prev)] if (t4 > 0 and cc == 0) else [],
                inc="sDMA", n=16)
            ya = op("vector", lambda v, cc=cc: v.tensor_add(
                f_acc[:, cc, :], f_acc[:, cc, :], f_ay[:]),
                waits=[("sDMA", S.v("sDMA"))], inc="sDVE")
            last = 0
            for sgm in range(4):
                last = op("vector", lambda v, cc=cc, sgm=sgm: v.bn_stats(
                    f_st[:, 2 * sgm:2 * sgm + 2, :].rearrange("p a b -> p (a b)"),
                    f_acc[:, cc, 512 * sgm:512 * (sgm + 1)]),
                    waits=[("sDVE", ya)] if sgm == 0 else (), inc="sDVE")
            ag = op("vector", lambda v: v.bn_aggr(
                f_mv[:], f_st[:].rearrange("p a b -> p (a b)")),
                waits=[("sDVE", last)], inc="sDVE")
            sq = op("scalar", lambda s: s.activation(
                f_rs[:], f_mv[:, 1:2], AF.Sqrt, bias=E["epsT"][:], scale=1.0),
                waits=[("sDVE", ag)], inc="sACT")
            rc_ = op("vector", lambda v: v.reciprocal(f_rs[:], f_rs[:]),
                     waits=[("sACT", sq)], inc="sDVE")
            nm = op("vector", lambda v, cc=cc: v.tensor_scalar(
                f_ay[:], f_acc[:, cc, :], f_mv[:, 0:1], f_rs[:],
                ALU.subtract, ALU.mult), waits=[("sDVE", rc_)], inc="sDVE")
            yn_gd = op("gpsimd", lambda g, cc=cc: g.dma_start(
                out=ynbuf[128 * cc:128 * (cc + 1), :], in_=f_ay[:]),
                waits=[("sDVE", nm)], inc="sGD", n=16)
        ynT_d = 0
        for dc in range(DC):
            w = [("sGD", yn_gd)]
            if t4 > 0 and dc == 0:
                w.append(("sPE", S.v("sPE")))
            ynT_d = op("sync", lambda s, dc=dc: s.dma_start_transpose(
                f_ynT[:, dc, :], ynbuf[:, 128 * dc:128 * (dc + 1)]),
                waits=w if dc == 0 else [("sGD", yn_gd)], inc="sDMA", n=16)

        for g in range(NGRP):
            iv_last = 0
            for sb in range(NSUB):
                n0 = 2048 * g + NGSZ * sb
                bg = (2 * sb) % 3
                bu = (2 * sb + 1) % 3
                lg = op("sync", lambda s, n0=n0, bg=bg: s.dma_start(
                    out=f_wp[bg][:], in_=wgTv[:, :, n0:n0 + NGSZ]),
                    waits=[("sPE", wp_free[bg])], inc="sDMA", n=16)
                lu = op("sync", lambda s, n0=n0, bu=bu: s.dma_start(
                    out=f_wp[bu][:], in_=wuTv[:, :, n0:n0 + NGSZ]),
                    waits=[("sPE", wp_free[bu])], inc="sDMA", n=16)
                for ic in range(2):
                    for tj in range(TT // 512):
                        rot = tj % 2
                        for kc in range(DC):
                            w = []
                            if kc == 0:
                                w = [("sDMA", max(lu, ynT_d))]
                                if gate_ev.get(rot):
                                    w.append(("sACT", gate_ev[rot]))
                            op("tensor", lambda t, ic=ic, tj=tj, kc=kc, rot=rot, bg=bg: t.matmul(
                                ps_a[:, rot, :],
                                f_wp[bg][:, kc, 128 * ic:128 * (ic + 1)],
                                f_ynT[:, kc, 512 * tj:512 * (tj + 1)],
                                start=(kc == 0), stop=(kc == DC - 1)),
                                waits=w, inc="sPE")
                        peg = S.v("sPE")
                        gate_ev[rot] = op("scalar", lambda s, rot=rot: s.activation(
                            f_gel[:], ps_a[:, rot, :], AF.Gelu_apprx_tanh),
                            waits=[("sPE", peg)], inc="sACT")
                        for kc in range(DC):
                            w = []
                            if kc == 0:
                                w = [("sDMA", lu)]
                                if up_ev.get(rot):
                                    w.append(("sDVE", up_ev[rot]))
                            op("tensor", lambda t, ic=ic, tj=tj, kc=kc, rot=rot, bu=bu: t.matmul(
                                ps_b[:, rot, :],
                                f_wp[bu][:, kc, 128 * ic:128 * (ic + 1)],
                                f_ynT[:, kc, 512 * tj:512 * (tj + 1)],
                                start=(kc == 0), stop=(kc == DC - 1)),
                                waits=w, inc="sPE")
                        peu = S.v("sPE")
                        iv_last = op("vector", lambda v, sb=sb, ic=ic, tj=tj, rot=rot: v.tensor_mul(
                            f_int[:, 2 * sb + ic, 512 * tj:512 * (tj + 1)],
                            f_gel[:], ps_b[:, rot, :]),
                            waits=[("sACT", gate_ev[rot]), ("sPE", peu)], inc="sDVE")
                        up_ev[rot] = iv_last
                wp_free[bg] = S.v("sPE")
                wp_free[bu] = S.v("sPE")
            if t4 == 0 and g + 1 < NGRP:
                emit_d_slabs(g + 1)
            for hh in range(2):
                for hq in range(2):
                    qi_ = 2 * hh + hq
                    wb = qi_ % 2
                    wl = op("gpsimd", lambda gg, g=g, hh=hh, hq=hq, wb=wb: gg.dma_start(
                        out=f_wd[wb][:],
                        in_=wdTv[:, 16 * g:16 * g + 16,
                                 1024 * hh + 512 * hq:1024 * hh + 512 * (hq + 1)]),
                        waits=[("sPE", wd_freepe[wb]), ("sGD", wd_slab_gd[g])],
                        inc="sGD", n=16)
                    for tc in range(TT // 128):
                        rot = tc % 2
                        pst = [ps_f, ps_s][rot]
                        for kc in range(DC):
                            w = []
                            if kc == 0:
                                w = [("sGD", wl)]
                                if qi_ == 0 and tc == 0:
                                    w.append(("sDVE", iv_last))
                                if dn_ev.get(rot):
                                    w.append(("sDVE", dn_ev[rot]))
                            op("tensor", lambda t, tc=tc, kc=kc, pst=pst, wb=wb: t.matmul(
                                pst[:], f_int[:, kc, 128 * tc:128 * (tc + 1)],
                                f_wd[wb][:, kc, :],
                                start=(kc == 0), stop=(kc == DC - 1)),
                                waits=w, inc="sPE")
                        ped = S.v("sPE")
                        dn_ev[rot] = op("vector", lambda v, hh=hh, hq=hq, tc=tc, pst=pst: v.tensor_add(
                            f_acc[:, tc, 1024 * hh + 512 * hq:1024 * hh + 512 * (hq + 1)],
                            f_acc[:, tc, 1024 * hh + 512 * hq:1024 * hh + 512 * (hq + 1)],
                            pst[:]), waits=[("sPE", ped)], inc="sDVE")
                    wd_freepe[wb] = S.v("sPE")
        st_prev = op("gpsimd", lambda g, t4=t4: g.dma_start(
            out=out[TT * t4:TT * (t4 + 1), :].rearrange("(a p) d -> p a d", p=128),
            in_=f_acc[:]), waits=[("sDVE", S.v("sDVE"))], inc="sST", n=16)


def make_inmaps(inp):
    x = np.ascontiguousarray(np.asarray(inp["L0_attn_x"], np.float32))
    W0 = np.asarray(inp["L0_attn_W0"], np.float32)
    W1 = np.asarray(inp["L0_attn_W1"], np.float32)
    b0 = np.asarray(inp["L0_attn_b0"], np.float32)
    b1 = np.asarray(inp["L0_attn_b1"], np.float32)
    lr_w = np.asarray(inp["L0_attn_lr_w"], np.float32)
    fg_w = np.asarray(inp["L0_attn_fg_w"], np.float32)
    lrfg = np.ascontiguousarray(np.stack([lr_w[0], fg_w[0]], axis=1))
    xflat = x.reshape(4 * 8192, D)
    maps = []
    for k in range(8):
        mk = slice(MK * k, MK * (k + 1))
        maps.append({
            "xfull": x,
            "xloc": np.ascontiguousarray(xflat[TLOC * k:TLOC * (k + 1)]),
            "w0kT": np.ascontiguousarray(W0[mk].T),
            "w1dm": np.ascontiguousarray(W1[:, mk]),
            "w1kT": np.ascontiguousarray(W1[:, mk].T),
            "b0k": np.ascontiguousarray(b0[mk].reshape(2, 128).T),
            "b1f": np.ascontiguousarray(b1.reshape(DC, 128).T),
            "lrfg": np.ascontiguousarray(lrfg.reshape(DC, 128, 2).transpose(1, 0, 2)),
            "lrb": np.asarray(inp["L0_attn_lr_b"], np.float32),
            "fgb": np.asarray(inp["L0_attn_fg_b"], np.float32),
            "gwp": np.asarray(inp["L0_ffn_gate_W_packed"], np.int32),
            "gsc": np.asarray(inp["L0_ffn_gate_scales"], np.float32),
            "uwp": np.asarray(inp["L0_ffn_up_W_packed"], np.int32),
            "usc": np.asarray(inp["L0_ffn_up_scales"], np.float32),
            "dwp": np.asarray(inp["L0_ffn_down_W_packed"], np.int32),
            "dsc": np.asarray(inp["L0_ffn_down_scales"], np.float32),
        })
    return maps


def assemble(results):
    return np.concatenate([r["out"] for r in results], axis=0).reshape(4, 8192, D)


def kernel(**inputs):
    maps = make_inmaps(inputs)
    nc = build()
    res = run_bass_kernel_spmd(nc, maps, core_ids=list(range(8)))
    return assemble(res.results)


# version 04
# speedup vs baseline: 1.2308x; 1.0021x over previous
from concourse.bass_utils import run_bass_kernel_spmd

import numpy as np
import concourse.bass as bass
import concourse.mybir as mybir

F32 = mybir.dt.float32
BF16 = mybir.dt.bfloat16
I32 = mybir.dt.int32
AF = mybir.ActivationFunctionType
ALU = mybir.AluOpType
AX = mybir.AxisListType
RG8 = [[0, 1, 2, 3, 4, 5, 6, 7]]

D = 2048
MK = 256
RW = 512
DC = D // 128
RC = RW // 128
NCH = 64
NF = 16384
TLOC = 4096
TT = 1024
NTILE = TLOC // TT
NGRP = 8
NSUB = 8
NGSZ = 256
TWO_N = 2.0 / (RW * D)


class Sched:
    def __init__(self, nc):
        self.nc = nc
        self.items = []
        self.cnt = {}
        self.sems = {}

    def sem(self, ctx, name):
        h = ctx.enter_context(self.nc.semaphore(name))
        self.sems[name] = h
        self.cnt[name] = 0

    def op(self, eng, fn, waits=(), inc=None, n=1):
        v = None
        if inc is not None:
            self.cnt[inc] += n
            v = self.cnt[inc]
        self.items.append((eng, fn, tuple(waits), inc, n))
        return v

    def v(self, name):
        return self.cnt[name]

    def replay(self, ename, eng):
        for (e, fn, waits, inc, n) in self.items:
            if e != ename:
                continue
            for (s, val) in waits:
                if val and val > 0:
                    eng.wait_ge(self.sems[s], val)
            inst = fn(eng)
            if inc is not None:
                inst.then_inc(self.sems[inc], n)


def build(n_steps=NCH, do_ffn=True, dbg=None, force_rs=False, dq_on=True):
    dbg = dbg or {}
    nc = bass.Bass("TRN2", target_bir_lowering=False)
    full_run = (n_steps == NCH) or force_rs

    xfull = nc.dram_tensor("xfull", [4, 8192, D], F32, kind="ExternalInput").ap()
    xloc = nc.dram_tensor("xloc", [TLOC, D], F32, kind="ExternalInput").ap()
    w0kTd = nc.dram_tensor("w0kT", [D, MK], F32, kind="ExternalInput").ap()
    w1dmd = nc.dram_tensor("w1dm", [D, MK], F32, kind="ExternalInput").ap()
    w1kTd = nc.dram_tensor("w1kT", [MK, D], F32, kind="ExternalInput").ap()
    b0kd = nc.dram_tensor("b0k", [128, 2], F32, kind="ExternalInput").ap()
    b1d = nc.dram_tensor("b1f", [128, DC], F32, kind="ExternalInput").ap()
    lrfgd = nc.dram_tensor("lrfg", [128, DC, 2], F32, kind="ExternalInput").ap()
    lrbd = nc.dram_tensor("lrb", [1], F32, kind="ExternalInput").ap()
    fgbd = nc.dram_tensor("fgb", [1], F32, kind="ExternalInput").ap()
    gwp = nc.dram_tensor("gwp", [NF, D // 2], I32, kind="ExternalInput").ap()
    gsc = nc.dram_tensor("gsc", [NF, D // 32], F32, kind="ExternalInput").ap()
    uwp = nc.dram_tensor("uwp", [NF, D // 2], I32, kind="ExternalInput").ap()
    usc = nc.dram_tensor("usc", [NF, D // 32], F32, kind="ExternalInput").ap()
    dwp = nc.dram_tensor("dwp", [D, NF // 2], I32, kind="ExternalInput").ap()
    dsc = nc.dram_tensor("dsc", [D, NF // 32], F32, kind="ExternalInput").ap()
    out = nc.dram_tensor("out", [TLOC, D], F32, kind="ExternalOutput").ap()

    car_in = nc.dram_tensor("car_in", [D, RW], BF16, kind="Internal").ap()
    car_out = nc.dram_tensor("car_out", [D, RW], BF16, kind="Internal").ap()
    npair = 32
    pairbuf = nc.dram_tensor("pairbuf", [npair, 8, D, 128], BF16, kind="Internal").ap()
    myattn = nc.dram_tensor("myattn", [npair, D, 128], BF16, kind="Internal").ap()
    wg_nm = nc.dram_tensor("wg_nm", [NF, D], BF16, kind="Internal").ap()
    wu_nm = nc.dram_tensor("wu_nm", [NF, D], BF16, kind="Internal").ap()
    wd_nm = nc.dram_tensor("wd_nm", [D, NF], BF16, kind="Internal").ap()
    wgT = nc.dram_tensor("wgT", [D, NF], BF16, kind="Internal").ap()
    wuT = nc.dram_tensor("wuT", [D, NF], BF16, kind="Internal").ap()
    wdT = nc.dram_tensor("wdT", [NF, D], BF16, kind="Internal").ap()
    ynbuf = nc.dram_tensor("ynbuf", [TT, D], BF16, kind="Internal").ap()

    dbg_outs = {}
    if dbg.get("p2st"):
        dbg_outs["d_p2st"] = nc.dram_tensor("d_p2st", [n_steps, D, RW], BF16,
                                            kind="ExternalOutput").ap()
    if dbg.get("pred"):
        dbg_outs["d_pred"] = nc.dram_tensor("d_pred", [n_steps, D, RW], BF16,
                                            kind="ExternalOutput").ap()
    if dbg.get("myattn"):
        dbg_outs["d_myattn"] = nc.dram_tensor("d_myattn", [npair, D, 128], BF16,
                                              kind="ExternalOutput").ap()
    if dbg.get("wrow"):
        dbg_outs["d_wrow"] = nc.dram_tensor("d_wrow", [3, 128, D], BF16,
                                            kind="ExternalOutput").ap()

    SB_BASE = 16768
    SB_LIMIT = 229344
    off = [SB_BASE]

    def salloc(name, shape, dtype, align=64):
        nbytes = int(np.prod(shape[1:])) * mybir.dt.size(dtype)
        off[0] = (off[0] + align - 1) // align * align
        t = nc.alloc_sbuf_tensor_at(name, list(shape), dtype, offset=off[0])
        off[0] += nbytes
        assert off[0] <= SB_LIMIT, f"SBUF overflow at {name}: {off[0]}"
        return t

    ident = salloc("ident", [128, 128], BF16)
    kones = salloc("kones", [1, 128], F32)
    epsT = salloc("epsT", [128, 1], F32)
    bq_sg = salloc("bq_sg", [128, 1], F32)
    onec = salloc("onec", [128, 1], BF16)
    bq_th = salloc("bq_th", [128, 1], F32)
    bq_bb = salloc("bq_bb", [128, 1], F32)
    bq_cc = salloc("bq_cc", [128, 1], F32)
    PERSIST_END = off[0]

    c_w0 = salloc("c_w0", [128, DC, MK], BF16)
    c_w1 = salloc("c_w1", [128, DC, MK + 1], BF16)
    c_w1T = salloc("c_w1T", [128, 2, D], BF16)
    c_b0 = salloc("c_b0", [128, 2, 1], F32)
    c_lrfg = salloc("c_lrfg", [128, DC, 2], BF16)
    c_lrb = salloc("c_lrb", [1, 1], F32)
    c_fgb = salloc("c_fgb", [1, 1], F32)
    c_fv = salloc("c_fv", [128, 1], F32)
    c_nl = salloc("c_nl", [128, 1], F32)
    c_b18 = salloc("c_b18", [128, DC, 1], BF16)
    c_xraw = salloc("c_xraw", [128, 2, D], F32)
    c_xn = [salloc(f"c_xn{i}", [128, RC, D], BF16) for i in range(2)]
    c_xnT = [salloc(f"c_xnT{i}", [128, DC, RW], BF16) for i in range(2)]
    c_h1T = salloc("c_h1T", [128, 2, RW], BF16)
    c_h2T = salloc("c_h2T", [128, 2, RW], BF16)
    c_h1r = salloc("c_h1r", [128, RC, MK + 1], BF16)
    c_sgT = salloc("c_sgT", [128, 2, RW], BF16)
    c_sgr = salloc("c_sgr", [128, RC, MK], BF16)
    c_dpre = salloc("c_dpre", [128, RC, MK], BF16)
    c_pT = salloc("c_pT", [128, DC, RW], BF16)
    c_pr = salloc("c_pr", [128, RC, D], BF16)
    c_par = salloc("c_par", [128, DC, RW], BF16)
    c_p2 = salloc("c_p2", [128, DC, RW], BF16)
    c_st = salloc("c_st", [128, 2, 8, 3], F32)
    c_mv = salloc("c_mv", [128, RC, 2], F32)
    c_rs = salloc("c_rs", [128, RC, 1], F32)
    c_sc = salloc("c_sc", [2, RW], F32)
    c_u4 = salloc("c_u4", [1, 8], F32)
    c_fl = salloc("c_fl", [1, 2], F32)
    TTT_END = off[0]
    assert TTT_END <= SB_LIMIT

    off[0] = PERSIST_END
    f_acc = salloc("f_acc", [128, TT // 128, D], F32)
    f_ynT = salloc("f_ynT", [128, DC, TT], BF16)
    f_int = salloc("f_int", [128, DC, TT], BF16)
    f_wp = [salloc(f"f_wp{i}", [128, DC, NGSZ], BF16) for i in range(3)]
    f_wd = [salloc(f"f_wd{i}", [128, DC, 512], BF16) for i in range(2)]
    f_ds = [salloc(f"f_ds{i}", [128, D], BF16) for i in range(2)]
    f_gel = salloc("f_gel", [128, RW], BF16)
    f_ay = salloc("f_ay", [128, D], BF16)
    f_st = salloc("f_st", [128, 8, 3], F32)
    f_mv = salloc("f_mv", [128, 2], F32)
    f_rs = salloc("f_rs", [128, 1], F32)
    FFN_END = off[0]
    assert FFN_END <= SB_LIMIT, FFN_END
    off[0] = PERSIST_END
    g_slab = [salloc(f"g_slab{i}", [128, NF], BF16) for i in range(2)]
    assert off[0] <= SB_LIMIT
    NW = 8
    q_pk, q_sc, q_wb, q_cd, q_s, q_m, q_a, q_b, q_out = \
        [], [], [], [], [], [], [], [], []
    for i in range(NW):
        q_pk.append(salloc(f"q_pk{i}", [128, 512], I32))
        q_sc.append(salloc(f"q_sc{i}", [128, 32], F32))
        q_wb.append(salloc(f"q_wb{i}", [128, 512], F32))
        q_cd.append(salloc(f"q_cd{i}", [128, 1024], BF16))
        q_s.append(salloc(f"q_s{i}", [128, 1024], BF16))
        q_m.append(salloc(f"q_m{i}", [128, 1024], BF16))
        q_a.append(salloc(f"q_a{i}", [128, 1024], BF16))
        q_b.append(salloc(f"q_b{i}", [128, 1024], BF16))
        q_out.append(salloc(f"q_out{i}", [128, 1024], BF16))
    assert off[0] <= SB_LIMIT, off[0]

    from contextlib import ExitStack
    with ExitStack() as ctx:
        S = Sched(nc)
        for name in ["sPE", "sACT", "sDVE", "sPOOL", "sDMA", "sGD",
                     "sCOLL", "sRS", "sST", "sLD", "sRB", "sXB0", "sXB1", "sSL",
                     "sQ0", "sQ1", "sQ2", "sQ3", "sQ4", "sQ5", "sQ6", "sQ7"]:
            S.sem(ctx, name)
        ps_a = ctx.enter_context(nc.psum_tensor("ps_a", [128, 2, RW], F32))
        ps_b = ctx.enter_context(nc.psum_tensor("ps_b", [128, 2, RW], F32))
        ps_d0 = ctx.enter_context(nc.psum_tensor("ps_d0", [128, 4, 128], BF16))
        ps_d1 = ctx.enter_context(nc.psum_tensor("ps_d1", [128, 4, 128], BF16))
        ps_s = ctx.enter_context(nc.psum_tensor("ps_s", [128, RW], F32))
        ps_f = ctx.enter_context(nc.psum_tensor("ps_f", [128, RW], F32))
        psd = [ps_d0, ps_d1]
        blk = ctx.enter_context(nc.Block())
        op = S.op

        op("gpsimd", lambda g: g.memset(ident[:], 0.0), inc="sPOOL")
        op("gpsimd", lambda g: g.affine_select(
            out=ident[:], in_=ident[:], compare_op=ALU.not_equal, fill=1.0,
            base=0, pattern=[[-1, 128]], channel_multiplier=1),
           waits=[("sPOOL", S.v("sPOOL"))], inc="sPOOL")
        op("gpsimd", lambda g: g.memset(kones[:], 1.0), inc="sPOOL")
        op("gpsimd", lambda g: g.memset(epsT[:], 1e-5), inc="sPOOL")
        op("gpsimd", lambda g: g.memset(bq_sg[:], 143.5), inc="sPOOL")
        op("gpsimd", lambda g: g.memset(onec[:], 1.0), inc="sPOOL")
        op("gpsimd", lambda g: g.memset(bq_th[:], -8.96875), inc="sPOOL")
        op("gpsimd", lambda g: g.memset(bq_bb[:], -142.0), inc="sPOOL")
        op("gpsimd", lambda g: g.memset(bq_cc[:], -288.0), inc="sPOOL")
        IDENT_POOL = S.v("sPOOL")

        scrf = c_xraw[:].rearrange("p a d -> p (a d)")

        def stage(view, src, dst, prev_dve):
            d_ = op("sync", lambda s, view=view, src=src: s.dma_start(out=view, in_=src),
                    waits=[("sDVE", prev_dve)], inc="sDMA", n=16)
            return op("vector", lambda v, dst=dst, view=view: v.tensor_copy(dst, view),
                      waits=[("sDMA", d_)], inc="sDVE")

        cv = 0
        v16 = scrf[:, 0:DC * MK].rearrange("p (a m) -> p a m", m=MK)
        cv = stage(v16, w0kTd.rearrange("(a p) m -> p a m", p=128), c_w0[:], cv)
        cv = stage(v16, w1dmd.rearrange("(a p) m -> p a m", p=128), c_w1[:, :, 0:MK], cv)
        vb1 = scrf[:, 0:DC].rearrange("p (a o) -> p a o", o=1)
        cv = stage(vb1, b1d.rearrange("p (a o) -> p a o", o=1), c_w1[:, :, MK:MK + 1], cv)
        cv = op("vector", lambda v: v.tensor_scalar_mul(
            c_b18[:], c_w1[:, :, MK:MK + 1], 0.125), waits=[("sDVE", cv)], inc="sDVE")
        v2d = scrf[:, 0:2 * D].rearrange("p (a d) -> p a d", d=D)
        cv = stage(v2d, w1kTd.rearrange("(a p) d -> p a d", p=128), c_w1T[:], cv)
        vb0 = scrf[:, 0:2].rearrange("p (a o) -> p a o", o=1)
        cv = stage(vb0, b0kd.rearrange("p (a o) -> p a o", o=1), c_b0[:], cv)
        vlf = scrf[:, 0:2 * DC].rearrange("p (a t) -> p a t", t=2)
        cv = stage(vlf, lrfgd, c_lrfg[:], cv)
        dsm = op("sync", lambda s: s.dma_start(out=c_lrb[:], in_=lrbd.rearrange("(a o) -> a o", a=1)),
                 waits=[("sDVE", cv)], inc="sDMA", n=16)
        dsm = op("sync", lambda s: s.dma_start(out=c_fgb[:], in_=fgbd.rearrange("(a o) -> a o", a=1)),
                 waits=[("sDMA", dsm)], inc="sDMA", n=16)
        P0_DVE = cv
        P0_DMA = S.v("sDMA")

        dq_jobs = []
        if do_ffn and dq_on:
            for (pk, sc, nat, rows, K) in [(gwp, gsc, wg_nm, NF, D),
                                           (uwp, usc, wu_nm, NF, D),
                                           (dwp, dsc, wd_nm, D, NF)]:
                for k0 in range(0, K, 1024):
                    for r0 in range(0, rows, 128):
                        dq_jobs.append((pk, sc, nat, r0, k0))

        def dq_stages(j, barrier_w, last, wr_last):
            (pk, sc, nat, r0, k0) = dq_jobs[j]
            st = j % NW
            qsem = f"sQ{st}"
            cd_hi = q_cd[st][:, 512:1024]
            cd_lo = q_cd[st][:, 0:512]
            o = {}

            def s0():
                w0 = list(barrier_w) if last is None else [("sDVE", last[0]), (qsem, last[1])]
                o["dl"] = op("scalar", lambda s: s.dma_start(
                    out=q_pk[st][:], in_=pk[r0:r0 + 128, k0 // 2:k0 // 2 + 512]),
                    waits=w0, inc=qsem, n=16)
                o["ds"] = op("scalar", lambda s: s.dma_start(
                    out=q_sc[st][:], in_=sc[r0:r0 + 128, k0 // 32:k0 // 32 + 32]),
                    inc=qsem, n=16)

            def s1():
                o["wb"] = op("vector", lambda v: v.tensor_scalar(
                    q_wb[st][:], q_pk[st][:], 2312.0, 0.0, ALU.add, ALU.bypass),
                    waits=[(qsem, o["ds"])], inc="sDVE")

            def s2():
                o["th"] = op("scalar", lambda s: s.activation(
                    cd_hi, q_wb[st][:], AF.Identity, bias=bq_th[:], scale=0.0625),
                    waits=[("sDVE", o["wb"])], inc="sACT")

            def s3():
                o["tl"] = op("vector", lambda v: v.scalar_tensor_tensor(
                    cd_lo, cd_hi, -16.0, q_wb[st][:], ALU.mult, ALU.add),
                    waits=[("sACT", o["th"])], inc="sDVE")

            def s4():
                o["sg"] = op("scalar", lambda s: s.activation(
                    q_s[st][:], q_cd[st][:], AF.Sign, bias=bq_sg[:], scale=-1.0),
                    waits=[("sDVE", o["tl"])], inc="sACT")

            def s5():
                o["mm"] = op("vector", lambda v: v.scalar_tensor_tensor(
                    q_m[st][:], q_s[st][:], 4.0, q_cd[st][:], ALU.mult, ALU.add),
                    waits=[("sACT", o["sg"])], inc="sDVE")

            def s6():
                o["aa"] = op("gpsimd", lambda g: g.tensor_scalar(
                    q_a[st][:], q_m[st][:], 0.5, -70.0, ALU.mult, ALU.add),
                    waits=[("sDVE", o["mm"])], inc="sPOOL")
                o["bb"] = op("scalar", lambda s: s.activation(
                    q_b[st][:], q_m[st][:], AF.Identity, bias=bq_bb[:], scale=1.0),
                    waits=[("sDVE", o["mm"])], inc="sACT")

            def s7():
                o["m1"] = op("vector", lambda v: v.tensor_max(
                    q_a[st][:], q_a[st][:], q_b[st][:]),
                    waits=[("sPOOL", o["aa"]), ("sACT", o["bb"])], inc="sDVE")

            def s8():
                o["cc"] = op("scalar", lambda s: s.activation(
                    q_b[st][:], q_m[st][:], AF.Identity, bias=bq_cc[:], scale=2.0),
                    waits=[("sDVE", o["m1"])], inc="sACT")

            def s9():
                o["m2"] = op("vector", lambda v: v.tensor_max(
                    q_a[st][:], q_a[st][:], q_b[st][:]),
                    waits=[("sACT", o["cc"])], inc="sDVE")

            def s10():
                o["vv"] = op("vector", lambda v: v.tensor_mul(
                    q_a[st][:], q_a[st][:], q_s[st][:]),
                    waits=[("sDVE", o["m2"])], inc="sDVE")

            def s11():
                scb = q_sc[st][:].rearrange(
                    "p (g o) -> p g o", o=1).broadcast_to([128, 32, 16])
                op("vector", lambda v, scb=scb: v.tensor_tensor(
                    q_out[st][:, 0:512].rearrange("p (g e) -> p g e", g=32),
                    q_a[st][:, 0:512].rearrange("p (g e) -> p g e", g=32),
                    scb, ALU.mult),
                    waits=[("sDVE", o["vv"]), (qsem, wr_last)], inc="sDVE")
                o["so"] = op("vector", lambda v, scb=scb: v.tensor_tensor(
                    q_out[st][:, 512:1024].rearrange("p (g e) -> p g e", g=32),
                    q_a[st][:, 512:1024].rearrange("p (g e) -> p g e", g=32),
                    scb, ALU.mult), inc="sDVE")

            def s12():
                o["wr"] = op("gpsimd", lambda g: g.dma_start(
                    out=nat[r0:r0 + 128, :].rearrange(
                        "r (t a) -> r t a", t=2)[:, :, k0 // 2:k0 // 2 + 512],
                    in_=q_out[st][:].rearrange("p (t a) -> p t a", t=2)),
                    waits=[("sDVE", o["so"])], inc=qsem, n=16)

            return [s0, s1, s2, s3, s4, s5, s6, s7, s8, s9, s10, s11, s12], o

        def emit_dq_wave_phase(barrier_w, boundary_cb=None):
            NS = 13
            last = [None] * NW
            wr_last = [0] * NW
            lane_stage = [None] * NW
            lane_pos = [0] * NW
            nxt = [l for l in range(NW)]
            ncomp = 0

            def start_lane(l):
                if nxt[l] >= len(dq_jobs):
                    return False
                j = nxt[l]
                nxt[l] += NW
                stages, o = dq_stages(j, barrier_w, last[l], wr_last[l])
                lane_stage[l] = (stages, o, l)
                lane_pos[l] = 0
                return True

            for l in range(NW):
                if not start_lane(l):
                    lane_stage[l] = None
            active = sum(1 for x in lane_stage if x)
            while active:
                for l in range(NW):
                    if lane_stage[l] is None:
                        continue
                    stages, o, st_ = lane_stage[l]
                    stages[lane_pos[l]]()
                    lane_pos[l] += 1
                    if lane_pos[l] == NS:
                        last[st_] = (o["so"], o["wr"])
                        wr_last[st_] = o["wr"]
                        lane_stage[l] = None
                        ncomp += 1
                        if boundary_cb and ncomp % NW == 0:
                            boundary_cb(ncomp)
                        start_lane(l)
                active = sum(1 for x in lane_stage if x)

        psd_evac = [0, 0]

        def emit_prep(c, extra_load_waits=()):
            par = c % 2
            xv = xfull.rearrange("b (n q) d -> n q b d", q=128)[c]
            nrm = 0
            dl = 0
            for hf in range(2):
                w = list(extra_load_waits) if hf == 0 else [("sDVE", nrm)]
                if dl:
                    w.append(("sLD", dl))
                dl = op("sync", lambda s, hf=hf, xv=xv: s.dma_start(
                    out=c_xraw[:], in_=xv[:, 2 * hf:2 * hf + 2, :]),
                    waits=w, inc="sLD", n=16)
                last = 0
                for rc2 in range(2):
                    for sgm in range(4):
                        last = op("vector", lambda v, rc2=rc2, sgm=sgm: v.bn_stats(
                            c_st[:, rc2, 2 * sgm:2 * sgm + 2, :].rearrange("p a b -> p (a b)"),
                            c_xraw[:, rc2, 512 * sgm:512 * (sgm + 1)]),
                            waits=[("sLD", dl)] if (rc2 == 0 and sgm == 0) else (),
                            inc="sDVE")
                for rc2 in range(2):
                    last = op("vector", lambda v, rc2=rc2, hf=hf: v.bn_aggr(
                        c_mv[:, 2 * hf + rc2, :], c_st[:, rc2, :, :].rearrange("p a b -> p (a b)")),
                        waits=[("sDVE", last)] if rc2 == 0 else (), inc="sDVE")
                sq = op("scalar", lambda s, hf=hf: s.activation(
                    c_rs[:, 2 * hf:2 * hf + 2, :].rearrange("p a o -> p (a o)"),
                    c_mv[:, 2 * hf:2 * hf + 2, 1:2].rearrange("p a o -> p (a o)"),
                    AF.Sqrt, bias=epsT[:], scale=1.0),
                    waits=[("sDVE", last), ("sPOOL", IDENT_POOL)], inc="sACT")
                rc_ = op("vector", lambda v, hf=hf: v.reciprocal(
                    c_rs[:, 2 * hf:2 * hf + 2, :].rearrange("p a o -> p (a o)"),
                    c_rs[:, 2 * hf:2 * hf + 2, :].rearrange("p a o -> p (a o)")),
                    waits=[("sACT", sq)], inc="sDVE")
                for rc2 in range(2):
                    nrm = op("vector", lambda v, rc2=rc2, hf=hf, par=par: v.tensor_scalar(
                        c_xn[par][:, 2 * hf + rc2, :], c_xraw[:, rc2, :],
                        c_mv[:, 2 * hf + rc2, 0:1], c_rs[:, 2 * hf + rc2, :],
                        ALU.subtract, ALU.mult),
                        waits=[("sDVE", rc_)] if rc2 == 0 else (), inc="sDVE")
            for dc in range(DC):
                sl = dc % 2
                for rcc in range(RC):
                    w = []
                    if dc == 0 and rcc == 0:
                        w = [("sDVE", nrm), ("sPOOL", IDENT_POOL)]
                    if rcc == 0 and psd_evac[sl]:
                        w.append(("sDVE", psd_evac[sl]))
                    op("tensor", lambda t, dc=dc, rcc=rcc, sl=sl, par=par: t.transpose(
                        psd[sl][:, rcc, :], c_xn[par][:, rcc, 128 * dc:128 * (dc + 1)],
                        ident[:]), waits=w, inc="sPE")
                pe_b = S.v("sPE")
                psd_evac[sl] = op("vector", lambda v, dc=dc, sl=sl, par=par: v.tensor_copy(
                    c_xnT[par][:, dc, :], psd[sl][:].rearrange("p a b -> p (a b)")),
                    waits=[("sPE", pe_b)], inc="sDVE")
            return nrm

        nrm0 = emit_prep(0, extra_load_waits=[("sDVE", P0_DVE)])
        prev_nrm = {0: nrm0}

        psb_free = {0: None, 1: None}
        psa_free = 0
        pending_rs = None
        deferred = None
        st1_prev = 0
        rb_prev = 0

        def emit_deferred(dctx):
            c1 = dctx["c"]
            xnT1 = dctx["xnT"]
            first = True
            for kc in range(DC):
                for ic in range(2):
                    w = []
                    if first:
                        w = [("sDVE", dctx["upd"]), ("sACT", psa_free)]
                        first = False
                    op("tensor", lambda t, kc=kc, ic=ic, xnT1=xnT1: t.matmul(
                        ps_a[:, ic, :], c_w0[:, kc, 128 * ic:128 * (ic + 1)],
                        xnT1[:, kc, :], start=(kc == 0), stop=(kc == DC - 1)),
                        waits=w, inc="sPE")
            pe_fwd2 = S.v("sPE")
            a2 = 0
            for mc in range(2):
                a2 = op("scalar", lambda s, mc=mc: s.activation(
                    c_h2T[:, mc, :], ps_a[:, mc, :], AF.Silu,
                    bias=c_b0[:, mc, :], scale=1.0),
                    waits=[("sPE", pe_fwd2)] if mc == 0 else (), inc="sACT")
            ev_p2 = 0
            ev_hist = {}
            for dc in range(DC):
                for kc in range(2):
                    w = []
                    if dc == 0 and kc == 0:
                        w = [("sACT", a2)]
                    if kc == 0 and dc >= 2:
                        w.append(("sACT", ev_hist[dc - 2]))
                    if kc == 0 and dc < 2 and psb_free[dc % 2]:
                        w.append(psb_free[dc % 2])
                    op("tensor", lambda t, dc=dc, kc=kc: t.matmul(
                        ps_b[:, dc % 2, :], c_w1T[:, kc, 128 * dc:128 * (dc + 1)],
                        c_h2T[:, kc, :], start=(kc == 0), stop=(kc == 1)),
                        waits=w, inc="sPE")
                pe_p2 = S.v("sPE")
                wex = [("sPE", pe_p2)]
                if dc == 0:
                    wex += [("sGD", S.v("sGD"))]
                ev_p2 = op("scalar", lambda s, dc=dc: s.activation(
                    c_p2[:, dc, :], ps_b[:, dc % 2, :], AF.Identity,
                    bias=c_b18[:, dc, :], scale=1.0), waits=wex, inc="sACT")
                ev_hist[dc] = ev_p2
                psb_free[dc % 2] = ("sACT", ev_p2)
            half = 32 if n_steps == NCH else max(1, n_steps // 2)
            h = (c1 // half) if full_run else 0
            p = (c1 % half) if full_run else c1 % npair
            gd = 0
            for b in range(4):
                gd = op("gpsimd", lambda g, b=b, p=p, h=h: g.dma_start(
                    out=pairbuf[p, 2 * b + h].rearrange("(a q) r -> q a r", q=128),
                    in_=c_p2[:, :, 128 * b:128 * (b + 1)]),
                    waits=[("sACT", ev_p2)] if b == 0 else (), inc="sGD", n=16)
            if dbg.get("p2st"):
                op("gpsimd", lambda g, c1=c1: g.dma_start(
                    out=dbg_outs["d_p2st"][c1].rearrange("(a q) r -> q a r", q=128),
                    in_=c_p2[:]), waits=[("sACT", ev_p2)], inc="sGD", n=16)
            rs = None
            if full_run and h == 1:
                rs = (lambda g, p=p: g.collective_compute(
                    "ReduceScatter", ALU.add, RG8,
                    ins=[pairbuf[p].rearrange("s d r -> (s d) r").opt()],
                    outs=[myattn[p].opt()]),
                    [("sGD", S.v("sGD"))])
            return a2, ev_p2, rs

        for c in range(n_steps):
            par = c % 2
            xn = c_xn[par]
            xnT = c_xnT[par]

            w0_ = [("sDVE", S.v("sDVE"))]
            if psa_free:
                w0_.append(("sACT", psa_free))
            if c == 0:
                w0_.append(("sDMA", P0_DMA))
            first = True
            for kc in range(DC):
                for ic in range(2):
                    op("tensor", lambda t, kc=kc, ic=ic, xnT=xnT: t.matmul(
                        ps_a[:, ic, :], c_w0[:, kc, 128 * ic:128 * (ic + 1)],
                        xnT[:, kc, :], start=(kc == 0), stop=(kc == DC - 1)),
                        waits=w0_ if first else (), inc="sPE")
                    first = False
                op("tensor", lambda t, kc=kc, xnT=xnT: t.matmul(
                    ps_s[0:1, :], c_lrfg[:, kc, 0:1], xnT[:, kc, :],
                    start=(kc == 0), stop=(kc == DC - 1)), inc="sPE")
                op("tensor", lambda t, kc=kc, xnT=xnT: t.matmul(
                    ps_s[32:33, :], c_lrfg[:, kc, 1:2], xnT[:, kc, :],
                    start=(kc == 0), stop=(kc == DC - 1)), inc="sPE")
            pe_fwd1 = S.v("sPE")

            act_h1 = 0
            for mc in range(2):
                op("scalar", lambda s, mc=mc: s.activation(
                    c_h1T[:, mc, :], ps_a[:, mc, :], AF.Silu,
                    bias=c_b0[:, mc, :], scale=1.0),
                    waits=[("sPE", pe_fwd1)] if mc == 0 else (), inc="sACT")
                act_h1 = op("scalar", lambda s, mc=mc: s.activation(
                    c_sgT[:, mc, :], ps_a[:, mc, :], AF.Derivative_silu,
                    bias=c_b0[:, mc, :], scale=1.0), inc="sACT")
            psa_free = act_h1
            sv_ = op("scalar", lambda s: s.activation(
                c_sc[0:1, :], ps_s[0:1, :], AF.Sigmoid, bias=c_lrb[:], scale=1.0),
                waits=[("sPE", pe_fwd1)], inc="sACT")
            e1 = op("vector", lambda v: v.tensor_reduce(
                c_u4[0:1, 6:7].rearrange("p (a o) -> p a o", o=1),
                c_sc[0:1, :].rearrange("p (a r) -> p a r", a=1), AX.X, ALU.add),
                waits=[("sACT", sv_)], inc="sDVE")
            nl_ = op("vector", lambda v: v.tensor_scalar_mul(
                c_u4[0:1, 4:5], c_u4[0:1, 6:7], -0.01 / RW),
                waits=[("sDVE", e1)], inc="sDVE")
            u4_ = op("vector", lambda v: v.tensor_reduce(
                c_u4[0:1, 0:4].rearrange("p (a o) -> p a o", o=1),
                ps_s[32:33, :].rearrange("p (a r) -> p a r", a=4), AX.X, ALU.add),
                waits=[("sDVE", e1)], inc="sDVE")
            sg4 = op("scalar", lambda s: s.activation(
                c_sc[0:1, 0:4], c_u4[0:1, 0:4], AF.Sigmoid, bias=c_fgb[:],
                scale=1.0 / 128), waits=[("sDVE", u4_)], inc="sACT")
            f1 = op("vector", lambda v: v.tensor_reduce(
                c_u4[0:1, 5:6].rearrange("p (a o) -> p a o", o=1),
                c_sc[0:1, 0:4].rearrange("p (a r) -> p a r", a=1), AX.X, ALU.add),
                waits=[("sACT", sg4)], inc="sDVE")
            fl_ = op("vector", lambda v: v.tensor_scalar_mul(
                c_fl[0:1, 0:1], c_u4[0:1, 5:6], 0.25), waits=[("sDVE", f1)], inc="sDVE")
            fl2 = op("vector", lambda v: v.tensor_copy(c_fl[0:1, 1:2], c_u4[0:1, 4:5]),
                     waits=[("sDVE", nl_)], inc="sDVE")

            ev_par = 0
            ev_hist = {}
            for dc in range(DC):
                for kc in range(2):
                    w = []
                    if dc == 0 and kc == 0:
                        w = [("sACT", act_h1)]
                    if kc == 0 and dc >= 2:
                        w = [("sACT", ev_hist[dc - 2])]
                    if kc == 0 and dc < 2 and psb_free[dc % 2]:
                        w.append(psb_free[dc % 2])
                    op("tensor", lambda t, dc=dc, kc=kc: t.matmul(
                        ps_b[:, dc % 2, :], c_w1T[:, kc, 128 * dc:128 * (dc + 1)],
                        c_h1T[:, kc, :], start=(kc == 0), stop=(kc == 1)),
                        waits=w, inc="sPE")
                pe_pk = S.v("sPE")
                wex = [("sPE", pe_pk)]
                if dc == 0 and st1_prev:
                    wex.append(("sGD", st1_prev))
                ev_par = op("scalar", lambda s, dc=dc: s.activation(
                    c_par[:, dc, :], ps_b[:, dc % 2, :], AF.Identity,
                    bias=c_b18[:, dc, :], scale=1.0), waits=wex, inc="sACT")
                ev_hist[dc] = ev_par
                psb_free[dc % 2] = ("sACT", ev_par)

            bc = op("tensor", lambda t: t.matmul(
                ps_s[:, 64:66], kones[:], c_fl[0:1, :], start=True, stop=True),
                waits=[("sDVE", max(fl_, fl2))], inc="sPE")
            fv_ = op("vector", lambda v: v.tensor_copy(c_fv[:], ps_s[:, 64:65]),
                     waits=[("sPE", bc)], inc="sDVE")
            nv_ = op("vector", lambda v: v.tensor_copy(c_nl[:], ps_s[:, 65:66]),
                     inc="sDVE")
            sc_done = nv_

            op("gpsimd", lambda g: g.dma_start(
                out=car_in.rearrange("(a p) r -> p a r", p=128)[:, 0:8, :],
                in_=c_par[:, 0:8, :]),
                waits=[("sACT", ev_hist[7]), ("sRB", rb_prev)], inc="sGD", n=16)
            st1 = op("gpsimd", lambda g: g.dma_start(
                out=car_in.rearrange("(a p) r -> p a r", p=128)[:, 8:16, :],
                in_=c_par[:, 8:16, :]),
                waits=[("sACT", ev_par)], inc="sGD", n=16)
            ar = op("gpsimd", lambda g: g.collective_compute(
                "AllReduce", ALU.add, RG8, ins=[car_in.opt()], outs=[car_out.opt()]),
                waits=[("sGD", st1)], inc="sCOLL")
            st1_prev = st1
            if pending_rs is not None:
                op("gpsimd", pending_rs[0], waits=pending_rs[1], inc="sRS")
                pending_rs = None

            last_evp2 = 0
            if deferred is not None:
                a2v, last_evp2, pending_rs = emit_deferred(deferred)
                psa_free = a2v
                deferred = None

            for mc in range(2):
                sl = mc % 2
                w0t = []
                if mc == 0:
                    w0t = [("sACT", act_h1)]
                if psd_evac[sl]:
                    w0t.append(("sDVE", psd_evac[sl]))
                for rcc in range(RC):
                    op("tensor", lambda t, mc=mc, rcc=rcc, sl=sl: t.transpose(
                        psd[sl][:, rcc, :], c_sgT[:, mc, 128 * rcc:128 * (rcc + 1)],
                        ident[:]), waits=w0t if rcc == 0 else (), inc="sPE")
                pe_b = S.v("sPE")
                psd_evac[sl] = op("vector", lambda v, mc=mc, sl=sl: v.tensor_copy(
                    c_sgr[:, :, 128 * mc:128 * (mc + 1)], psd[sl][:]),
                    waits=[("sPE", pe_b)], inc="sDVE")
            sgr_done = S.v("sDVE")
            h1r_done = 0
            for mc in range(2):
                sl = mc % 2
                for rcc in range(RC):
                    w = [("sDVE", psd_evac[sl])] if rcc == 0 else ()
                    op("tensor", lambda t, mc=mc, rcc=rcc, sl=sl: t.transpose(
                        psd[sl][:, rcc, :], c_h1T[:, mc, 128 * rcc:128 * (rcc + 1)],
                        ident[:]), waits=w, inc="sPE")
                pe_b = S.v("sPE")
                psd_evac[sl] = op("vector", lambda v, mc=mc, sl=sl: v.tensor_scalar_mul(
                    c_h1r[:, :, 128 * mc:128 * (mc + 1)], psd[sl][:], c_nl[:]),
                    waits=[("sPE", pe_b), ("sDVE", sc_done)], inc="sDVE")
            for rcc in range(RC):
                h1r_done = op("vector", lambda v, rcc=rcc: v.tensor_copy(
                    c_h1r[:, rcc, MK:MK + 1], c_nl[:]),
                    waits=[("sDVE", sc_done)] if rcc == 0 else (), inc="sDVE")

            if c + 1 < n_steps:
                emit_prep(c + 1, extra_load_waits=[("sDVE", prev_nrm[c])])
                prev_nrm[c + 1] = S.v("sDVE")

            rb1 = op("sync", lambda s: s.dma_start(
                out=c_pT[:], in_=car_out.rearrange("(a p) r -> p a r", p=128)),
                waits=[("sCOLL", ar)], inc="sRB", n=16)
            if dbg.get("pred"):
                op("gpsimd", lambda g, c=c: g.dma_start(
                    out=dbg_outs["d_pred"][c].rearrange("(a p) r -> p a r", p=128),
                    in_=car_out.rearrange("(a p) r -> p a r", p=128)),
                    waits=[("sCOLL", ar)], inc="sGD", n=16)
            rbx = 0
            for rcc in range(RC):
                qn = "sync" if rcc % 2 == 0 else "scalar"
                wq = [("sCOLL", ar)] if rcc < 2 else ()
                rbx = op(qn, lambda s, rcc=rcc: s.dma_start_transpose(
                    c_pr[:, rcc, :], car_out[:, 128 * rcc:128 * (rcc + 1)]),
                    waits=wq, inc="sRB", n=16)
            rb_prev = rbx


            dT1 = op("vector", lambda v, xnT=xnT: v.tensor_sub(
                c_pT[:].rearrange("p a r -> p (a r)"),
                c_pT[:].rearrange("p a r -> p (a r)"),
                xnT[:].rearrange("p a r -> p (a r)")),
                waits=[("sRB", rbx)], inc="sDVE")
            dT2 = op("vector", lambda v: v.tensor_scalar_mul(
                c_pT[:].rearrange("p a r -> p (a r)"),
                c_pT[:].rearrange("p a r -> p (a r)"), TWO_N),
                waits=[("sDVE", dT1)], inc="sDVE")
            dR1 = op("vector", lambda v, xn=xn: v.tensor_sub(
                c_pr[:].rearrange("p a r -> p (a r)"),
                c_pr[:].rearrange("p a r -> p (a r)"),
                xn[:].rearrange("p a r -> p (a r)")),
                waits=[("sRB", rbx)], inc="sDVE")
            dR2 = op("vector", lambda v: v.tensor_scalar_mul(
                c_pr[:].rearrange("p a r -> p (a r)"),
                c_pr[:].rearrange("p a r -> p (a r)"), TWO_N),
                waits=[("sDVE", dR1)], inc="sDVE")

            dhv = ps_b[:].rearrange("p a (c m) -> p (a c) m", m=MK)
            first = True
            for ic in range(RC):
                for kc in range(DC):
                    w = []
                    if first:
                        w = [("sDVE", dT2)]
                        if psb_free[0]:
                            w.append(psb_free[0])
                        if psb_free[1]:
                            w.append(psb_free[1])
                        first = False
                    op("tensor", lambda t, ic=ic, kc=kc, dhv=dhv: t.matmul(
                        dhv[:, ic, :], c_pT[:, kc, 128 * ic:128 * (ic + 1)],
                        c_w1[:, kc, 0:MK], start=(kc == 0), stop=(kc == DC - 1)),
                        waits=w, inc="sPE")
            pe_dh = S.v("sPE")
            dpre_ = op("vector", lambda v, dhv=dhv: v.scalar_tensor_tensor(
                c_dpre[:].rearrange("p a m -> p (a m)"),
                dhv[:].rearrange("p a m -> p (a m)"), c_nl[:],
                c_sgr[:].rearrange("p a m -> p (a m)"), ALU.mult, ALU.mult),
                waits=[("sPE", pe_dh), ("sDVE", sgr_done)], inc="sDVE")
            psb_free[0] = psb_free[1] = ("sDVE", dpre_)

            upd_w1 = 0
            uw_hist = {}
            for dc in range(DC):
                for kc in range(RC):
                    w = []
                    if dc == 0 and kc == 0:
                        w = [("sDVE", max(dR2, h1r_done, dpre_))]
                    if kc == 0 and dc >= 2:
                        w.append(("sDVE", uw_hist[dc - 2]))
                    if kc == 0 and dc < 2 and psb_free[dc % 2]:
                        w.append(psb_free[dc % 2])
                    op("tensor", lambda t, dc=dc, kc=kc: t.matmul(
                        ps_b[:, dc % 2, 0:MK + 1], c_pr[:, kc, 128 * dc:128 * (dc + 1)],
                        c_h1r[:, kc, :], start=(kc == 0), stop=(kc == RC - 1)),
                        waits=w, inc="sPE")
                pe_g = S.v("sPE")
                upd_w1 = op("vector", lambda v, dc=dc: v.scalar_tensor_tensor(
                    c_w1[:, dc, :], c_w1[:, dc, :], c_fv[:],
                    ps_b[:, dc % 2, 0:MK + 1], ALU.mult, ALU.add),
                    waits=[("sPE", pe_g), ("sDVE", fv_)] if dc == 0 else [("sPE", pe_g)],
                    inc="sDVE")
                uw_hist[dc] = upd_w1
                psb_free[dc % 2] = ("sDVE", upd_w1)
            b18_ = op("vector", lambda v: v.tensor_scalar_mul(
                c_b18[:], c_w1[:, :, MK:MK + 1], 0.125),
                waits=[("sDVE", upd_w1), ("sACT", last_evp2)], inc="sDVE")
            upd_w0 = 0
            uw_hist = {}
            for dc in range(DC):
                for kc in range(RC):
                    w = []
                    if dc == 0 and kc == 0:
                        w = [("sDVE", dpre_)]
                    if kc == 0 and dc >= 2:
                        w.append(("sDVE", uw_hist[dc - 2]))
                    if kc == 0 and dc < 2 and psb_free[dc % 2]:
                        w.append(psb_free[dc % 2])
                    op("tensor", lambda t, dc=dc, kc=kc, xn=xn: t.matmul(
                        ps_b[:, dc % 2, 0:MK], xn[:, kc, 128 * dc:128 * (dc + 1)],
                        c_dpre[:, kc, :], start=(kc == 0), stop=(kc == RC - 1)),
                        waits=w, inc="sPE")
                pe_g = S.v("sPE")
                upd_w0 = op("vector", lambda v, dc=dc: v.scalar_tensor_tensor(
                    c_w0[:, dc, :], c_w0[:, dc, :], c_fv[:],
                    ps_b[:, dc % 2, 0:MK], ALU.mult, ALU.add),
                    waits=[("sPE", pe_g)], inc="sDVE")
                uw_hist[dc] = upd_w0
                psb_free[dc % 2] = ("sDVE", upd_w0)
            for mc in range(2):
                for kc in range(RC):
                    op("tensor", lambda t, mc=mc, kc=kc: t.matmul(
                        ps_s[:, 128 + mc:129 + mc],
                        c_dpre[:, kc, 128 * mc:128 * (mc + 1)],
                        onec[:], start=(kc == 0), stop=(kc == RC - 1)),
                        waits=[("sDVE", dpre_)] if (mc == 0 and kc == 0) else (),
                        inc="sPE")
            pe_b0 = S.v("sPE")
            upd_b0 = op("vector", lambda v: v.scalar_tensor_tensor(
                c_b0[:].rearrange("p a o -> p (a o)"),
                c_b0[:].rearrange("p a o -> p (a o)"), c_fv[:],
                ps_s[:, 128:130], ALU.mult, ALU.add),
                waits=[("sPE", pe_b0)], inc="sDVE")

            flip_done = 0
            for mc in range(2):
                for d4 in range(4):
                    sl = (2 * mc + d4) % 2
                    w = []
                    if mc == 0 and d4 == 0:
                        w = [("sDVE", max(upd_w1, b18_))]
                    if psd_evac[sl]:
                        w.append(("sDVE", psd_evac[sl]))
                    for j in range(4):
                        dc = 4 * d4 + j
                        op("tensor", lambda t, dc=dc, mc=mc, sl=sl: t.transpose(
                            psd[sl][:, dc % 4, :], c_w1[:, dc, 128 * mc:128 * (mc + 1)],
                            ident[:]), waits=w if j == 0 else (), inc="sPE")
                    pe_b = S.v("sPE")
                    flip_done = op("vector", lambda v, mc=mc, d4=d4, sl=sl: v.tensor_copy(
                        c_w1T[:, mc, 512 * d4:512 * (d4 + 1)],
                        psd[sl][:].rearrange("p a b -> p (a b)")),
                        waits=[("sPE", pe_b)], inc="sDVE")
                    psd_evac[sl] = flip_done

            deferred = {"c": c, "xnT": xnT,
                        "upd": max(upd_w0, upd_b0, flip_done)}

        if deferred is not None:
            _, _, pending_rs2 = emit_deferred(deferred)
            deferred = None
            if pending_rs is not None:
                op("gpsimd", pending_rs[0], waits=pending_rs[1], inc="sRS")
            pending_rs = pending_rs2
        if pending_rs is not None:
            op("gpsimd", pending_rs[0], waits=pending_rs[1], inc="sRS")
            pending_rs = None

        if dbg.get("myattn"):
            op("gpsimd", lambda g: g.dma_start(
                out=dbg_outs["d_myattn"].rearrange("n d r -> (n d r)").rearrange("(a b) -> a b", a=1),
                in_=myattn.rearrange("n d r -> (n d r)").rearrange("(a b) -> a b", a=1)),
                waits=[("sRS", S.v("sRS"))], inc="sGD", n=16)

        barrier = [("sDVE", S.v("sDVE")), ("sACT", S.v("sACT")),
                   ("sPE", S.v("sPE")), ("sGD", S.v("sGD")),
                   ("sDMA", S.v("sDMA")), ("sPOOL", S.v("sPOOL")),
                   ("sLD", S.v("sLD")), ("sRB", S.v("sRB"))]

        if do_ffn:
            wd_slab_gd = {}
            slab_sl_free = {0: 0, 1: 0}

            def emit_gu_slab_pair(nat, tout, q, lane_w):
                w0 = list(lane_w)
                if slab_sl_free[0]:
                    w0.append(("sSL", slab_sl_free[0]))
                w1 = [("sSL", slab_sl_free[1])] if slab_sl_free[1] else []
                xl = op("sync", lambda s, nat=nat, q=q: s.dma_start_transpose(
                    g_slab[0][:, 0:NF], nat[:, 128 * q:128 * q + 128]),
                    waits=w0, inc="sXB0", n=16)
                xh = op("sync", lambda s, nat=nat, q=q: s.dma_start_transpose(
                    g_slab[1][:, 0:NF], nat[:, 1024 + 128 * q:1024 + 128 * q + 128]),
                    waits=w1, inc="sXB1", n=16)
                for ti in range(2):
                    ov = tout[256 * q:256 * q + 256, :].rearrange(
                        "(a t) n -> t a n", t=2)[ti:ti + 1].rearrange(
                        "o a n -> (o a) n")
                    slab_sl_free[ti] = op(
                        "sync", lambda s, ov=ov, ti=ti: s.dma_start(
                            out=ov, in_=g_slab[ti][:, 0:NF]),
                        waits=[(["sXB0", "sXB1"][ti], [xl, xh][ti])],
                        inc="sSL", n=16)

            slab_sched = {
                128: [(wg_nm, wgT, q) for q in range(4)],
                256: [(wg_nm, wgT, q) for q in range(4, 8)],
                384: [(wu_nm, wuT, q) for q in range(4)],
                512: [(wu_nm, wuT, q) for q in range(4, 8)],
            }

            def dq_boundary(ncomp):
                for (nat, tout, q) in slab_sched.pop(ncomp, []):
                    lane_w = [(f"sQ{l}", 48 * (ncomp // NW)) for l in range(NW)]
                    emit_gu_slab_pair(nat, tout, q, lane_w)

            def emit_d_slabs(grp):
                for j in range(8):
                    qg = 8 * grp + j
                    w_ = [("sGD", S.v("sGD"))]
                    if j == 0:
                        w_ += [(f"sQ{i}", S.v(f"sQ{i}")) for i in range(8)]
                    xl = op("scalar", lambda s, qg=qg: s.dma_start_transpose(
                        f_ds[0][:], wd_nm[:, 128 * qg:128 * qg + 128]),
                        waits=w_, inc="sXB0", n=16)
                    xh = op("scalar", lambda s, qg=qg: s.dma_start_transpose(
                        f_ds[1][:], wd_nm[:, 8192 + 128 * qg:8192 + 128 * qg + 128]),
                        inc="sXB1", n=16)
                    for ti in range(2):
                        ov = wdT[256 * qg:256 * qg + 256, :].rearrange(
                            "(a t) d -> t a d", t=2)[ti:ti + 1].rearrange(
                            "o a d -> (o a) d")
                        op("gpsimd", lambda g, ov=ov, ti=ti: g.dma_start(
                            out=ov, in_=f_ds[ti][:]),
                            waits=[(["sXB0", "sXB1"][ti], [xl, xh][ti])],
                            inc="sGD", n=16)
                wd_slab_gd[grp] = S.v("sGD")

            emit_dq_wave_phase(barrier, dq_boundary)
            if dq_on:
                assert not slab_sched, f"unemitted gu slabs: {list(slab_sched)}"
            else:
                for B in sorted(slab_sched):
                    for (nat, tout, q) in slab_sched.pop(B):
                        emit_gu_slab_pair(nat, tout, q, list(barrier))
            barrier = barrier + [(f"sQ{i}", S.v(f"sQ{i}")) for i in range(8)]
            emit_d_slabs(0)
            emit_ffn(S, dict(
                xloc=xloc, myattn=myattn, ynbuf=ynbuf, out=out,
                wgT=wgT, wuT=wuT, wdT=wdT,
                f_acc=f_acc, f_ynT=f_ynT, f_int=f_int, f_wp=f_wp, f_wd=f_wd,
                f_gel=f_gel, f_ay=f_ay, f_st=f_st, f_mv=f_mv, f_rs=f_rs,
                epsT=epsT, ps_a=ps_a, ps_b=ps_b, ps_f=ps_f, ps_s=ps_s,
                n_rs=S.v("sRS"), full_run=full_run,
                emit_d_slabs=emit_d_slabs, wd_slab_gd=wd_slab_gd,
                dq_barrier=[("sGD", S.v("sGD")), ("sDMA", S.v("sDMA")),
                            ("sXB0", S.v("sXB0")), ("sXB1", S.v("sXB1")),
                            ("sSL", S.v("sSL"))]))
        op("gpsimd", lambda g: g.engine_nop(),
           waits=[("sST", S.v("sST")), ("sGD", S.v("sGD")),
                  ("sDMA", S.v("sDMA")), ("sDVE", S.v("sDVE")),
                  ("sACT", S.v("sACT")), ("sPE", S.v("sPE")),
                  ("sRS", S.v("sRS")), ("sCOLL", S.v("sCOLL")),
                  ("sLD", S.v("sLD")), ("sRB", S.v("sRB")),
                  ("sXB0", S.v("sXB0")), ("sXB1", S.v("sXB1")),
                  ("sSL", S.v("sSL"))]
                 + [(f"sQ{i}", S.v(f"sQ{i}")) for i in range(8)])

        @blk.sync
        def _(e):
            S.replay("sync", e)

        @blk.vector
        def _(e):
            S.replay("vector", e)

        @blk.scalar
        def _(e):
            S.replay("scalar", e)

        @blk.tensor
        def _(e):
            S.replay("tensor", e)

        @blk.gpsimd
        def _(e):
            S.replay("gpsimd", e)

    return nc


def emit_ffn(S, E):
    op = S.op
    f_acc, f_ynT, f_int = E["f_acc"], E["f_ynT"], E["f_int"]
    f_wp, f_wd = E["f_wp"], E["f_wd"]
    f_gel, f_ay = E["f_gel"], E["f_ay"]
    f_st, f_mv, f_rs = E["f_st"], E["f_mv"], E["f_rs"]
    ps_a, ps_b, ps_f, ps_s = E["ps_a"], E["ps_b"], E["ps_f"], E["ps_s"]
    xloc, myattn, ynbuf, out = E["xloc"], E["myattn"], E["ynbuf"], E["out"]
    wgT, wuT, wdT = E["wgT"], E["wuT"], E["wdT"]
    dqb = list(E["dq_barrier"])
    wgTv = wgT.rearrange("(a p) n -> p a n", p=128)
    wuTv = wuT.rearrange("(a p) n -> p a n", p=128)
    wdTv = wdT.rearrange("(a p) h -> p a h", p=128)
    emit_d_slabs = E["emit_d_slabs"]
    wd_slab_gd = E["wd_slab_gd"]

    wp_free = [0, 0, 0]
    wd_freepe = [0, 0]
    st_prev = 0
    gate_ev = {}
    up_ev = {}
    dn_ev = {}
    yn_gd = 0

    for t4 in range(NTILE):
        for cc in range(8):
            ch = 8 * t4 + cc
            wat = list(dqb) if (t4 == 0 and cc == 0) else []
            if E["full_run"]:
                wat.append(("sRS", min(ch + 1, E["n_rs"])))
            wat.append(("sDVE", S.v("sDVE")))
            wat.append(("sGD", yn_gd))
            xa = op("sync", lambda s, ch=ch: s.dma_start_transpose(
                f_ay[:], myattn[ch]), waits=wat, inc="sDMA", n=16)
            xl = op("sync", lambda s, ch=ch, cc=cc: s.dma_start(
                out=f_acc[:, cc, :], in_=xloc[128 * ch:128 * (ch + 1), :]),
                waits=[("sST", st_prev)] if (t4 > 0 and cc == 0) else [],
                inc="sDMA", n=16)
            ya = op("vector", lambda v, cc=cc: v.tensor_add(
                f_acc[:, cc, :], f_acc[:, cc, :], f_ay[:]),
                waits=[("sDMA", S.v("sDMA"))], inc="sDVE")
            last = 0
            for sgm in range(4):
                last = op("vector", lambda v, cc=cc, sgm=sgm: v.bn_stats(
                    f_st[:, 2 * sgm:2 * sgm + 2, :].rearrange("p a b -> p (a b)"),
                    f_acc[:, cc, 512 * sgm:512 * (sgm + 1)]),
                    waits=[("sDVE", ya)] if sgm == 0 else (), inc="sDVE")
            ag = op("vector", lambda v: v.bn_aggr(
                f_mv[:], f_st[:].rearrange("p a b -> p (a b)")),
                waits=[("sDVE", last)], inc="sDVE")
            sq = op("scalar", lambda s: s.activation(
                f_rs[:], f_mv[:, 1:2], AF.Sqrt, bias=E["epsT"][:], scale=1.0),
                waits=[("sDVE", ag)], inc="sACT")
            rc_ = op("vector", lambda v: v.reciprocal(f_rs[:], f_rs[:]),
                     waits=[("sACT", sq)], inc="sDVE")
            nm = op("vector", lambda v, cc=cc: v.tensor_scalar(
                f_ay[:], f_acc[:, cc, :], f_mv[:, 0:1], f_rs[:],
                ALU.subtract, ALU.mult), waits=[("sDVE", rc_)], inc="sDVE")
            yn_gd = op("gpsimd", lambda g, cc=cc: g.dma_start(
                out=ynbuf[128 * cc:128 * (cc + 1), :], in_=f_ay[:]),
                waits=[("sDVE", nm)], inc="sGD", n=16)
        ynT_d = 0
        for dc in range(DC):
            w = [("sGD", yn_gd)]
            if t4 > 0 and dc == 0:
                w.append(("sPE", S.v("sPE")))
            ynT_d = op("sync", lambda s, dc=dc: s.dma_start_transpose(
                f_ynT[:, dc, :], ynbuf[:, 128 * dc:128 * (dc + 1)]),
                waits=w if dc == 0 else [("sGD", yn_gd)], inc="sDMA", n=16)

        for g in range(NGRP):
            iv_last = 0
            for sb in range(NSUB):
                n0 = 2048 * g + NGSZ * sb
                bg = (2 * sb) % 3
                bu = (2 * sb + 1) % 3
                lg = op("sync", lambda s, n0=n0, bg=bg: s.dma_start(
                    out=f_wp[bg][:], in_=wgTv[:, :, n0:n0 + NGSZ]),
                    waits=[("sPE", wp_free[bg])], inc="sDMA", n=16)
                lu = op("sync", lambda s, n0=n0, bu=bu: s.dma_start(
                    out=f_wp[bu][:], in_=wuTv[:, :, n0:n0 + NGSZ]),
                    waits=[("sPE", wp_free[bu])], inc="sDMA", n=16)
                for ic in range(2):
                    for tj in range(TT // 512):
                        rot = tj % 2
                        for kc in range(DC):
                            w = []
                            if kc == 0:
                                w = [("sDMA", max(lu, ynT_d))]
                                if gate_ev.get(rot):
                                    w.append(("sACT", gate_ev[rot]))
                            op("tensor", lambda t, ic=ic, tj=tj, kc=kc, rot=rot, bg=bg: t.matmul(
                                ps_a[:, rot, :],
                                f_wp[bg][:, kc, 128 * ic:128 * (ic + 1)],
                                f_ynT[:, kc, 512 * tj:512 * (tj + 1)],
                                start=(kc == 0), stop=(kc == DC - 1)),
                                waits=w, inc="sPE")
                        peg = S.v("sPE")
                        gate_ev[rot] = op("scalar", lambda s, rot=rot: s.activation(
                            f_gel[:], ps_a[:, rot, :], AF.Gelu_apprx_tanh),
                            waits=[("sPE", peg)], inc="sACT")
                        for kc in range(DC):
                            w = []
                            if kc == 0:
                                w = [("sDMA", lu)]
                                if up_ev.get(rot):
                                    w.append(("sDVE", up_ev[rot]))
                            op("tensor", lambda t, ic=ic, tj=tj, kc=kc, rot=rot, bu=bu: t.matmul(
                                ps_b[:, rot, :],
                                f_wp[bu][:, kc, 128 * ic:128 * (ic + 1)],
                                f_ynT[:, kc, 512 * tj:512 * (tj + 1)],
                                start=(kc == 0), stop=(kc == DC - 1)),
                                waits=w, inc="sPE")
                        peu = S.v("sPE")
                        iv_last = op("vector", lambda v, sb=sb, ic=ic, tj=tj, rot=rot: v.tensor_mul(
                            f_int[:, 2 * sb + ic, 512 * tj:512 * (tj + 1)],
                            f_gel[:], ps_b[:, rot, :]),
                            waits=[("sACT", gate_ev[rot]), ("sPE", peu)], inc="sDVE")
                        up_ev[rot] = iv_last
                wp_free[bg] = S.v("sPE")
                wp_free[bu] = S.v("sPE")
            if t4 == 0 and g + 1 < NGRP:
                emit_d_slabs(g + 1)
            for hh in range(2):
                for hq in range(2):
                    qi_ = 2 * hh + hq
                    wb = qi_ % 2
                    wl = op("gpsimd", lambda gg, g=g, hh=hh, hq=hq, wb=wb: gg.dma_start(
                        out=f_wd[wb][:],
                        in_=wdTv[:, 16 * g:16 * g + 16,
                                 1024 * hh + 512 * hq:1024 * hh + 512 * (hq + 1)]),
                        waits=[("sPE", wd_freepe[wb]), ("sGD", wd_slab_gd[g])],
                        inc="sGD", n=16)
                    for tc in range(TT // 128):
                        rot = tc % 2
                        pst = [ps_f, ps_s][rot]
                        for kc in range(DC):
                            w = []
                            if kc == 0:
                                w = [("sGD", wl)]
                                if qi_ == 0 and tc == 0:
                                    w.append(("sDVE", iv_last))
                                if dn_ev.get(rot):
                                    w.append(("sDVE", dn_ev[rot]))
                            op("tensor", lambda t, tc=tc, kc=kc, pst=pst, wb=wb: t.matmul(
                                pst[:], f_int[:, kc, 128 * tc:128 * (tc + 1)],
                                f_wd[wb][:, kc, :],
                                start=(kc == 0), stop=(kc == DC - 1)),
                                waits=w, inc="sPE")
                        ped = S.v("sPE")
                        dn_ev[rot] = op("vector", lambda v, hh=hh, hq=hq, tc=tc, pst=pst: v.tensor_add(
                            f_acc[:, tc, 1024 * hh + 512 * hq:1024 * hh + 512 * (hq + 1)],
                            f_acc[:, tc, 1024 * hh + 512 * hq:1024 * hh + 512 * (hq + 1)],
                            pst[:]), waits=[("sPE", ped)], inc="sDVE")
                    wd_freepe[wb] = S.v("sPE")
        st_prev = op("gpsimd", lambda g, t4=t4: g.dma_start(
            out=out[TT * t4:TT * (t4 + 1), :].rearrange("(a p) d -> p a d", p=128),
            in_=f_acc[:]), waits=[("sDVE", S.v("sDVE"))], inc="sST", n=16)


def make_inmaps(inp):
    x = np.ascontiguousarray(np.asarray(inp["L0_attn_x"], np.float32))
    W0 = np.asarray(inp["L0_attn_W0"], np.float32)
    W1 = np.asarray(inp["L0_attn_W1"], np.float32)
    b0 = np.asarray(inp["L0_attn_b0"], np.float32)
    b1 = np.asarray(inp["L0_attn_b1"], np.float32)
    lr_w = np.asarray(inp["L0_attn_lr_w"], np.float32)
    fg_w = np.asarray(inp["L0_attn_fg_w"], np.float32)
    lrfg = np.ascontiguousarray(np.stack([lr_w[0], fg_w[0]], axis=1))
    xflat = x.reshape(4 * 8192, D)
    maps = []
    for k in range(8):
        mk = slice(MK * k, MK * (k + 1))
        maps.append({
            "xfull": x,
            "xloc": np.ascontiguousarray(xflat[TLOC * k:TLOC * (k + 1)]),
            "w0kT": np.ascontiguousarray(W0[mk].T),
            "w1dm": np.ascontiguousarray(W1[:, mk]),
            "w1kT": np.ascontiguousarray(W1[:, mk].T),
            "b0k": np.ascontiguousarray(b0[mk].reshape(2, 128).T),
            "b1f": np.ascontiguousarray(b1.reshape(DC, 128).T),
            "lrfg": np.ascontiguousarray(lrfg.reshape(DC, 128, 2).transpose(1, 0, 2)),
            "lrb": np.asarray(inp["L0_attn_lr_b"], np.float32),
            "fgb": np.asarray(inp["L0_attn_fg_b"], np.float32),
            "gwp": np.asarray(inp["L0_ffn_gate_W_packed"], np.int32),
            "gsc": np.asarray(inp["L0_ffn_gate_scales"], np.float32),
            "uwp": np.asarray(inp["L0_ffn_up_W_packed"], np.int32),
            "usc": np.asarray(inp["L0_ffn_up_scales"], np.float32),
            "dwp": np.asarray(inp["L0_ffn_down_W_packed"], np.int32),
            "dsc": np.asarray(inp["L0_ffn_down_scales"], np.float32),
        })
    return maps


def assemble(results):
    return np.concatenate([r["out"] for r in results], axis=0).reshape(4, 8192, D)


def kernel(**inputs):
    maps = make_inmaps(inputs)
    nc = build()
    res = run_bass_kernel_spmd(nc, maps, core_ids=list(range(8)))
    return assemble(res.results)
